# version 02
# baseline (speedup 1.0000x reference)
import contextlib
import numpy as np

import concourse.bass as bass
import concourse.mybir as mybir
from concourse.bass_utils import run_bass_kernel_spmd
from concourse.alu_op_type import AluOpType as ALU

F32 = mybir.dt.float32
BF16 = mybir.dt.bfloat16
AF = mybir.ActivationFunctionType
AX = mybir.AxisListType

ENG = ("sync", "gpsimd", "vector", "scalar", "tensor")
RACE_DETECT = True

T = 4096
D = 1024
JD = D // 128
NT = 512
NCORE = 8


class Prog:

    def __init__(self, nc):
        self.nc = nc
        self.ops = {e: [] for e in ENG}
        self.sems = {}
        self.tally = {}

    def make_sems(self, es, names):
        for n in names:
            self.sems[n] = es.enter_context(self.nc.semaphore(n))
            self.tally[n] = 0

    DMA_SEMS = ("LWA", "LWB", "LX", "WBA", "WBB", "ST", "DB", "DG")

    def wait(self, eng, sem, val):
        assert val <= self.tally[sem], ("wait ahead of tally", eng, sem, val, self.tally[sem])
        if sem in self.DMA_SEMS:
            assert val == self.tally[sem] or val == 0, ("partial DMA wait", eng, sem, val,
                                                        self.tally[sem])
        if val > 0:
            s = self.sems[sem]
            self.ops[eng].append(lambda e, s=s, v=val: e.wait_ge(s, v))

    def dwait(self, eng, sem):
        self.wait(eng, sem, self.tally[sem])

    def op(self, eng, fn, inc=None, n=1):
        if inc is None:
            self.ops[eng].append(fn)
            return None
        self.tally[inc] += n
        s = self.sems[inc]

        def run(e, fn=fn, s=s, n=n):
            fn(e).then_inc(s, n)

        self.ops[eng].append(run)
        return self.tally[inc]

    def dma(self, eng, out, in_, inc):
        return self.op(eng, lambda e, o=out, i=in_: e.dma_start(out=o, in_=i), inc, 16)

    def replay(self, block):
        for eng in ENG:
            dec = getattr(block, eng)
            ops = self.ops[eng]

            def body(e, ops=ops):
                for f in ops:
                    f(e)

            body.__name__ = f"_{eng}"
            dec(body)


class Ctx:

    def __init__(self, nc, p, es):
        self.nc = nc
        self.p = p
        self.es = es
        self._dram = {}
        self._sbuf = {}
        self._state = {}
        sb = lambda name, shape, dt: es.enter_context(nc.sbuf_tensor(name, shape, dt))
        self.xs = sb("xs", [128, JD * NT], F32)
        self.xln = sb("xln", [128, JD * 1024], BF16)
        self.hb = sb("hb", [128, 4 * JD * 1024], BF16)
        self.arena = sb("arena", [128, 4 * JD * 1024], BF16)
        self.wbA = sb("wbA", [128, JD * 1024], BF16)
        self.wbB = sb("wbB", [128, JD * 1024], BF16)
        self.abuf = sb("abuf", [128, 2 * NT], F32)
        self.ones_r = sb("ones_r", [1, 128], F32)
        self.ones_c = sb("ones_c", [128, 1], F32)
        self.rmean = sb("rmean", [1, NT], F32)
        self.rvar = sb("rvar", [1, NT], F32)
        self.rt2 = sb("rt2", [1, NT], F32)
        self.rA = sb("rA", [1, NT], F32)
        self.rB = self.rt2
        self.ceps = sb("ceps", [1, 2], F32)
        self.pb = [es.enter_context(nc.psum_tensor(f"pb{i}", [128, NT], F32))
                   for i in range(8)]
        self.inited = False

    def dram(self, name, shape, dt=F32, kind="ExternalInput"):
        if name not in self._dram:
            self._dram[name] = self.nc.dram_tensor(name, shape, dt, kind=kind).ap()
        return self._dram[name]

    def sb(self, name, shape, dt):
        if name not in self._sbuf:
            self._sbuf[name] = self.es.enter_context(self.nc.sbuf_tensor(name, shape, dt))
        return self._sbuf[name]

    def init_common(self):
        if self.inited:
            return self._state["ev_common"]
        p = self.p
        p.op("vector", lambda e: e.memset(self.ones_r[:], 1.0))
        p.op("vector", lambda e: e.memset(self.ones_c[:], 1.0))
        p.op("vector", lambda e: e.memset(self.ceps[0:1, 0:1], 1e-5))
        p.op("vector", lambda e: e.memset(self.ceps[0:1, 1:2], 1e-6))
        ev = p.op("vector", lambda e: e.nop(), "EV")
        self._state["ev_common"] = ev
        self.inited = True
        return ev

    def barrier(self):
        p = self.p
        mm, ev, es_ = p.tally["MM"], p.tally["EV"], p.tally["ES"]
        for eng in ENG:
            if eng != "tensor":
                p.wait(eng, "MM", mm)
            if eng != "vector":
                p.wait(eng, "EV", ev)
            if eng != "scalar":
                p.wait(eng, "ES", es_)
            for dsem in ("ST", "LX", "LWA", "LWB", "WBA", "WBB"):
                p.dwait(eng, dsem)


def emit_l0ffn(C, xin, xout, wp="f0", first=True):
    HID = 4096
    JH = HID // 128
    S = 1024
    NS = T // S
    TB = S // NT

    nc = C.nc
    p = C.p
    nm = (wp + "_") if wp else ""
    x, y = xin, xout
    w1t = C.dram(nm + "w1t", [D, HID])
    w2t = C.dram(nm + "w2t", [D, HID])
    h1t = C.dram(nm + "h1t", [HID, HID])
    h2t = C.dram(nm + "h2t", [HID, HID])
    w3t = C.dram(nm + "w3t", [HID, D])
    s1v = C.dram(nm + "s1v", [128, JH])
    s2v = C.dram(nm + "s2v", [128, JH])
    h1sv = C.dram(nm + "h1sv", [128, JH])
    h2sv = C.dram(nm + "h2sv", [128, JH])
    s3v = C.dram(nm + "s3v", [128, JD])
    wc1 = C.dram(nm + "wc1", [JH, 128, JD * 128], BF16, "Internal")
    wc2 = C.dram(nm + "wc2", [JH, 128, JD * 128], BF16, "Internal")
    wc3 = C.dram(nm + "wc3", [JH, 128, JH * 128], BF16, "Internal")
    wc4 = C.dram(nm + "wc4", [JH, 128, JH * 128], BF16, "Internal")
    wc5 = C.dram(nm + "wc5", [JD, 128, JH * 128], BF16, "Internal")

    if True:
        xs = C.xs
        xln = C.xln
        hb = C.hb
        arena = C.arena
        wbA = C.wbA
        wbB = C.wbB
        abuf = C.abuf
        s1c = C.sb(nm + "s1c", [128, JH], F32)
        s2c = C.sb(nm + "s2c", [128, JH], F32)
        h1sc = C.sb(nm + "h1sc", [128, JH], F32)
        h2sc = C.sb(nm + "h2sc", [128, JH], F32)
        s3h = C.sb(nm + "s3h", [128, JD], F32)
        ones_r = C.ones_r
        ones_c = C.ones_c
        rmean, rvar, rt2, rA, rB = C.rmean, C.rvar, C.rt2, C.rA, C.rB
        ceps = C.ceps
        pb = C.pb

        arena_f = arena.bitcast(F32)
        gbuf = arena
        xsq = arena_f[:, 10240:10240 + JD * NT]
        xres = arena_f[:, 0:JD * S]
        ybuf = [arena_f[:, 8192 + i * NT:8192 + (i + 1) * NT] for i in range(2)]

        ev_common = C.init_common()
        state = C._state.setdefault(wp or "l0ffn", {})
        if first:
            for dst, srcd in ((s1c, s1v), (s2c, s2v), (h1sc, h1sv), (h2sc, h2sv),
                              (s3h, s3v)):
                p.dma("sync", dst[:], srcd, "LX")
            p.dwait("vector", "LX")
            p.op("vector", lambda e: e.tensor_scalar_mul(out=s3h[:], in0=s3h[:],
                                                         scalar1=0.5))
            p.op("vector", lambda e: e.drain())
            state["ev_init"] = p.op("vector", lambda e: e.nop(), "EV")
            state["wb_after"] = {}
        ev_init = max(state["ev_init"], ev_common)
        wb_after = state["wb_after"]

        xs_free = (0, ev_init)
        xln_free_mm = 0
        wb_free_mm = 0
        hb_free_mm = 0
        arena_free = (0, 0)
        psAB_free_ev = 0
        psCD_free_ev = 0
        prev_mm_stats = 0

        def wchunk_load(s, tag, o, wsrc, wc, ktiles, slot, free_mm, lsem, wsem):
            if free_mm:
                p.wait("gpsimd", "MM", free_mm)
            if s == 0 and first:
                if (tag, o - 2) in wb_after:
                    p.dwait("gpsimd", wsem)
                view3 = slot.rearrange("p (k o) -> p k o", k=ktiles)
                src = wsrc[:, o * 128:(o + 1) * 128].rearrange("(k p) o -> p k o", p=128)
                p.dma("gpsimd", view3, src, lsem)
                p.dwait("sync", lsem)
                wb_after[(tag, o)] = p.dma("sync", wc[o], slot, wsem)
            else:
                p.dwait("gpsimd", wsem)
                p.dma("gpsimd", slot, wc[o], lsem)

        for s in range(NS):
            ev_xln = {}
            for tb in range(TB):
                t0 = s * S + tb * NT
                p.wait("sync", "MM", xs_free[0])
                p.wait("sync", "EV", xs_free[1])
                src = x[:, t0:t0 + NT].rearrange("(j p) t -> p j t", p=128)
                lx_xs = p.dma("sync", xs[:].rearrange("p (j t) -> p j t", j=JD), src, "LX")
                p.wait("scalar", "MM", prev_mm_stats)
                p.dwait("scalar", "LX")
                es_xsq = p.op("scalar", lambda e: e.activation(out=xsq, in_=xs[:],
                                                               func=AF.Square), "ES")
                p.wait("tensor", "EV", max(ev_init, psAB_free_ev))
                p.dwait("tensor", "LX")
                for j in range(JD):
                    p.op("tensor", lambda e, j=j: e.matmul(
                        out=pb[4][0:1, :], lhsT=ones_c[:], rhs=xs[:, j * NT:(j + 1) * NT],
                        start=(j == 0), stop=(j == JD - 1)))
                p.wait("tensor", "ES", es_xsq)
                mm_stats = 0
                for j in range(JD):
                    mm_stats = p.op("tensor", lambda e, j=j: e.matmul(
                        out=pb[5][0:1, :], lhsT=ones_c[:], rhs=xsq[:, j * NT:(j + 1) * NT],
                        start=(j == 0), stop=(j == JD - 1)),
                        "MM" if j == JD - 1 else None)
                prev_mm_stats = mm_stats
                p.wait("vector", "MM", mm_stats)
                p.op("vector", lambda e: e.tensor_scalar_mul(out=rmean[:], in0=pb[4][0:1, :],
                                                             scalar1=1.0 / D))
                p.op("vector", lambda e: e.tensor_scalar_mul(out=rvar[:], in0=pb[5][0:1, :],
                                                             scalar1=1.0 / D))
                p.op("vector", lambda e: e.tensor_mul(out=rt2[:], in0=rmean[:], in1=rmean[:]))
                psAB_free_ev = p.op("vector", lambda e: e.tensor_sub(out=rvar[:], in0=rvar[:],
                                                                     in1=rt2[:]), "EV")
                p.wait("scalar", "EV", psAB_free_ev)
                p.op("scalar", lambda e: e.activation(out=rt2[:], in_=rvar[:], func=AF.Ln,
                                                      bias=ceps[0:1, 0:1]))
                p.op("scalar", lambda e: e.drain())
                es_rstd = p.op("scalar", lambda e: e.activation(out=rA[:], in_=rt2[:],
                                                                func=AF.Exp, scale=-0.5), "ES")
                p.wait("vector", "ES", es_rstd)
                p.op("vector", lambda e: e.tensor_mul(out=rB[:], in0=rmean[:], in1=rA[:]))
                ev_rB = p.op("vector", lambda e: e.tensor_scalar_mul(out=rB[:], in0=rB[:],
                                                                     scalar1=-1.0), "EV")
                p.wait("tensor", "EV", max(ev_rB, psCD_free_ev))
                p.op("tensor", lambda e: e.matmul(out=pb[6][:], lhsT=ones_r[:], rhs=rA[:],
                                                  start=True, stop=True))
                mm_bcast = p.op("tensor", lambda e: e.matmul(out=pb[7][:], lhsT=ones_r[:],
                                                             rhs=rB[:], start=True, stop=True),
                                "MM")
                p.wait("vector", "MM", max(mm_bcast, xln_free_mm))
                last = 0
                for j in range(JD):
                    p.op("vector", lambda e, j=j: e.tensor_mul(
                        out=abuf[:, 0:NT], in0=xs[:, j * NT:(j + 1) * NT], in1=pb[6][:]))
                    last = p.op("vector", lambda e, j=j, tb=tb: e.tensor_add(
                        out=xln[:, j * S + tb * NT: j * S + (tb + 1) * NT],
                        in0=abuf[:, 0:NT], in1=pb[7][:]), "EV")
                ev_xln[tb] = last
                psCD_free_ev = last
                xs_free = (mm_stats, last)

            g12_done = {}
            es_after_g12 = {}
            ev_after_g12 = {}
            for o in range(JH):
                slot1 = wbA[:, (o % 2) * JD * 128:(o % 2 + 1) * JD * 128]
                slot2 = wbB[:, (o % 2) * JD * 128:(o % 2 + 1) * JD * 128]
                fm = g12_done.get(o - 2, wb_free_mm)
                wchunk_load(s, 1, o, w1t, wc1, JD, slot1, fm, "LWA", "WBA")
                wchunk_load(s, 2, o, w2t, wc2, JD, slot2, 0, "LWB", "WBB")
                p.dwait("tensor", "LWA")
                p.dwait("tensor", "LWB")
                if o == 0:
                    p.wait("tensor", "EV", ev_xln[TB - 1])
                for tb in range(TB):
                    i = o * TB + tb
                    pw1, pw2 = pb[i % 2], pb[2 + i % 2]
                    if i >= 2:
                        p.wait("tensor", "ES", es_after_g12[i - 2])
                        p.wait("tensor", "EV", ev_after_g12[i - 2])
                    for k in range(JD):
                        p.op("tensor", lambda e, k=k, tb=tb, slot1=slot1, pw1=pw1: e.matmul(
                            out=pw1[:], lhsT=slot1[:, k * 128:(k + 1) * 128],
                            rhs=xln[:, k * S + tb * NT: k * S + (tb + 1) * NT],
                            start=(k == 0), stop=(k == JD - 1)))
                    mm = 0
                    for k in range(JD):
                        mm = p.op("tensor", lambda e, k=k, tb=tb, slot2=slot2, pw2=pw2: e.matmul(
                            out=pw2[:], lhsT=slot2[:, k * 128:(k + 1) * 128],
                            rhs=xln[:, k * S + tb * NT: k * S + (tb + 1) * NT],
                            start=(k == 0), stop=(k == JD - 1)),
                            "MM" if k == JD - 1 else None)
                    p.wait("scalar", "MM", mm)
                    es_s = p.op("scalar", lambda e, i=i, o=o, pw1=pw1: e.activation(
                        out=abuf[:, (i % 2) * NT:(i % 2 + 1) * NT], in_=pw1[:], func=AF.Silu,
                        scale=s1c[:, o:o + 1]), "ES")
                    es_after_g12[i] = es_s
                    p.wait("vector", "MM", mm)
                    p.wait("vector", "ES", es_s)
                    if i == 0:
                        p.wait("vector", "MM", hb_free_mm)
                    ev_h = p.op("vector", lambda e, i=i, o=o, tb=tb, pw2=pw2:
                                e.scalar_tensor_tensor(
                        out=hb[:, o * S + tb * NT: o * S + (tb + 1) * NT], in0=pw2[:],
                        scalar=s2c[:, o:o + 1], in1=abuf[:, (i % 2) * NT:(i % 2 + 1) * NT],
                        op0=ALU.mult, op1=ALU.mult), "EV")
                    ev_after_g12[i] = ev_h
                g12_done[o] = mm
            ev_h_all = ev_after_g12[JH * TB - 1]
            mm_g12_done = p.tally["MM"]
            xln_free_mm = mm_g12_done
            wb_free_mm = mm_g12_done

            def big_gemm(s, tag, wsrc, wc, rhs_buf, rhs_S, n_out, ktiles,
                         evac_fn, first_waits):
                done = {}
                es_after = {}
                nonlocal wb_free_mm
                for o in range(n_out):
                    lsem, wsem = ("LWA", "WBA") if o % 2 == 0 else ("LWB", "WBB")
                    slot = (wbA if o % 2 == 0 else wbB)[:, 0:ktiles * 128]
                    fm = done.get(o - 2, wb_free_mm)
                    wchunk_load(s, tag, o, wsrc, wc, ktiles, slot, fm, lsem, wsem)
                    p.dwait("tensor", lsem)
                    if o == 0:
                        for sem, val in first_waits:
                            p.wait("tensor", sem, val)
                    mm = 0
                    for tb in range(TB):
                        i = o * TB + tb
                        pw = pb[i % 4]
                        if i >= 4:
                            sem, val = es_after[i - 4]
                            p.wait("tensor", sem, val)
                        for k in range(ktiles):
                            mm = p.op("tensor", lambda e, k=k, tb=tb, slot=slot, pw=pw,
                                      rhs_buf=rhs_buf, rhs_S=rhs_S: e.matmul(
                                out=pw[:], lhsT=slot[:, k * 128:(k + 1) * 128],
                                rhs=rhs_buf[:, k * rhs_S + tb * NT: k * rhs_S + (tb + 1) * NT],
                                start=(k == 0), stop=(k == ktiles - 1)),
                                "MM" if k == ktiles - 1 else None)
                        es_after[i] = evac_fn(o, tb, pw, i, mm)
                    done[o] = mm
                wb_free_mm = p.tally["MM"]
                return p.tally["MM"]

            af = arena_free

            def evac_g3(o, tb, pw, i, mm, af=af):
                p.wait("scalar", "MM", mm)
                if i == 0:
                    p.wait("scalar", "EV", af[0])
                    p.dwait("scalar", "ST")
                t = p.op("scalar", lambda e, o=o, tb=tb, pw=pw: e.activation(
                    out=gbuf[:, o * S + tb * NT: o * S + (tb + 1) * NT], in_=pw[:],
                    func=AF.Gelu, scale=h1sc[:, o:o + 1]), "ES")
                return ("ES", t)

            mm_g3_done = big_gemm(s, 3, h1t, wc3, hb, S, JH, JH, evac_g3,
                                  [("EV", ev_h_all)])
            es_g3_all = p.tally["ES"]

            def evac_g4(o, tb, pw, i, mm):
                p.wait("scalar", "MM", mm)
                t = p.op("scalar", lambda e, o=o, tb=tb, pw=pw: e.activation(
                    out=hb[:, o * S + tb * NT: o * S + (tb + 1) * NT], in_=pw[:],
                    func=AF.Gelu, scale=h2sc[:, o:o + 1]), "ES")
                return ("ES", t)

            mm_g4_done = big_gemm(s, 4, h2t, wc4, gbuf, S, JH, JH, evac_g4,
                                  [("ES", es_g3_all)])
            es_g4_all = p.tally["ES"]

            p.wait("sync", "MM", mm_g4_done)
            srcx = x[:, s * S:(s + 1) * S].rearrange("(j p) t -> p j t", p=128)
            lx_res = p.dma("sync", xres.rearrange("p (j t) -> p j t", j=JD), srcx, "LX")

            ev_after_g5 = {}
            st_after_g5 = {}

            def evac_g5(o, tb, pw, i, mm, af=af):
                p.wait("vector", "MM", mm)
                p.dwait("vector", "LX")
                p.dwait("vector", "ST")
                t = p.op("vector", lambda e, o=o, tb=tb, i=i, pw=pw: e.scalar_tensor_tensor(
                    out=ybuf[i % 2], in0=pw[:], scalar=s3h[:, o:o + 1],
                    in1=xres[:, o * S + tb * NT: o * S + (tb + 1) * NT],
                    op0=ALU.mult, op1=ALU.add), "EV")
                ev_after_g5[i] = t
                p.wait("sync", "EV", t)
                t0 = s * S + tb * NT
                stt = p.dma("sync", y[o * 128:(o + 1) * 128, t0:t0 + NT], ybuf[i % 2], "ST")
                st_after_g5[i] = stt
                return ("EV", t)

            mm_g5_done = big_gemm(s, 5, w3t, wc5, hb, S, JD, JH, evac_g5,
                                  [("ES", es_g4_all)])
            hb_free_mm = mm_g5_done
            arena_free = (ev_after_g5[JD * TB - 1], st_after_g5[JD * TB - 1])

        p.dwait("sync", "ST")
        p.dwait("sync", "WBA")
        p.dwait("sync", "WBB")


ALLSEMS = ["LWA", "LWB", "LX", "WBA", "WBB", "ST", "MM", "EV", "ES"]


def build_l0ffn(dbg=False):
    nc = bass.Bass("TRN2", target_bir_lowering=False, detect_race_conditions=RACE_DETECT)
    x = nc.dram_tensor("x", [D, T], F32, kind="ExternalInput").ap()
    y = nc.dram_tensor("y", [D, T], F32, kind="ExternalOutput").ap()
    with contextlib.ExitStack() as es:
        p = Prog(nc)
        p.make_sems(es, ALLSEMS)
        C = Ctx(nc, p, es)
        emit_l0ffn(C, x, y, wp="", first=True)
        with nc.Block() as block:
            p.replay(block)
    return nc



def _colmajor(v):
    return np.ascontiguousarray(np.asarray(v, np.float32).reshape(-1, 128).T)


def layout_l0ffn(inp):
    f = np.float32
    return dict(
        w1t=np.ascontiguousarray(np.asarray(inp["L0_ffn_w1_w"], f).T),
        w2t=np.ascontiguousarray(np.asarray(inp["L0_ffn_w2_w"], f).T),
        h1t=np.ascontiguousarray(np.asarray(inp["L0_ffn_h1_w"], f).T),
        h2t=np.ascontiguousarray(np.asarray(inp["L0_ffn_h2_w"], f).T),
        w3t=np.ascontiguousarray(np.asarray(inp["L0_ffn_w3_w"], f).T),
        s1v=_colmajor(inp["L0_ffn_w1_s"]),
        s2v=_colmajor(inp["L0_ffn_w2_s"]),
        h1sv=_colmajor(inp["L0_ffn_h1_s"]),
        h2sv=_colmajor(inp["L0_ffn_h2_s"]),
        s3v=_colmajor(inp["L0_ffn_w3_s"]),
    )


def _norm_block(p, pb, src3, xsq3, dst3, ones_c, ones_r, rows, ceps, eps_idx,
                tmp, gain, with_mean, prev_sq_mm, psAB_free, psCD_free,
                waits_t=(), waits_s=(), waits_v=(), dst2=None):
    rmean, rvar, rt2, rA, rB = rows
    p.wait("scalar", "MM", prev_sq_mm)
    for sem, val in waits_s:
        p.wait("scalar", sem, val)
    es_xsq = p.op("scalar", lambda e: e.activation(out=xsq3, in_=src3, func=AF.Square), "ES")
    for sem, val in waits_t:
        p.wait("tensor", sem, val)
    p.wait("tensor", "EV", psAB_free)
    if with_mean:
        for j in range(JD):
            p.op("tensor", lambda e, j=j: e.matmul(
                out=pb[4][0:1, :], lhsT=ones_c[:], rhs=src3[:, j, :],
                start=(j == 0), stop=(j == JD - 1)))
    p.wait("tensor", "ES", es_xsq)
    mm_stats = 0
    for j in range(JD):
        mm_stats = p.op("tensor", lambda e, j=j: e.matmul(
            out=pb[5][0:1, :], lhsT=ones_c[:], rhs=xsq3[:, j, :],
            start=(j == 0), stop=(j == JD - 1)), "MM" if j == JD - 1 else None)
    p.wait("vector", "MM", mm_stats)
    for sem, val in waits_v:
        p.wait("vector", sem, val)
    p.op("vector", lambda e: e.tensor_scalar_mul(out=rvar[:], in0=pb[5][0:1, :],
                                                 scalar1=1.0 / D))
    if with_mean:
        p.op("vector", lambda e: e.tensor_scalar_mul(out=rmean[:], in0=pb[4][0:1, :],
                                                     scalar1=1.0 / D))
        p.op("vector", lambda e: e.tensor_mul(out=rt2[:], in0=rmean[:], in1=rmean[:]))
        p.op("vector", lambda e: e.tensor_sub(out=rvar[:], in0=rvar[:], in1=rt2[:]))
    ev_rows = p.op("vector", lambda e: e.nop(), "EV")
    p.wait("scalar", "EV", ev_rows)
    p.op("scalar", lambda e: e.activation(out=rt2[:], in_=rvar[:], func=AF.Ln,
                                          bias=ceps[0:1, eps_idx:eps_idx + 1]))
    p.op("scalar", lambda e: e.drain())
    es_rstd = p.op("scalar", lambda e: e.activation(out=rA[:], in_=rt2[:],
                                                    func=AF.Exp, scale=-0.5), "ES")
    ev_rB = es_rstd
    if with_mean:
        p.wait("vector", "ES", es_rstd)
        p.op("vector", lambda e: e.tensor_mul(out=rB[:], in0=rmean[:], in1=rA[:]))
        ev_rB = p.op("vector", lambda e: e.tensor_scalar_mul(out=rB[:], in0=rB[:],
                                                             scalar1=-1.0), "EV")
    if with_mean:
        p.wait("tensor", "EV", max(ev_rB, psCD_free))
        p.op("tensor", lambda e: e.matmul(out=pb[6][:], lhsT=ones_r[:], rhs=rA[:],
                                          start=True, stop=True))
        mm_bc = p.op("tensor", lambda e: e.matmul(out=pb[7][:], lhsT=ones_r[:], rhs=rB[:],
                                                  start=True, stop=True), "MM")
    else:
        p.wait("tensor", "ES", es_rstd)
        p.wait("tensor", "EV", psCD_free)
        mm_bc = p.op("tensor", lambda e: e.matmul(out=pb[6][:], lhsT=ones_r[:], rhs=rA[:],
                                                  start=True, stop=True), "MM")
    p.wait("vector", "MM", mm_bc)
    last = 0
    for j in range(JD):
        p.op("vector", lambda e, j=j: e.tensor_mul(out=tmp[:], in0=src3[:, j, :],
                                                   in1=pb[6][:]))
        if with_mean:
            last = p.op("vector", lambda e, j=j: e.tensor_add(
                out=dst3[:, j, :], in0=tmp[:], in1=pb[7][:]), "EV")
            if dst2 is not None:
                last = p.op("vector", lambda e, j=j: e.tensor_add(
                    out=dst2[:, j, :], in0=tmp[:], in1=pb[7][:]), "EV")
        else:
            last = p.op("vector", lambda e, j=j: e.tensor_scalar_mul(
                out=dst3[:, j, :], in0=tmp[:], scalar1=gain[:, j:j + 1]), "EV")
    return mm_stats, last


def emit_l0attn(C, xin, xout, wp="a0", first=True):
    H2 = 2048
    JH2 = H2 // 128
    S = 1024
    NS = T // S
    TB_T = T // NT
    TB = S // NT
    CH = 128
    NC = T // CH

    nc = C.nc
    p = C.p
    nm = (wp + "_") if wp else ""
    x, y = xin, xout
    wqt = C.dram(nm + "wqt", [D, D])
    wkt = C.dram(nm + "wkt", [D, D])
    wvt = C.dram(nm + "wvt", [D, D])
    wot = C.dram(nm + "wot", [D, D])
    wgt = C.dram(nm + "wgt", [D, H2])
    wut = C.dram(nm + "wut", [D, H2])
    wdt = C.dram(nm + "wdt", [H2, D])
    n1v = C.dram(nm + "n1v", [128, JD])
    n2v = C.dram(nm + "n2v", [128, JD])
    alv = C.dram(nm + "alv", [128, JD])
    bwv = C.dram(nm + "bwv", [128, JD])
    bbv = C.dram(nm + "bbv", [1, 1])
    msl = C.dram(nm + "msl", [128, 128])
    mli = C.dram(nm + "mli", [128, 128])
    xisc = C.dram(nm + "xisc", [D, T], F32, "Internal")
    qsc = C.dram(nm + "qsc", [JD, 128, T], BF16, "Internal")
    ksc = C.dram(nm + "ksc", [JD, 128, T], BF16, "Internal")
    vsc = C.dram(nm + "vsc", [JD, 128, T], BF16, "Internal")
    osc = C.dram(nm + "osc", [JD, 128, T], BF16, "Internal")
    wco = C.dram(nm + "wco", [JD, 128, JD * 128], BF16, "Internal")
    wcg = C.dram(nm + "wcg", [JH2, 128, JD * 128], BF16, "Internal")
    wcu = C.dram(nm + "wcu", [JH2, 128, JD * 128], BF16, "Internal")
    wcd = C.dram(nm + "wcd", [JD, 128, JH2 * 128], BF16, "Internal")

    if True:
        xs = C.xs
        xln = C.xln
        hb = C.hb
        arena = C.arena
        wbA = C.wbA
        wbB = C.wbB
        abuf = C.abuf
        n1c = C.sb(nm + "n1c", [128, JD], F32)
        n2c = C.sb(nm + "n2c", [128, JD], F32)
        amc = C.sb(nm + "amc", [128, JD], F32)
        alc = C.sb(nm + "alc", [128, JD], F32)
        bwc = C.sb(nm + "bwc", [128, JD], BF16)
        bwf = C.sb(nm + "bwf", [128, JD], F32)
        betc = C.sb(nm + "betc", [128, NC], F32)
        carry = [C.sb(nm + f"carry{j}", [128, 1], F32) for j in range(JD)]
        bbb = C.sb(nm + "bbb", [128, 1], F32)
        bb1 = C.sb(nm + "bb1", [1, 1], F32)
        maskl = C.sb(nm + "maskl", [128, 128], BF16)
        maski = C.sb(nm + "maski", [128, 128], BF16)
        identb = C.sb(nm + "identb", [128, 128], BF16)
        ones_r = C.ones_r
        ones_c = C.ones_c
        rmean, rvar, rt2, rA = C.rmean, C.rvar, C.rt2, C.rA
        rB = rt2
        ceps = C.ceps
        pb = C.pb

        pwr = xln[:, 0:1024]
        pwrT = xln[:, 1024:2048]
        ubuf = xln[:, 2048:3072]
        wtb = xln[:, 3072:3200]
        wtT = xln[:, 3200:3328]
        t1f = xs[:, 0:1024]

        rows = (rmean, rvar, rt2, rA, rB)
        arena_f = arena.bitcast(F32)
        hb_f = hb.bitcast(F32)
        ambc = arena_f[:, 12288:12288 + JD * NT]
        xi_tb = arena_f[:, 0:JD * NT]
        qf = arena[:, 0:1024]
        kf = arena[:, 1024:2048]
        vtm = arena[:, 2048:3072]
        ktm = arena[:, 3072:4096]
        otm = arena[:, 4096:5120]
        ofm = arena[:, 5120:6144]
        vstg = arena[:, 6144:7168]
        Sf = hb_f[:, 0:8192]
        Sbb = hb[:, 16384:24576]
        kstage = xln[:, 0:JD * NT]
        xsq = abuf[:, NT:2 * NT]
        vsq = arena_f[:, 8192:8192 + JD * NT]
        xnew = arena_f[:, 0:JD * S]
        xresC = arena_f[:, JD * S:2 * JD * S]
        mbuf = hb[:, 0:JH2 * S]
        ostripe = xln[:, 0:JD * S]
        h2st = xln[:, 0:JD * S]
        tmpNT = abuf[:, 0:NT]

        C.init_common()

        for dst, srcd in ((n1c, n1v), (n2c, n2v), (alc, alv), (bwf, bwv)):
            p.dma("sync", dst[:], srcd, "LX")
        p.dma("sync", bb1[:], bbv, "LX")
        p.wait("gpsimd", "MM", p.tally["MM"])
        p.dwait("gpsimd", "WBA")
        p.dma("gpsimd", maskl[:], msl, "LWA")
        p.dma("gpsimd", maski[:], mli, "LWA")
        for j in range(JD):
            p.op("vector", lambda e, j=j: e.memset(carry[j][:], 0.0))
        p.dwait("vector", "LWA")
        p.op("vector", lambda e: e.tensor_sub(out=identb[:], in0=maski[:], in1=maskl[:]))
        p.dwait("vector", "LX")
        p.op("vector", lambda e: e.tensor_copy(out=bwc[:], in_=bwf[:]))
        ev_init0 = p.op("vector", lambda e: e.nop(), "EV")
        p.wait("scalar", "EV", ev_init0)
        es_al = p.op("scalar", lambda e: e.activation(out=alc[:], in_=alc[:],
                                                      func=AF.Sigmoid), "ES")
        p.wait("vector", "ES", es_al)
        p.op("vector", lambda e: e.tensor_scalar(out=amc[:], in0=alc[:], scalar1=-1.0,
                                                 scalar2=1.0, op0=ALU.mult, op1=ALU.add))
        p.op("vector", lambda e: e.drain())
        p.op("vector", lambda e: e.memset(abuf[:, 0:NT], 0.0))
        for j in range(JD):
            p.op("vector", lambda e, j=j: e.tensor_scalar(
                out=ambc[:, j * NT:(j + 1) * NT], in0=abuf[:, 0:NT], scalar1=0.0,
                scalar2=amc[:, j:j + 1], op0=ALU.mult, op1=ALU.add))
        p.op("vector", lambda e: e.drain())
        ev_init = p.op("vector", lambda e: e.nop(), "EV")
        p.wait("tensor", "EV", ev_init)
        mm_bb = p.op("tensor", lambda e: e.matmul(out=pb[6][:, 0:1], lhsT=ones_r[:],
                                                  rhs=bb1[:], start=True, stop=True), "MM")
        p.wait("vector", "MM", mm_bb)
        ev_bbb = p.op("vector", lambda e: e.tensor_copy(out=bbb[:], in_=pb[6][:, 0:1]), "EV")

        hb3full = hb.ap().rearrange("p (j t) -> p j t", j=JD)
        xs3 = xs[:].rearrange("p (j t) -> p j t", j=JD)
        vsq3 = vsq.rearrange("p (j t) -> p j t", j=JD)
        xi3 = xi_tb.rearrange("p (j t) -> p j t", j=JD)
        xs_free = (0, ev_init0)
        xi_free = (0, 0)
        psAB_free = ev_bbb
        psCD_free = ev_bbb
        prev_sq = 0
        ev_h = 0
        st_xi = 0
        for tb in range(TB_T):
            t0 = tb * NT
            p.wait("sync", "MM", xs_free[0])
            p.wait("sync", "EV", xs_free[1])
            lx_xs = p.dma("sync", xs3, x[:, t0:t0 + NT].rearrange("(j p) t -> p j t", p=128),
                          "LX")
            if True:
                mm_s1, ev_ln = _norm_block(
                    p, pb, xs3, vsq3, xi3, ones_c, ones_r, rows, ceps, 0, tmpNT, None,
                    True, prev_sq, psAB_free, psCD_free,
                    waits_t=[("LX", lx_xs), ("EV", ev_init), ("MM", xi_free[0])],
                    waits_s=[("LX", lx_xs)],
                    waits_v=[("ST", st_xi)])
                xs_free = (mm_s1, ev_ln)
                prev_sq = mm_s1
                psAB_free = ev_ln
                psCD_free = ev_ln
                ev_h = ev_ln
            if True:
                p.wait("sync", "EV", ev_ln)
                st_xi = p.dma("sync", xisc[:, t0:t0 + NT].rearrange("(j p) t -> p j t", p=128),
                              xi3, "ST")
            if True:
                mm_s2, ev_done = _norm_block(
                    p, pb, xi3, vsq3, hb3full[:, :, t0:t0 + NT], ones_c, ones_r,
                    rows, ceps, 1,
                    tmpNT, n1c, False, mm_s1, ev_ln, ev_ln,
                    waits_t=[("EV", ev_ln)], waits_s=[("EV", ev_ln)])
                prev_sq = mm_s2
                psAB_free = ev_done
                psCD_free = ev_done
                ev_h = ev_done
                xi_free = (mm_s2, ev_done)
        ev_h_all = ev_h

        def proj_matmuls(wslot, o, tb, pw, wait_es):
            if wait_es is not None:
                p.wait("tensor", wait_es[0], wait_es[1])
            mm = 0
            for k in range(JD):
                mm = p.op("tensor", lambda e, k=k, o=o, tb=tb, pw=pw, wslot=wslot: e.matmul(
                    out=pw[:], lhsT=wslot[:, k * 1024 + o * 128: k * 1024 + (o + 1) * 128],
                    rhs=hb[:, k * T + tb * NT: k * T + (tb + 1) * NT],
                    start=(k == 0), stop=(k == JD - 1)), "MM" if k == JD - 1 else None)
            return mm

        p.wait("gpsimd", "EV", ev_init0)
        p.dma("gpsimd", wbA[:].rearrange("p (k o) -> p k o", k=JD),
              wqt.rearrange("(k p) o -> p k o", p=128), "LWA")
        p.dwait("tensor", "LWA")
        p.wait("tensor", "EV", ev_h_all)
        st_slot = {}
        ev_evac = {}
        for tb in range(TB_T):
            for o in range(JD):
                i = tb * JD + o
                pw = pb[i % 4]
                if i >= 4:
                    p.wait("tensor", "EV", ev_evac[i - 4])
                mm = proj_matmuls(wbA, o, tb, pw, None)
                p.wait("vector", "MM", mm)
                if tb > 0:
                    p.dwait("vector", "ST")
                ev = p.op("vector", lambda e, o=o, pw=pw: e.tensor_copy(
                    out=kstage[:, o * NT:(o + 1) * NT], in_=pw[:]), "EV")
                ev_evac[i] = ev
                p.wait("sync", "EV", ev)
                st_slot[(tb, o)] = p.dma("sync", qsc[o, :, tb * NT:(tb + 1) * NT],
                                         kstage[:, o * NT:(o + 1) * NT], "ST")
        mm_q_done = p.tally["MM"]
        st_q_done = p.tally["ST"]

        p.wait("gpsimd", "MM", mm_q_done)
        p.dma("gpsimd", wbB[:].rearrange("p (k o) -> p k o", k=JD),
              wkt.rearrange("(k p) o -> p k o", p=128), "LWB")
        p.dwait("tensor", "LWB")
        p.wait("tensor", "EV", p.tally["EV"])
        ev_evac = {}
        es_sq = {}
        st_slot = {}
        st_floor = st_q_done
        prev_ss_mm = 0
        for tb in range(TB_T):
            mm_o = {}
            for o in range(JD):
                i = tb * JD + o
                pw = pb[i % 4]
                if i >= 4:
                    p.wait("tensor", "EV", ev_evac[i - 4])
                    p.wait("tensor", "ES", es_sq[i - 4])
                mm = proj_matmuls(wbB, o, tb, pw, None)
                mm_o[o] = mm
                p.wait("vector", "MM", mm)
                p.dwait("vector", "ST")
                ev = p.op("vector", lambda e, o=o, pw=pw: e.tensor_copy(
                    out=kstage[:, o * NT:(o + 1) * NT], in_=pw[:]), "EV")
                ev_evac[i] = ev
                if True:
                    p.wait("scalar", "MM", max(mm, prev_ss_mm))
                    p.wait("scalar", "EV", ev)
                    es_sq[i] = p.op("scalar", lambda e, o=o, pw=pw: e.activation(
                        out=vsq3[:, o, :], in_=pw[:], func=AF.Square), "ES")
            if True:
                p.wait("tensor", "ES", es_sq[tb * JD + JD - 1])
                p.wait("tensor", "EV", psAB_free)
                mm_ss = 0
                for o in range(JD):
                    mm_ss = p.op("tensor", lambda e, o=o: e.matmul(
                        out=pb[4][0:1, :], lhsT=ones_c[:], rhs=vsq3[:, o, :],
                        start=(o == 0), stop=(o == JD - 1)), "MM" if o == JD - 1 else None)
                prev_ss_mm = mm_ss
            if True:
                p.wait("vector", "MM", mm_ss)
                psAB_free = p.op("vector", lambda e: e.tensor_single_scalar(
                    out=rvar[:], in_=pb[4][0:1, :], scalar=1e-24, op=ALU.max), "EV")
                p.wait("scalar", "EV", psAB_free)
                p.op("scalar", lambda e: e.activation(out=rt2[:], in_=rvar[:], func=AF.Ln))
                p.op("scalar", lambda e: e.drain())
                es_r = p.op("scalar", lambda e: e.activation(out=rA[:], in_=rt2[:],
                                                             func=AF.Exp, scale=-0.5), "ES")
                p.wait("tensor", "ES", es_r)
                p.wait("tensor", "EV", psCD_free)
                mm_bc = p.op("tensor", lambda e: e.matmul(out=pb[6][:], lhsT=ones_r[:],
                                                          rhs=rA[:], start=True, stop=True), "MM")
            if True:
                p.wait("vector", "MM", mm_bc)
                lastev = 0
                for o in range(JD):
                    lastev = p.op("vector", lambda e, o=o: e.tensor_mul(
                        out=kstage[:, o * NT:(o + 1) * NT],
                        in0=kstage[:, o * NT:(o + 1) * NT], in1=pb[6][:]), "EV")
                    p.wait("sync", "EV", lastev)
                    st_slot[(tb, o)] = p.dma("sync", ksc[o, :, tb * NT:(tb + 1) * NT],
                                             kstage[:, o * NT:(o + 1) * NT], "ST")
                psCD_free = lastev
        mm_k_done = p.tally["MM"]

        p.wait("gpsimd", "MM", mm_k_done)
        p.dma("gpsimd", wbA[:].rearrange("p (k o) -> p k o", k=JD),
              wvt.rearrange("(k p) o -> p k o", p=128), "LWA")
        p.dwait("tensor", "LWA")
        p.wait("tensor", "EV", p.tally["EV"])
        ev_evac = {}
        st_slot = {}
        st_floor = p.tally["ST"]
        for tb in range(TB_T):
            for o in range(JD):
                i = tb * JD + o
                pw = pb[i % 4]
                if i >= 4:
                    p.wait("tensor", "EV", ev_evac[i - 4])
                mm = proj_matmuls(wbA, o, tb, pw, None)
                p.wait("vector", "MM", mm)
                ev = p.op("vector", lambda e, o=o, pw=pw: e.tensor_scalar_mul(
                    out=xs[:, o * NT:(o + 1) * NT], in0=pw[:], scalar1=alc[:, o:o + 1]), "EV")
                ev_evac[i] = ev
            for o in range(JD):
                p.dwait("vector", "ST")
                p.op("vector", lambda e: e.drain())
                p.op("vector", lambda e, o=o: e.tensor_tensor_scan(
                    out=kstage[:, o * NT:(o + 1) * NT].bitcast(F32) if False else
                    xs[:, o * NT:(o + 1) * NT], data0=ambc[:, o * NT:(o + 1) * NT],
                    data1=xs[:, o * NT:(o + 1) * NT], initial=carry[o][:],
                    op0=ALU.mult, op1=ALU.add))
                ev = p.op("vector", lambda e, o=o: e.tensor_copy(
                    out=kstage[:, o * NT:(o + 1) * NT], in_=xs[:, o * NT:(o + 1) * NT]), "EV")
                p.op("vector", lambda e: e.drain())
                p.op("vector", lambda e, o=o: e.tensor_copy(
                    out=carry[o][:], in_=xs[:, (o + 1) * NT - 1:(o + 1) * NT]))
                p.wait("sync", "EV", ev)
                st_slot[(tb, o)] = p.dma("sync", vsc[o, :, tb * NT:(tb + 1) * NT],
                                         kstage[:, o * NT:(o + 1) * NT], "ST")
            p.op("vector", lambda e: e.drain())
        mm_v_done = p.tally["MM"]
        ev_v_done = p.tally["EV"]

        p.wait("tensor", "EV", p.tally["EV"])
        es_bet = {}
        for c in range(NC):
            pw = pb[4 + c % 2]
            if c >= 2:
                p.wait("tensor", "ES", es_bet[c - 2])
            mm = 0
            for k in range(JD):
                mm = p.op("tensor", lambda e, k=k, c=c, pw=pw: e.matmul(
                    out=pw[:, 0:1], lhsT=hb[:, k * T + c * 128: k * T + (c + 1) * 128],
                    rhs=bwc[:, k:k + 1], start=(k == 0), stop=(k == JD - 1)),
                    "MM" if k == JD - 1 else None)
            p.wait("scalar", "MM", mm)
            es_bet[c] = p.op("scalar", lambda e, c=c, pw=pw: e.activation(
                out=betc[:, c:c + 1], in_=pw[:, 0:1], func=AF.Sigmoid, bias=bbb[:]), "ES")
        es_beta_done = p.tally["ES"]

        pb6b = pb[6].bitcast(BF16)
        p.wait("vector", "ES", es_beta_done)
        p.op("vector", lambda e: e.memset(Sf[:, :], 0.0))
        ev_S0 = p.op("vector", lambda e: e.memset(Sbb[:, :], 0.0), "EV")

        sb_ready = ("EV", ev_S0)
        loads_free_mm = 0
        ev_sf_last = 0
        st_o = {}
        for c in range(NC):
            cs = slice(c * CH, (c + 1) * CH)
            p.wait("sync", "MM", loads_free_mm)
            lq = p.dma("sync", qf.rearrange("p (j t) -> p j t", j=JD),
                       qsc[:, :, cs].rearrange("j p t -> p j t"), "LX")
            lk = p.dma("sync", kf.rearrange("p (j t) -> p j t", j=JD),
                       ksc[:, :, cs].rearrange("j p t -> p j t"), "LX")
            lv = p.dma("sync", vstg.rearrange("p (j t) -> p j t", j=JD),
                       vsc[:, :, cs].rearrange("j p t -> p j t"), "LX")
            p.wait("tensor", sb_ready[0], sb_ready[1])
            p.dwait("tensor", "LX")
            mm = 0
            for j in range(JD):
                mm = p.op("tensor", lambda e, j=j: e.transpose(
                    out=pb6b[:, j * 128:(j + 1) * 128], in_=kf[:, j * 128:(j + 1) * 128],
                    identity=identb[:]), "MM" if j == JD - 1 else None)
            mm_trk = mm
            p.wait("vector", "MM", mm_trk)
            ev_ktm = p.op("vector", lambda e: e.tensor_copy(out=ktm[:], in_=pb6b[:]), "EV")
            p.wait("tensor", "EV", ev_ktm)
            for j in range(JD):
                mm = p.op("tensor", lambda e, j=j: e.transpose(
                    out=pb6b[:, j * 128:(j + 1) * 128], in_=vstg[:, j * 128:(j + 1) * 128],
                    identity=identb[:]), "MM" if j == JD - 1 else None)
            mm_trv = mm
            p.wait("vector", "MM", mm_trv)
            ev_vtm = p.op("vector", lambda e: e.tensor_copy(out=vtm[:], in_=pb6b[:]), "EV")
            mm = 0
            for h in range(2):
                for j in range(JD):
                    mm = p.op("tensor", lambda e, h=h, j=j: e.matmul(
                        out=pb[h][:], lhsT=kf[:, j * 128:(j + 1) * 128],
                        rhs=Sbb[:, j * 1024 + h * NT: j * 1024 + (h + 1) * NT],
                        start=(j == 0), stop=(j == JD - 1)),
                        "MM" if (h == 1 and j == JD - 1) else None)
            mm_m1 = mm
            for j in range(JD):
                mm = p.op("tensor", lambda e, j=j: e.matmul(
                    out=pb[4][:, 0:128], lhsT=kf[:, j * 128:(j + 1) * 128],
                    rhs=kf[:, j * 128:(j + 1) * 128],
                    start=(j == 0), stop=(j == JD - 1)), "MM" if j == JD - 1 else None)
            mm_G = mm
            for j in range(JD):
                mm = p.op("tensor", lambda e, j=j: e.matmul(
                    out=pb[5][:, 0:128], lhsT=qf[:, j * 128:(j + 1) * 128],
                    rhs=kf[:, j * 128:(j + 1) * 128],
                    start=(j == 0), stop=(j == JD - 1)), "MM" if j == JD - 1 else None)
            mm_W = mm
            for h in range(2):
                for j in range(JD):
                    p.op("tensor", lambda e, h=h, j=j: e.matmul(
                        out=pb[2 + h][:], lhsT=qf[:, j * 128:(j + 1) * 128],
                        rhs=Sbb[:, j * 1024 + h * NT: j * 1024 + (h + 1) * NT],
                        start=(j == 0), stop=False))
            p.wait("vector", "MM", mm_G)
            ev_A = p.op("vector", lambda e, c=c: e.scalar_tensor_tensor(
                out=pwr[:, 0:128], in0=pb[4][:, 0:128], scalar=betc[:, c:c + 1],
                in1=maskl[:], op0=ALU.mult, op1=ALU.mult), "EV")
            p.wait("vector", "MM", mm_W)
            ev_W = p.op("vector", lambda e: e.tensor_mul(
                out=wtb[:], in0=pb[5][:, 0:128], in1=maski[:]), "EV")
            p.wait("vector", "MM", mm_m1)
            p.wait("vector", "EV", ev_vtm)
            p.op("vector", lambda e: e.tensor_sub(out=t1f[:, 0:NT], in0=vtm[:, 0:NT],
                                                  in1=pb[0][:]))
            p.op("vector", lambda e: e.tensor_sub(out=t1f[:, NT:2 * NT], in0=vtm[:, NT:2 * NT],
                                                  in1=pb[1][:]))
            ev_U0 = p.op("vector", lambda e, c=c: e.tensor_scalar_mul(
                out=ubuf[:], in0=t1f[:], scalar1=betc[:, c:c + 1]), "EV")
            p.wait("tensor", "EV", ev_A)
            ev_P = ev_A
            for m in range(6):
                p.wait("tensor", "EV", ev_P)
                mm_T = p.op("tensor", lambda e, m=m: e.transpose(
                    out=pb6b[:, m * 128:(m + 1) * 128], in_=pwr[:, m * 128:(m + 1) * 128],
                    identity=identb[:]), "MM")
                p.wait("vector", "MM", mm_T)
                ev_T = p.op("vector", lambda e, m=m: e.tensor_copy(
                    out=pwrT[:, m * 128:(m + 1) * 128],
                    in_=pb6b[:, m * 128:(m + 1) * 128]), "EV")
                p.wait("tensor", "EV", ev_T)
                mm_P = p.op("tensor", lambda e, m=m: e.matmul(
                    out=pb[7][:, 0:128], lhsT=pwrT[:, m * 128:(m + 1) * 128],
                    rhs=pwr[:, m * 128:(m + 1) * 128], start=True, stop=True), "MM")
                p.wait("vector", "MM", mm_P)
                ev_P = p.op("vector", lambda e, m=m: e.tensor_copy(
                    out=pwr[:, (m + 1) * 128:(m + 2) * 128], in_=pb[7][:, 0:128]), "EV")
            p.wait("tensor", "EV", ev_P)
            mm_T6 = p.op("tensor", lambda e: e.transpose(
                out=pb6b[:, 6 * 128:7 * 128], in_=pwr[:, 6 * 128:7 * 128],
                identity=identb[:]), "MM")
            p.wait("vector", "MM", mm_T6)
            ev_T6 = p.op("vector", lambda e: e.tensor_copy(
                out=pwrT[:, 6 * 128:7 * 128], in_=pb6b[:, 6 * 128:7 * 128]), "EV")
            ev_app = ev_U0
            for m in [6, 5, 4, 3, 2, 1, 0]:
                p.wait("tensor", "EV", ev_app if m != 6 else max(ev_app, ev_T6))
                if m == 6:
                    p.wait("tensor", "EV", ev_T6)
                mm1 = p.op("tensor", lambda e, m=m: e.matmul(
                    out=pb[7][:], lhsT=pwrT[:, m * 128:(m + 1) * 128],
                    rhs=ubuf[:, 0:NT], start=True, stop=True), "MM")
                mm2 = p.op("tensor", lambda e, m=m: e.matmul(
                    out=pb[0][:], lhsT=pwrT[:, m * 128:(m + 1) * 128],
                    rhs=ubuf[:, NT:2 * NT], start=True, stop=True), "MM")
                op = ALU.add if m > 0 else ALU.subtract
                p.wait("vector", "MM", mm2)
                p.op("vector", lambda e, op=op: e.tensor_tensor(
                    out=ubuf[:, 0:NT], in0=ubuf[:, 0:NT], in1=pb[7][:], op=op))
                ev_app = p.op("vector", lambda e, op=op: e.tensor_tensor(
                    out=ubuf[:, NT:2 * NT], in0=ubuf[:, NT:2 * NT], in1=pb[0][:], op=op), "EV")
            ev_Ufinal = ev_app
            p.wait("tensor", "EV", max(ev_W, ev_T6))
            mm_wt = p.op("tensor", lambda e: e.transpose(
                out=pb6b[:, 7 * 128:8 * 128], in_=wtb[:], identity=identb[:]), "MM")
            p.wait("vector", "MM", mm_wt)
            ev_wtT = p.op("vector", lambda e: e.tensor_copy(
                out=wtT[:], in_=pb6b[:, 7 * 128:8 * 128]), "EV")
            p.wait("tensor", "EV", max(ev_Ufinal, ev_wtT))
            p.op("tensor", lambda e: e.matmul(out=pb[2][:], lhsT=wtT[:], rhs=ubuf[:, 0:NT],
                                              start=False, stop=True))
            mm_O = p.op("tensor", lambda e: e.matmul(out=pb[3][:], lhsT=wtT[:],
                                                     rhs=ubuf[:, NT:2 * NT],
                                                     start=False, stop=True), "MM")
            loads_free_mm = mm_O
            p.wait("vector", "MM", mm_O)
            p.op("vector", lambda e: e.tensor_copy(out=otm[:, 0:NT], in_=pb[2][:]))
            ev_otm = p.op("vector", lambda e: e.tensor_copy(out=otm[:, NT:2 * NT],
                                                            in_=pb[3][:]), "EV")
            p.wait("tensor", "EV", max(ev_otm, ev_wtT))
            for j in range(JD):
                mm = p.op("tensor", lambda e, j=j: e.transpose(
                    out=pb6b[:, j * 128:(j + 1) * 128], in_=otm[:, j * 128:(j + 1) * 128],
                    identity=identb[:]), "MM" if j == JD - 1 else None)
            mm_otr = mm
            p.wait("vector", "MM", mm_otr)
            p.dwait("vector", "ST")
            ev_ofm = p.op("vector", lambda e: e.tensor_copy(out=ofm[:], in_=pb6b[:]), "EV")
            p.wait("sync", "EV", ev_ofm)
            st_o[c] = p.dma("sync", osc[:, :, cs].rearrange("j p t -> p j t"),
                            ofm.rearrange("p (j t) -> p j t", j=JD), "ST")
            ev_sf = {}
            for j in range(JD):
                pwa, pwb = (pb[0], pb[1]) if j % 2 == 0 else (pb[4], pb[5])
                if j >= 2:
                    p.wait("tensor", "EV", ev_sf[j - 2])
                elif j == 0:
                    p.wait("tensor", "EV", ev_Ufinal)
                mm = 0
                for h in range(2):
                    mm = p.op("tensor", lambda e, j=j, h=h, pwa=pwa, pwb=pwb: e.matmul(
                        out=(pwa if h == 0 else pwb)[:], lhsT=ktm[:, j * 128:(j + 1) * 128],
                        rhs=ubuf[:, h * NT:(h + 1) * NT], start=True, stop=True),
                        "MM" if h == 1 else None)
                p.wait("vector", "MM", mm)
                p.op("vector", lambda e, j=j, pwa=pwa: e.tensor_add(
                    out=Sf[:, j * 1024: j * 1024 + NT], in0=Sf[:, j * 1024: j * 1024 + NT],
                    in1=pwa[:]))
                ev_sf[j] = p.op("vector", lambda e, j=j, pwb=pwb: e.tensor_add(
                    out=Sf[:, j * 1024 + NT: (j + 1) * 1024],
                    in0=Sf[:, j * 1024 + NT: (j + 1) * 1024], in1=pwb[:]), "EV")
                p.wait("scalar", "EV", ev_sf[j])
                es_sb = p.op("scalar", lambda e, j=j: e.activation(
                    out=Sbb[:, j * 1024:(j + 1) * 1024], in_=Sf[:, j * 1024:(j + 1) * 1024],
                    func=AF.Copy), "ES")
            sb_ready = ("ES", es_sb)
            ev_sf_last = ev_sf[JD - 1]
        mm_deltadone = p.tally["MM"]
        ev_deltadone = p.tally["EV"]
        es_deltadone = p.tally["ES"]
        st_odone = p.tally["ST"]

        wb_after = {}
        xnew3 = xnew.rearrange("p (j t) -> p j t", j=JD)
        xresC3 = xresC.rearrange("p (j t) -> p j t", j=JD)
        h2st3 = h2st.rearrange("p (j t) -> p j t", j=JD)
        wb_free_mm = mm_deltadone
        wb_free_wb = 0
        ost_free = (mm_deltadone, 0)
        xnew_free_ev = 0
        mbuf_free_mm = 0
        st_y = {}
        prev_sq = 0
        psAB_free = p.tally["EV"]
        psCD_free = p.tally["EV"]
        p.wait("vector", "EV", psAB_free)

        def wchunk(tag, o, wsrc, wc, ktiles, slot, free_mm, s, lsem, wsem):
            if free_mm:
                p.wait("gpsimd", "MM", free_mm)
            if s == 0:
                if (tag, o - 2) in wb_after:
                    p.dwait("gpsimd", wsem)
                view3 = slot.rearrange("p (k o) -> p k o", k=ktiles)
                src = wsrc[:, o * 128:(o + 1) * 128].rearrange("(k p) o -> p k o", p=128)
                p.dma("gpsimd", view3, src, lsem)
                p.dwait("sync", lsem)
                wb_after[(tag, o)] = p.dma("sync", wc[o], slot, wsem)
            else:
                p.dwait("gpsimd", wsem)
                p.dma("gpsimd", slot, wc[o], lsem)

        for s in range(NS):
            p.wait("sync", "MM", ost_free[0])
            p.wait("sync", "EV", xnew_free_ev)
            lo = p.dma("sync", ostripe.rearrange("p (j t) -> p j t", j=JD),
                       osc[:, :, s * S:(s + 1) * S].rearrange("j p t -> p j t"), "LX")
            lxr = p.dma("sync", xresC3, xisc[:, s * S:(s + 1) * S].rearrange(
                "(j p) t -> p j t", p=128), "LX")
            g1_done = {}
            ev_g1 = {}
            for o in range(JD):
                lsem, wsem = ("LWA", "WBA") if o % 2 == 0 else ("LWB", "WBB")
                slot = (wbA if o % 2 == 0 else wbB)[:, 0:JD * 128]
                wchunk(1, o, wot, wco, JD, slot, g1_done.get(o - 2, wb_free_mm), s,
                       lsem, wsem)
                p.dwait("tensor", lsem)
                if o == 0:
                    p.dwait("tensor", "LX")
                mm = 0
                for tb in range(TB):
                    i = o * TB + tb
                    pw = pb[i % 4]
                    if i >= 4:
                        p.wait("tensor", "EV", ev_g1[i - 4])
                    for k in range(JD):
                        mm = p.op("tensor", lambda e, k=k, tb=tb, slot=slot, pw=pw: e.matmul(
                            out=pw[:], lhsT=slot[:, k * 128:(k + 1) * 128],
                            rhs=ostripe[:, k * S + tb * NT: k * S + (tb + 1) * NT],
                            start=(k == 0), stop=(k == JD - 1)),
                            "MM" if k == JD - 1 else None)
                    p.wait("vector", "MM", mm)
                    p.dwait("vector", "LX")
                    if i == 0:
                        p.dwait("vector", "ST")
                    ev_g1[i] = p.op("vector", lambda e, o=o, tb=tb, pw=pw: e.tensor_add(
                        out=xnew[:, o * S + tb * NT: o * S + (tb + 1) * NT],
                        in0=xresC[:, o * S + tb * NT: o * S + (tb + 1) * NT], in1=pw[:]), "EV")
                g1_done[o] = mm
            mm_c1_done = p.tally["MM"]
            ev_c1_all = ev_g1[JD * TB - 1]
            wb_free_mm = mm_c1_done
            p.wait("sync", "EV", ev_c1_all)
            lxr2 = p.dma("sync", xresC3, x[:, s * S:(s + 1) * S].rearrange(
                "(j p) t -> p j t", p=128), "LX")
            vsqC3 = hb_f[:, 8192:8192 + JD * NT].rearrange("p (j t) -> p j t", j=JD)
            ev_h2 = {}
            for tb in range(TB):
                src3 = xnew3[:, :, tb * NT:(tb + 1) * NT]
                dst3 = h2st3[:, :, tb * NT:(tb + 1) * NT]
                mm_stats, ev_done = _norm_block(
                    p, pb, src3, vsqC3, dst3, ones_c, ones_r, rows, ceps, 1, tmpNT, n2c,
                    False, prev_sq, psAB_free, psCD_free,
                    waits_t=[("EV", ev_c1_all), ("MM", mm_c1_done)],
                    waits_s=[("EV", ev_c1_all)])
                prev_sq = mm_stats
                psAB_free = ev_done
                psCD_free = ev_done
                ev_h2[tb] = ev_done
            ev_h2_all = ev_h2[TB - 1]
            es_g3 = {}
            ev_g3 = {}
            g3_done = {}
            for o in range(JH2):
                slotg = wbA[:, (o % 2) * JD * 128:(o % 2 + 1) * JD * 128]
                slotu = wbB[:, (o % 2) * JD * 128:(o % 2 + 1) * JD * 128]
                wchunk(3, o, wgt, wcg, JD, slotg, g3_done.get(o - 2, wb_free_mm), s,
                       "LWA", "WBA")
                wchunk(4, o, wut, wcu, JD, slotu, 0, s, "LWB", "WBB")
                p.dwait("tensor", "LWA")
                p.dwait("tensor", "LWB")
                if o == 0:
                    p.wait("tensor", "EV", ev_h2_all)
                for tb in range(TB):
                    i = o * TB + tb
                    pw1, pw2 = pb[i % 2], pb[2 + i % 2]
                    if i >= 2:
                        p.wait("tensor", "ES", es_g3[i - 2])
                        p.wait("tensor", "EV", ev_g3[i - 2])
                    for k in range(JD):
                        p.op("tensor", lambda e, k=k, tb=tb, slotg=slotg, pw1=pw1: e.matmul(
                            out=pw1[:], lhsT=slotg[:, k * 128:(k + 1) * 128],
                            rhs=h2st[:, k * S + tb * NT: k * S + (tb + 1) * NT],
                            start=(k == 0), stop=(k == JD - 1)))
                    mm = 0
                    for k in range(JD):
                        mm = p.op("tensor", lambda e, k=k, tb=tb, slotu=slotu, pw2=pw2:
                                  e.matmul(
                            out=pw2[:], lhsT=slotu[:, k * 128:(k + 1) * 128],
                            rhs=h2st[:, k * S + tb * NT: k * S + (tb + 1) * NT],
                            start=(k == 0), stop=(k == JD - 1)),
                            "MM" if k == JD - 1 else None)
                    p.wait("scalar", "MM", mm)
                    es_g3[i] = p.op("scalar", lambda e, i=i, pw1=pw1: e.activation(
                        out=abuf[:, (i % 2) * NT:(i % 2 + 1) * NT], in_=pw1[:],
                        func=AF.Silu), "ES")
                    p.wait("vector", "MM", mm)
                    p.wait("vector", "ES", es_g3[i])
                    if i == 0:
                        p.wait("vector", "MM", mbuf_free_mm)
                    ev_g3[i] = p.op("vector", lambda e, i=i, o=o, tb=tb, pw2=pw2:
                                    e.tensor_mul(
                        out=mbuf[:, o * S + tb * NT: o * S + (tb + 1) * NT],
                        in0=abuf[:, (i % 2) * NT:(i % 2 + 1) * NT], in1=pw2[:]), "EV")
                g3_done[o] = mm
            ev_c3_all = ev_g3[JH2 * TB - 1]
            mm_c3_done = p.tally["MM"]
            wb_free_mm = mm_c3_done
            g4_done = {}
            ev_g4 = {}
            for o in range(JD):
                lsem, wsem = ("LWA", "WBA") if o % 2 == 0 else ("LWB", "WBB")
                slot = (wbA if o % 2 == 0 else wbB)[:, 0:JH2 * 128]
                wchunk(5, o, wdt, wcd, JH2, slot, g4_done.get(o - 2, wb_free_mm), s,
                       lsem, wsem)
                p.dwait("tensor", lsem)
                if o == 0:
                    p.wait("tensor", "EV", ev_c3_all)
                for tb in range(TB):
                    i = o * TB + tb
                    pw = pb[i % 4]
                    if i >= 4:
                        p.wait("tensor", "EV", ev_g4[i - 4])
                    mm = 0
                    for k in range(JH2):
                        mm = p.op("tensor", lambda e, k=k, tb=tb, slot=slot, pw=pw: e.matmul(
                            out=pw[:], lhsT=slot[:, k * 128:(k + 1) * 128],
                            rhs=mbuf[:, k * S + tb * NT: k * S + (tb + 1) * NT],
                            start=(k == 0), stop=(k == JH2 - 1)),
                            "MM" if k == JH2 - 1 else None)
                    p.wait("vector", "MM", mm)
                    p.dwait("vector", "LX")
                    p.op("vector", lambda e, o=o, tb=tb, pw=pw: e.tensor_add(
                        out=xnew[:, o * S + tb * NT: o * S + (tb + 1) * NT],
                        in0=xnew[:, o * S + tb * NT: o * S + (tb + 1) * NT], in1=pw[:]))
                    ev_g4[i] = p.op("vector", lambda e, o=o, tb=tb: e.tensor_add(
                        out=xnew[:, o * S + tb * NT: o * S + (tb + 1) * NT],
                        in0=xnew[:, o * S + tb * NT: o * S + (tb + 1) * NT],
                        in1=xresC[:, o * S + tb * NT: o * S + (tb + 1) * NT]), "EV")
                    p.wait("sync", "EV", ev_g4[i])
                    st_y[(s, i)] = p.dma(
                        "sync", y[o * 128:(o + 1) * 128, s * S + tb * NT: s * S + (tb + 1) * NT],
                        xnew[:, o * S + tb * NT: o * S + (tb + 1) * NT], "ST")
                g4_done[o] = mm
            mm_c4_done = p.tally["MM"]
            wb_free_mm = mm_c4_done
            mbuf_free_mm = mm_c4_done
            ost_free = (mm_c3_done, 0)
            xnew_free_ev = p.tally["EV"]

        p.dwait("sync", "ST")
        p.dwait("sync", "WBA")
        p.dwait("sync", "WBB")


def build_l0attn():
    nc = bass.Bass("TRN2", target_bir_lowering=False, detect_race_conditions=RACE_DETECT)
    x = nc.dram_tensor("x", [D, T], F32, kind="ExternalInput").ap()
    y = nc.dram_tensor("y", [D, T], F32, kind="ExternalOutput").ap()
    with contextlib.ExitStack() as es:
        p = Prog(nc)
        p.make_sems(es, ALLSEMS)
        Cx = Ctx(nc, p, es)
        emit_l0attn(Cx, x, y, wp="", first=True)
        with nc.Block() as block:
            p.replay(block)
    return nc


def layout_l0attn(inp):
    f = np.float32
    tril = np.tril(np.ones((128, 128), f), -1)
    trili = np.tril(np.ones((128, 128), f), 0)
    return dict(
        wqt=np.ascontiguousarray(np.asarray(inp["L0_attn_Wq"], f)),
        wkt=np.ascontiguousarray(np.asarray(inp["L0_attn_Wk"], f)),
        wvt=np.ascontiguousarray(np.asarray(inp["L0_attn_Wv"], f)),
        wot=np.ascontiguousarray(np.asarray(inp["L0_attn_Wo"], f)),
        wgt=np.ascontiguousarray(np.asarray(inp["L0_attn_w_gate"], f)),
        wut=np.ascontiguousarray(np.asarray(inp["L0_attn_w_up"], f)),
        wdt=np.ascontiguousarray(np.asarray(inp["L0_attn_w_down"], f)),
        n1v=_colmajor(inp["L0_attn_norm1_w"]),
        n2v=_colmajor(inp["L0_attn_norm2_w"]),
        alv=_colmajor(inp["L0_attn_alpha_logit"]),
        bwv=_colmajor(np.asarray(inp["L0_attn_beta_w"], f)[:, 0]),
        bbv=np.asarray(inp["L0_attn_beta_b"], f).reshape(1, 1),
        msl=tril, mli=trili,
    )


def emit_l1attn(C, xin, xout, wp="a1", first=True):
    NH = 8
    DK = 128
    DC = 32
    DR = 32
    SCALE = 1.0 / float(np.sqrt(DK + DR))

    nc = C.nc
    p = C.p
    nm = (wp + "_") if wp else ""
    x, y = xin, xout
    wc96d = C.dram(nm + "wc96", [D, 96])
    wuqd = C.dram(nm + "wuq", [DC, D])
    wukd = C.dram(nm + "wuk", [DC, D])
    wuvd = C.dram(nm + "wuv", [DC, D])
    wqrd = C.dram(nm + "wqr", [DC, NH * DR])
    wod = C.dram(nm + "wo", [D, D])
    bc96d = C.dram(nm + "bc96", [96, 1])
    buqd = C.dram(nm + "buq", [128, NH])
    bukd = C.dram(nm + "buk", [128, NH])
    buvd = C.dram(nm + "buv", [1, D])
    bqrd = C.dram(nm + "bqr", [DR, NH])
    bod = C.dram(nm + "bo", [128, JD])
    osc = C.dram(nm + "osc", [NH, 128, T], BF16, "Internal")

    if True:
        xs = C.xs
        arena = C.arena
        hbt = C.hb
        hb_f = C.hb.bitcast(F32)
        cq = hbt[0:32, 0:T]
        xitb = hbt[:, 4096:8192]
        akt = hbt[:, 8192:9216]
        ostg = hbt[:, 9216:10240]
        ybuf = hb_f[:, 5120:6144]
        sq = hb_f[:, 6144:6656]
        vsq = hb_f[:, 8192:12288]
        wo = C.wbA
        wc96 = C.wbB[:, 0:JD * 96]
        wuq = C.wbB[0:DC, 768:768 + D]
        wuk = C.wbB[0:DC, 1792:1792 + D]
        wuv = C.wbB[0:DC, 2816:2816 + D]
        wqr = C.wbB[0:DC, 3840:3840 + NH * DR]
        bc96 = C.sb(nm + "bc96s", [96, 1], F32)
        buq = C.sb(nm + "buqs", [128, NH], F32)
        buk = C.sb(nm + "buks", [128, NH], F32)
        buv = C.abuf[0:1, 0:D]
        bqr = C.sb(nm + "bqrs", [DR, NH], F32)
        bo = C.sb(nm + "bos", [128, JD], F32)
        bvh = hb_f[:, 7296:7424]
        ones_r = C.ones_r
        ones_c = C.ones_c
        ones128 = hbt[:, 14848:14976]
        km33 = hb_f[0:33, 7168:7170]
        rowb33 = hb_f[0:33, 6656:7168]
        rmean, rvar, rt2, rA = C.rmean, C.rvar, C.rt2, C.rA
        ceps = C.ceps
        pb = C.pb
        rows = (rmean, rvar, rt2, rA, rt2)

        cslab = arena[0:96, 0:T]
        kr2 = arena[0:33, T:2 * T]
        qh = arena[:, 2 * T:3 * T]
        qr2 = arena[0:33, 3 * T:4 * T]
        kch = arena[:, 4 * T:5 * T]
        v_tm = arena[:, 5 * T:6 * T]
        arena_f = arena.bitcast(F32)
        ostripe = arena[:, 0:JD * 1024]
        xresC = arena_f[:, 4096:4096 + JD * 1024]

        C.init_common()

        p.wait("gpsimd", "MM", p.tally["MM"])
        p.dwait("gpsimd", "WBA")
        p.dwait("gpsimd", "WBB")
        p.dma("gpsimd", wc96.rearrange("p (k o) -> p k o", k=JD),
              wc96d.rearrange("(k p) o -> p k o", p=128), "LWA")
        p.dma("gpsimd", wuq, wuqd, "LWA")
        p.dma("gpsimd", wuk, wukd, "LWA")
        p.dma("gpsimd", wuv, wuvd, "LWA")
        p.dma("gpsimd", wqr, wqrd, "LWA")
        p.dma("gpsimd", wo[:].rearrange("p (k o) -> p k o", k=JD),
              wod.rearrange("(k p) o -> p k o", p=128), "LWA")
        for dst, srcd in ((bc96, bc96d), (buq, buqd), (buk, bukd), (buv, buvd),
                          (bqr, bqrd), (bo, bod)):
            p.dma("sync", dst[:], srcd, "LX")
        p.op("vector", lambda e: e.memset(ones128, 1.0))
        p.dwait("vector", "LX")
        p.op("vector", lambda e: e.tensor_scalar_mul(out=buq[:], in0=buq[:], scalar1=SCALE))
        p.op("vector", lambda e: e.tensor_scalar_mul(out=bqr[:], in0=bqr[:], scalar1=SCALE))
        p.op("vector", lambda e: e.drain())
        ev_init = p.op("vector", lambda e: e.nop(), "EV")
        p.dwait("tensor", "LWA")

        xs3 = xs[:].rearrange("p (j t) -> p j t", j=JD)
        xi3 = xitb.rearrange("p (j t) -> p j t", j=JD)
        vsq3 = vsq.rearrange("p (j t) -> p j t", j=JD)
        xs_free = (0, ev_init)
        psAB_free = ev_init
        psCD_free = ev_init
        prev_sq = 0
        pbfree = {i: 0 for i in range(8)}
        pbfree_es = {}
        ev_cp = {}
        for tb in range(T // NT):
            t0 = tb * NT
            p.wait("sync", "MM", xs_free[0])
            p.wait("sync", "EV", xs_free[1])
            lx_xs = p.dma("sync", xs3, x[:, t0:t0 + NT].rearrange("(j p) t -> p j t", p=128),
                          "LX")
            mm_ln, ev_ln = _norm_block(
                p, pb, xs3, vsq3, xi3, ones_c, ones_r, rows, ceps, 0, sq[:, 0:NT], None,
                True, prev_sq, psAB_free, psCD_free,
                waits_t=[("LX", lx_xs), ("EV", ev_init)], waits_s=[("LX", lx_xs)])
            prev_sq = mm_ln
            psAB_free = ev_ln
            psCD_free = ev_ln
            xs_free = (mm_ln, ev_ln)
            pw = pb[tb % 2]
            p.wait("tensor", "EV", max(pbfree[tb % 2], ev_ln))
            mm = 0
            for k in range(JD):
                mm = p.op("tensor", lambda e, k=k, pw=pw: e.matmul(
                    out=pw[0:96, :], lhsT=wc96[:, k * 96:(k + 1) * 96],
                    rhs=xitb[:, k * NT:(k + 1) * NT],
                    start=(k == 0), stop=(k == JD - 1)), "MM" if k == JD - 1 else None)
            p.wait("vector", "MM", mm)
            ev = p.op("vector", lambda e, tb=tb, pw=pw: e.tensor_scalar_add(
                out=cslab[:, tb * NT:(tb + 1) * NT].to_broadcast([96, NT]) if False
                else cslab[:, tb * NT:(tb + 1) * NT],
                in0=pw[0:96, :], scalar1=bc96[:]), "EV")
            pbfree[tb % 2] = ev
            ev_cp[tb] = ev
        ev_cp_all = ev_cp[T // NT - 1]
        pbfree[4] = psAB_free
        pbfree[5] = psAB_free
        pbfree[6] = psCD_free
        pbfree[7] = psCD_free
        p.wait("sync", "EV", ev_cp_all)
        p.dma("sync", kr2[0:32, :], cslab[32:64, :], "LX")
        p.dma("sync", cq, cslab[64:96, :], "LX")
        p.dwait("vector", "LX")
        ev_kr = p.op("vector", lambda e: e.memset(kr2[32:33, :], 1.0), "EV")

        st_osc = {}
        for h in range(NH):
            hs = slice(h * 128, (h + 1) * 128)
            p.wait("tensor", "EV", max(pbfree[6], ev_kr))
            mm_bv = p.op("tensor", lambda e, h=h: e.matmul(
                out=pb[6][:, 0:128], lhsT=ones_r[:], rhs=buv[0:1, h * 128:(h + 1) * 128],
                start=True, stop=True), "MM")
            p.wait("vector", "MM", mm_bv)
            ev_bvh = p.op("vector", lambda e: e.tensor_copy(out=bvh, in_=pb[6][:, 0:128]),
                          "EV")
            pbfree[6] = ev_bvh
            p.op("vector", lambda e: e.memset(km33[32:33, 0:1], 0.0))
            es_k = {}
            for kb in range(T // NT):
                pw = pb[kb % 2]
                p.wait("tensor", "ES", pbfree_es.get(kb % 2, 0))
                p.wait("tensor", "EV", pbfree[kb % 2])
                mm = p.op("tensor", lambda e, h=h, kb=kb, pw=pw: e.matmul(
                    out=pw[:], lhsT=wuk[:, h * 128:(h + 1) * 128],
                    rhs=cslab[0:32, kb * NT:(kb + 1) * NT], start=True, stop=True), "MM")
                p.wait("scalar", "MM", mm)
                es1 = p.op("scalar", lambda e, h=h, kb=kb, pw=pw: e.activation(
                    out=kch[:, kb * NT:(kb + 1) * NT], in_=pw[:], func=AF.Identity,
                    bias=buk[:, h:h + 1]), "ES")
                pbfree_es[kb % 2] = es1
                es2 = p.op("scalar", lambda e, kb=kb: e.activation(
                    out=sq[:, 0:NT], in_=kch[:, kb * NT:(kb + 1) * NT], func=AF.Square),
                    "ES")
                p.wait("tensor", "ES", es2)
                p.wait("tensor", "EV", pbfree[4])
                mmq = p.op("tensor", lambda e: e.matmul(
                    out=pb[4][32:33, :], lhsT=ones_c[:], rhs=sq[:, 0:NT],
                    start=True, stop=True), "MM")
                p.wait("vector", "MM", mmq)
                p.op("vector", lambda e: e.tensor_reduce(
                    out=km33[32:33, 1:2], in_=pb[4][32:33, :], axis=AX.X, op=ALU.max))
                ev = p.op("vector", lambda e: e.tensor_tensor(
                    out=km33[32:33, 0:1], in0=km33[32:33, 0:1], in1=km33[32:33, 1:2],
                    op=ALU.max), "EV")
                pbfree[4] = ev
            p.wait("scalar", "EV", pbfree[4])
            es_km = p.op("scalar", lambda e: e.activation(
                out=km33[32:33, 0:1], in_=km33[32:33, 0:1], func=AF.Sqrt), "ES")
            for kt in range(T // 128):
                pw = pb[kt % 2]
                p.wait("tensor", "ES", pbfree_es.get(kt % 2, 0))
                p.wait("tensor", "EV", pbfree[kt % 2])
                mm = p.op("tensor", lambda e, h=h, kt=kt, pw=pw: e.matmul(
                    out=pw[:, 0:128], lhsT=cslab[0:32, kt * 128:(kt + 1) * 128],
                    rhs=wuv[:, h * 128:(h + 1) * 128], start=True, stop=True), "MM")
                p.wait("vector", "MM", mm)
                p.wait("vector", "EV", ev_bvh)
                ev = p.op("vector", lambda e, kt=kt, pw=pw: e.tensor_add(
                    out=v_tm[:, kt * 128:(kt + 1) * 128], in0=pw[:, 0:128], in1=bvh),
                    "EV")
                pbfree[kt % 2] = ev
                pbfree_es[kt % 2] = 0
            p.wait("vector", "ES", es_km)
            for qb in range(T // NT):
                pw = pb[qb % 2]
                pw2 = pb[2 + qb % 2]
                p.wait("tensor", "ES", pbfree_es.get(qb % 2, 0))
                p.wait("tensor", "EV", pbfree[qb % 2])
                mm1 = p.op("tensor", lambda e, h=h, qb=qb, pw=pw: e.matmul(
                    out=pw[:], lhsT=wuq[:, h * 128:(h + 1) * 128],
                    rhs=cq[:, qb * NT:(qb + 1) * NT], start=True, stop=True), "MM")
                p.wait("scalar", "MM", mm1)
                es1 = p.op("scalar", lambda e, h=h, qb=qb, pw=pw: e.activation(
                    out=qh[:, qb * NT:(qb + 1) * NT], in_=pw[:], func=AF.Identity,
                    scale=SCALE, bias=buq[:, h:h + 1]), "ES")
                pbfree_es[qb % 2] = es1
                es2 = p.op("scalar", lambda e, qb=qb: e.activation(
                    out=sq[:, 0:NT], in_=qh[:, qb * NT:(qb + 1) * NT], func=AF.Square), "ES")
                p.wait("tensor", "ES", es2)
                p.wait("tensor", "EV", pbfree[4])
                p.op("tensor", lambda e: e.matmul(
                    out=pb[4][32:33, :], lhsT=ones_c[:], rhs=sq[:, 0:NT],
                    start=True, stop=False))
                p.wait("tensor", "ES", pbfree_es.get(2 + qb % 2, 0))
                p.wait("tensor", "EV", pbfree[2 + qb % 2])
                mm2 = p.op("tensor", lambda e, h=h, qb=qb, pw2=pw2: e.matmul(
                    out=pw2[0:32, :], lhsT=wqr[:, h * DR:(h + 1) * DR],
                    rhs=cq[:, qb * NT:(qb + 1) * NT], start=True, stop=True), "MM")
                p.wait("scalar", "MM", mm2)
                es3 = p.op("scalar", lambda e, h=h, qb=qb, pw2=pw2: e.activation(
                    out=qr2[0:32, qb * NT:(qb + 1) * NT], in_=pw2[0:32, :], func=AF.Identity,
                    scale=SCALE, bias=bqr[:, h:h + 1]), "ES")
                pbfree_es[2 + qb % 2] = es3
                es4 = p.op("scalar", lambda e, qb=qb: e.activation(
                    out=sq[0:32, 0:NT], in_=qr2[0:32, qb * NT:(qb + 1) * NT],
                    func=AF.Square), "ES")
                p.wait("tensor", "ES", es4)
                mm3 = p.op("tensor", lambda e: e.matmul(
                    out=pb[4][32:33, :], lhsT=ones_c[0:32, :], rhs=sq[0:32, 0:NT],
                    start=False, stop=True), "MM")
                p.wait("scalar", "MM", mm3)
                es5 = p.op("scalar", lambda e: e.activation(
                    out=rowb33[32:33, :], in_=pb[4][32:33, :], func=AF.Sqrt), "ES")
                p.wait("vector", "ES", es5)
                ev = p.op("vector", lambda e, qb=qb: e.tensor_scalar(
                    out=qr2[32:33, qb * NT:(qb + 1) * NT], in0=rowb33[32:33, :],
                    scalar1=km33[32:33, 0:1], scalar2=-1.0,
                    op0=ALU.mult, op1=ALU.mult), "EV")
                pbfree[4] = ev
            for qb in range(T // NT):
                i_av = 0
                p.wait("tensor", "EV", max(pbfree[2], pbfree[3]))
                es_exp = {}
                mm_av = {}
                for kt in range(T // 128):
                    pw = pb[kt % 2]
                    p.wait("tensor", "ES", pbfree_es.get(kt % 2, 0))
                    p.wait("tensor", "EV", pbfree[kt % 2])
                    if kt >= 2:
                        p.wait("tensor", "MM", mm_av[kt - 2])
                    p.op("tensor", lambda e, h=h, kt=kt, qb=qb, pw=pw: e.matmul(
                        out=pw[:], lhsT=kch[:, kt * 128:(kt + 1) * 128],
                        rhs=qh[:, qb * NT:(qb + 1) * NT], start=True, stop=False))
                    mm = p.op("tensor", lambda e, h=h, kt=kt, qb=qb, pw=pw: e.matmul(
                        out=pw[:], lhsT=kr2[:, kt * 128:(kt + 1) * 128],
                        rhs=qr2[:, qb * NT:(qb + 1) * NT], start=False, stop=True), "MM")
                    p.wait("scalar", "MM", mm)
                    ese = p.op("scalar", lambda e, kt=kt, pw=pw: e.activation(
                        out=akt[:, (kt % 2) * NT:(kt % 2 + 1) * NT], in_=pw[:],
                        func=AF.Exp), "ES")
                    es_exp[kt] = ese
                    pbfree_es[kt % 2] = ese
                    pbfree[kt % 2] = 0
                    p.wait("tensor", "ES", ese)
                    p.op("tensor", lambda e, kt=kt: e.matmul(
                        out=pb[2][:], lhsT=v_tm[:, kt * 128:(kt + 1) * 128],
                        rhs=akt[:, (kt % 2) * NT:(kt % 2 + 1) * NT],
                        start=(kt == 0), stop=(kt == T // 128 - 1)))
                    mm_av[kt] = p.op("tensor", lambda e, kt=kt: e.matmul(
                        out=pb[3][:], lhsT=ones128,
                        rhs=akt[:, (kt % 2) * NT:(kt % 2 + 1) * NT],
                        start=(kt == 0), stop=(kt == T // 128 - 1)), "MM")
                mm_done = mm_av[T // 128 - 1]
                p.wait("vector", "MM", mm_done)
                p.dwait("vector", "ST")
                p.op("vector", lambda e, qb=qb: e.reciprocal(
                    out=ybuf[:, 0:NT], in_=pb[3][:]))
                ev_o = p.op("vector", lambda e, qb=qb: e.tensor_mul(
                    out=ostg[:, (qb % 2) * NT:(qb % 2 + 1) * NT], in0=pb[2][:],
                    in1=ybuf[:, 0:NT]), "EV")
                pbfree[2] = ev_o
                pbfree[3] = ev_o
                p.wait("sync", "EV", ev_o)
                st_osc[(h, qb)] = p.dma(
                    "sync", osc[h, :, qb * NT:(qb + 1) * NT],
                    ostg[:, (qb % 2) * NT:(qb % 2 + 1) * NT], "ST")
        mm_attn_done = p.tally["MM"]

        ost_free = (mm_attn_done, 0)
        ev_g = {}
        for s in range(4):
            S = 1024
            p.wait("sync", "MM", ost_free[0])
            p.wait("sync", "EV", ost_free[1])
            p.dwait("sync", "ST")
            lo = p.dma("sync", ostripe.rearrange("p (j t) -> p j t", j=JD),
                       osc[:, :, s * S:(s + 1) * S].rearrange("j p t -> p j t"), "LX")
            lxr = p.dma("sync", xresC.rearrange("p (j t) -> p j t", j=JD),
                        x[:, s * S:(s + 1) * S].rearrange("(j p) t -> p j t", p=128), "LX")
            for o in range(JD):
                for tb in range(2):
                    i = o * 2 + tb
                    pw = pb[4 + i % 4]
                    if i == 0:
                        p.dwait("tensor", "LX")
                    if i >= 4:
                        p.wait("tensor", "EV", ev_g[(s, i - 4)])
                    elif s > 0:
                        p.wait("tensor", "EV", ev_g[(s - 1, 12 + i)])
                    else:
                        p.wait("tensor", "EV", pbfree[4 + i % 4])
                    mm = 0
                    for k in range(JD):
                        mm = p.op("tensor", lambda e, k=k, o=o, tb=tb, pw=pw, s=s: e.matmul(
                            out=pw[:], lhsT=wo[:, k * 1024 + o * 128: k * 1024 + (o + 1) * 128],
                            rhs=ostripe[:, k * S + tb * NT: k * S + (tb + 1) * NT],
                            start=(k == 0), stop=(k == JD - 1)),
                            "MM" if k == JD - 1 else None)
                    p.wait("vector", "MM", mm)
                    p.dwait("vector", "LX")
                    p.dwait("vector", "ST")
                    ev = p.op("vector", lambda e, o=o, tb=tb, pw=pw, i=i: e.scalar_tensor_tensor(
                        out=ybuf[:, (i % 2) * NT:(i % 2 + 1) * NT], in0=pw[:],
                        scalar=bo[:, o:o + 1],
                        in1=xresC[:, o * S + tb * NT: o * S + (tb + 1) * NT],
                        op0=ALU.add, op1=ALU.add), "EV")
                    ev_g[(s, i)] = ev
                    p.wait("sync", "EV", ev)
                    p.dma("sync", y[o * 128:(o + 1) * 128, s * S + tb * NT: s * S + (tb + 1) * NT],
                          ybuf[:, (i % 2) * NT:(i % 2 + 1) * NT], "ST")
            ost_free = (p.tally["MM"], p.tally["EV"])

        p.dwait("sync", "ST")


def build_l1attn():
    nc = bass.Bass("TRN2", target_bir_lowering=False, detect_race_conditions=RACE_DETECT)
    x = nc.dram_tensor("x", [D, T], F32, kind="ExternalInput").ap()
    y = nc.dram_tensor("y", [D, T], F32, kind="ExternalOutput").ap()
    with contextlib.ExitStack() as es:
        p = Prog(nc)
        p.make_sems(es, ALLSEMS)
        Cx = Ctx(nc, p, es)
        emit_l1attn(Cx, x, y, wp="", first=True)
        with nc.Block() as block:
            p.replay(block)
    return nc


def layout_l1attn(inp):
    f = np.float32
    g = lambda k: np.asarray(inp[k], f)
    wc96 = np.hstack([g("L1_attn_W_DKV"), g("L1_attn_W_KR"), g("L1_attn_W_DQ")])
    bc96 = np.concatenate([g("L1_attn_b_DKV"), g("L1_attn_b_KR"), g("L1_attn_b_DQ")])
    return dict(
        wc96=np.ascontiguousarray(wc96),
        wuq=np.ascontiguousarray(g("L1_attn_W_UQ")),
        wuk=np.ascontiguousarray(g("L1_attn_W_UK")),
        wuv=np.ascontiguousarray(g("L1_attn_W_UV")),
        wqr=np.ascontiguousarray(g("L1_attn_W_QR")),
        wo=np.ascontiguousarray(g("L1_attn_W_O")),
        bc96=np.ascontiguousarray(bc96.reshape(96, 1)),
        buq=np.ascontiguousarray(g("L1_attn_b_UQ").reshape(8, 128).T),
        buk=np.ascontiguousarray(g("L1_attn_b_UK").reshape(8, 128).T),
        buv=np.ascontiguousarray(g("L1_attn_b_UV").reshape(1, 1024)),
        bqr=np.ascontiguousarray(g("L1_attn_b_QR").reshape(8, 32).T),
        bo=_colmajor(g("L1_attn_b_O")),
    )


def emit_l1ffn(C, xin, xout, wp="f1", first=True):
    NE = 15
    F = 1024
    FS = 2048
    S = 1024
    NS = T // S
    TB = S // NT

    nc = C.nc
    p = C.p
    nm = (wp + "_") if wp else ""
    x, y = xin, xout
    gwd = C.dram(nm + "gw", [128, JD * NE])
    s1d = C.dram(nm + "s1", [D, 2 * FS])
    s2d = C.dram(nm + "s2", [FS, D])
    f1d = C.dram(nm + "f1", [NE, D, 2 * F])
    f2d = C.dram(nm + "f2", [NE, F, D])
    fb1d = C.dram(nm + "fb1", [128, NE * 16])
    fb2d = C.dram(nm + "fb2", [1, NE * D])
    gmd = C.dram(nm + "gm", [128, NE * 8])
    sb1d = C.dram(nm + "sb1", [128, 32])
    sgmd = C.dram(nm + "sgm", [128, 16])
    sb2d = C.dram(nm + "sb2", [1, D])
    idd = C.dram(nm + "idd", [128, 128])
    wcs1 = C.dram(nm + "wcs1", [32, 128, JD * 128], BF16, "Internal")
    wcs2 = C.dram(nm + "wcs2", [16, 128, D], BF16, "Internal")
    wcf1 = C.dram(nm + "wcf1", [NE * 16, 128, JD * 128], BF16, "Internal")
    wcf2 = C.dram(nm + "wcf2", [NE * 8, 128, D], BF16, "Internal")

    if True:
        xs = C.xs
        xln = C.xln
        arena_f = C.arena.bitcast(F32)
        hbt = C.hb
        hb_f = C.hb.bitcast(F32)
        yacc = arena_f[:, 0:8 * D]
        ybuf = arena_f[:, 8192:9216]
        vsq = arena_f[:, 9216:13312]
        ssb = arena_f[:, 13312:14336]
        sb2r = arena_f[0:1, 14336:15360]
        ashr = hbt[:, 0:16 * S]
        aexp = hbt[:, 16 * S:24 * S]
        fc2bA = hbt[0:1, 24 * S:32 * S]
        wbA = C.wbA
        wbB = C.wbB
        fc2bB = C.wbA[0:1, 2048:6144]
        fc2bC = C.wbB[0:1, 2048:5120]
        abuf = C.abuf
        wbA_f = C.wbA.bitcast(F32)
        identb = wbA_f[:, 3072:3200]
        gatew = wbA_f[:, 3200:3320]
        fb1 = wbA_f[:, 3328:3568]
        gmc = wbA_f[:, 3584:3704]
        comb = arena_f[:, 15360:15480]
        et = arena_f[:, 15480:15600]
        etd = arena_f[:, 15600:15720]
        lgs = arena_f[:, 15720:15840]
        scr = arena_f[:, 15840:15968]
        r1 = arena_f[:, 15968:15976]
        r2 = arena_f[:, 15976:15984]
        combb = C.arena[:, 31968:32088]
        ones_rb = C.sb(nm + "ones_rb", [1, 128], BF16)
        sb1 = C.sb(nm + "sb1c", [128, 32], F32)
        sgm = C.sb(nm + "sgmc", [128, 16], F32)
        ones_r = C.ones_r
        ones_c = C.ones_c
        rmean, rvar, rt2, rA = C.rmean, C.rvar, C.rt2, C.rA
        ceps = C.ceps
        pb = C.pb
        rows = (rmean, rvar, rt2, rA, rt2)

        C.init_common()
        wb_after = {}

        def fc2b_ap(e_i, dblk):
            c = e_i * D + dblk * NT
            if e_i < 8:
                return fc2bA[0:1, c:c + NT]
            if e_i < 12:
                return fc2bB[0:1, c - 8 * D:c - 8 * D + NT]
            return fc2bC[0:1, c - 12 * D:c - 12 * D + NT]

        p.wait("gpsimd", "MM", p.tally["MM"])
        p.dwait("gpsimd", "WBA")
        p.dwait("gpsimd", "WBB")
        p.dma("gpsimd", fc2bA, fb2d[0:1, 0:8 * D], "LWA")
        p.dma("gpsimd", fc2bB, fb2d[0:1, 8 * D:12 * D], "LWA")
        p.dma("gpsimd", fc2bC, fb2d[0:1, 12 * D:15 * D], "LWA")
        p.dma("sync", gatew, gwd, "LX")
        p.dma("sync", fb1, fb1d, "LX")
        p.dma("sync", gmc, gmd, "LX")
        for dst, srcd in ((sb1[:], sb1d), (sgm[:], sgmd), (sb2r, sb2d), (identb, idd)):
            p.dma("sync", dst, srcd, "LX")
        p.op("vector", lambda e: e.memset(ones_rb[:], 1.0))
        p.dwait("vector", "LX")
        ev_init = p.op("vector", lambda e: e.nop(), "EV")
        p.wait("tensor", "EV", ev_init)
        p.dwait("tensor", "LWA")
        mm_ssb = 0
        for hblk in range(2):
            mm_ssb = p.op("tensor", lambda e, hblk=hblk: e.matmul(
                out=pb[6 + hblk][:], lhsT=ones_r[:], rhs=sb2r[0:1, hblk * NT:(hblk + 1) * NT],
                start=True, stop=True), "MM")
            p.wait("vector", "MM", mm_ssb)
            p.op("vector", lambda e, hblk=hblk: e.tensor_copy(
                out=ssb[:, hblk * NT:(hblk + 1) * NT], in_=pb[6 + hblk][:]))
        ev_ssbcp = p.op("vector", lambda e: e.nop(), "EV")
        p.wait("tensor", "EV", ev_ssbcp)
        ev_ssb = p.op("vector", lambda e: e.nop(), "EV")

        def wchunk(tag, idx, o, src2d, wc, wcrow, ktiles, slot, free_mm, s, lsem, wsem):
            if free_mm:
                p.wait("gpsimd", "MM", free_mm)
            if s == 0 and first:
                if (tag, idx, o - 2) in wb_after or (tag, idx, o - 4) in wb_after:
                    p.dwait("gpsimd", wsem)
                if ktiles == 1:
                    view3 = slot[:, 0:src2d.shape[1]]
                    srcv = src2d[o * 128:(o + 1) * 128, :]
                else:
                    view3 = slot.rearrange("p (k o) -> p k o", k=ktiles)
                    srcv = src2d[:, o * 128:(o + 1) * 128].rearrange("(k p) o -> p k o", p=128)
                p.dma("gpsimd", view3, srcv, lsem)
                p.dwait("sync", lsem)
                wb_after[(tag, idx, o)] = p.dma("sync", wc[wcrow], slot, wsem)
            else:
                p.dwait("gpsimd", wsem)
                p.dma("gpsimd", slot, wc[wcrow], lsem)

        xlnf = hb_f[:, 8192:8192 + JD * NT]
        xlnf3 = xlnf.rearrange("p (j t) -> p j t", j=JD)
        xs3 = xs[:].rearrange("p (j t) -> p j t", j=JD)
        vsq3 = vsq.rearrange("p (j t) -> p j t", j=JD)
        xln3 = xln[:].rearrange("p (j t) -> p j t", j=JD)
        xs_free = (0, ev_ssb)
        xlnf_free_mm_local = [0]
        psAB_free = ev_ssb
        psCD_free = ev_ssb
        prev_sq = 0
        wbfree_mm = {0: 0, 1: 0}
        pbfree = {i: 0 for i in range(8)}
        pbfree_es = {i: 0 for i in range(8)}
        xln_free_mm = 0
        xlnf_free_mm = 0
        yaccfree_mm = 0
        st_y = 0
        for s in range(NS):
            ev_xln = {}
            for tb in range(TB):
                t0 = s * S + tb * NT
                p.wait("sync", "MM", xs_free[0])
                p.wait("sync", "EV", xs_free[1])
                lx_xs = p.dma("sync", xs3,
                              x[:, t0:t0 + NT].rearrange("(j p) t -> p j t", p=128), "LX")
                dst3 = xln3[:, :, tb * NT:(tb + 1) * NT]
                mm_ln, ev_ln = _norm_block(
                    p, pb, xs3, vsq3, dst3, ones_c, ones_r, rows, ceps, 0,
                    abuf[:, 0:NT], None, True, prev_sq, psAB_free, psCD_free,
                    waits_t=[("LX", lx_xs), ("EV", ev_init)],
                    waits_s=[("LX", lx_xs)],
                    waits_v=[("MM", max(xln_free_mm, xlnf_free_mm_local[0]))], dst2=xlnf3)
                prev_sq = mm_ln
                psAB_free = ev_ln
                psCD_free = ev_ln
                ev_xln[tb] = ev_ln
                pw = pb[2 + tb % 2]
                p.wait("tensor", "EV", max(pbfree[2 + tb % 2], ev_ln))
                p.wait("tensor", "ES", pbfree_es[2 + tb % 2])
                pbfree_es[2 + tb % 2] = 0
                mm = 0
                for t4 in range(4):
                    for k in range(JD):
                        mm = p.op("tensor", lambda e, t4=t4, k=k, pw=pw: e.matmul(
                            out=pw[:, t4 * 16: t4 * 16 + NE],
                            lhsT=xlnf[:, k * NT + t4 * 128: k * NT + (t4 + 1) * 128],
                            rhs=gatew[:, k * NE:(k + 1) * NE],
                            start=(k == 0), stop=(k == JD - 1)),
                            "MM" if k == JD - 1 else None)
                p.wait("vector", "MM", mm)
                ev = 0
                for t4 in range(4):
                    tt = tb * 4 + t4
                    ev = p.op("vector", lambda e, t4=t4, tt=tt, pw=pw: e.tensor_copy(
                        out=lgs[:, tt * NE:(tt + 1) * NE],
                        in_=pw[:, t4 * 16: t4 * 16 + NE]), "EV")
                pbfree[2 + tb % 2] = ev
                xlnf_free_mm_local[0] = mm
                xs_free = (mm_ln, ev_ln)
            ev_xln_all = ev_xln[TB - 1]
            xlnf_free_mm = xlnf_free_mm_local[0]

            for tt in range(8):
                p.op("vector", lambda e, tt=tt: e.tensor_reduce(
                    out=r1[:, tt:tt + 1], in_=lgs[:, tt * NE:(tt + 1) * NE], axis=AX.X,
                    op=ALU.max))
            _evv = p.op("vector", lambda e: e.nop(), "EV")
            p.wait("vector", "EV", _evv)
            ev_d = 0
            for tt in range(8):
                ev_d = p.op("vector", lambda e, tt=tt: e.tensor_single_scalar(
                    out=etd[:, tt * NE:(tt + 1) * NE], in_=lgs[:, tt * NE:(tt + 1) * NE],
                    scalar=r1[:, tt:tt + 1], op=ALU.subtract), "EV")
            p.wait("scalar", "EV", ev_d)
            es_exp_all = 0
            for tt in range(8):
                es_exp_all = p.op("scalar", lambda e, tt=tt: e.activation(
                    out=et[:, tt * NE:(tt + 1) * NE], in_=etd[:, tt * NE:(tt + 1) * NE],
                    func=AF.Exp), "ES")
            def vedge():
                evv = p.op("vector", lambda e: e.nop(), "EV")
                p.wait("vector", "EV", evv)

            p.wait("vector", "ES", es_exp_all)
            for rnd in range(3):
                for tt in range(8):
                    src = et[:, tt * NE:(tt + 1) * NE] if rnd == 0 else comb[:, tt * NE:(tt + 1) * NE]
                    dstr = (r1 if rnd < 2 else r2)
                    p.op("vector", lambda e, tt=tt, src=src, dstr=dstr: e.tensor_reduce(
                        out=dstr[:, tt:tt + 1], in_=src, axis=AX.X, op=ALU.max))
                if rnd == 2:
                    break
                vedge()
                for tt in range(8):
                    src = et[:, tt * NE:(tt + 1) * NE] if rnd == 0 else comb[:, tt * NE:(tt + 1) * NE]
                    p.op("vector", lambda e, tt=tt, src=src: e.tensor_single_scalar(
                        out=scr[:, tt * 16:tt * 16 + NE], in_=src,
                        scalar=r1[:, tt:tt + 1], op=ALU.subtract))
                for tt in range(8):
                    p.op("vector", lambda e, tt=tt: e.tensor_single_scalar(
                        out=scr[:, tt * 16:tt * 16 + NE], in_=scr[:, tt * 16:tt * 16 + NE],
                        scalar=0.0, op=ALU.is_ge))
                for tt in range(8):
                    src = et[:, tt * NE:(tt + 1) * NE] if rnd == 0 else comb[:, tt * NE:(tt + 1) * NE]
                    p.op("vector", lambda e, tt=tt, src=src: e.scalar_tensor_tensor(
                        out=comb[:, tt * NE:(tt + 1) * NE], in0=scr[:, tt * 16:tt * 16 + NE],
                        scalar=-1e30, in1=src, op0=ALU.mult, op1=ALU.add))
            vedge()
            ev_comb = {}
            for tt in range(8):
                p.op("vector", lambda e, tt=tt: e.tensor_single_scalar(
                    out=scr[:, tt * 16:tt * 16 + NE], in_=et[:, tt * NE:(tt + 1) * NE],
                    scalar=r2[:, tt:tt + 1], op=ALU.subtract))
            for tt in range(8):
                p.op("vector", lambda e, tt=tt: e.tensor_single_scalar(
                    out=scr[:, tt * 16:tt * 16 + NE], in_=scr[:, tt * 16:tt * 16 + NE],
                    scalar=0.0, op=ALU.is_ge))
            for tt in range(8):
                p.op("vector", lambda e, tt=tt: e.tensor_mul(
                    out=comb[:, tt * NE:(tt + 1) * NE], in0=et[:, tt * NE:(tt + 1) * NE],
                    in1=scr[:, tt * 16:tt * 16 + NE]))
            for tt in range(8):
                p.op("vector", lambda e, tt=tt: e.tensor_reduce(
                    out=r1[:, tt:tt + 1], in_=comb[:, tt * NE:(tt + 1) * NE], axis=AX.X,
                    op=ALU.add))
            vedge()
            for tt in range(8):
                p.op("vector", lambda e, tt=tt: e.reciprocal(
                    out=r2[:, tt:tt + 1], in_=r1[:, tt:tt + 1]))
            vedge()
            for tt in range(8):
                p.op("vector", lambda e, tt=tt: e.tensor_single_scalar(
                    out=comb[:, tt * NE:(tt + 1) * NE], in_=comb[:, tt * NE:(tt + 1) * NE],
                    scalar=r2[:, tt:tt + 1], op=ALU.mult))
                ev_comb[tt] = p.op("vector", lambda e, tt=tt: e.tensor_copy(
                    out=combb[:, tt * NE:(tt + 1) * NE],
                    in_=comb[:, tt * NE:(tt + 1) * NE]), "EV")
            ev_comb_all = ev_comb[7]

            p.wait("vector", "MM", yaccfree_mm)
            for tt in range(8):
                for dblk in range(2):
                    p.op("vector", lambda e, tt=tt, dblk=dblk: e.tensor_copy(
                        out=yacc[:, tt * D + dblk * NT: tt * D + (dblk + 1) * NT],
                        in_=ssb[:, dblk * NT:(dblk + 1) * NT]))
            ev_yinit = p.op("vector", lambda e: e.nop(), "EV")

            g1done = {}
            es_after = {}
            ev_after = {}
            for o in range(16):
                slotA = wbA[:, (o % 2) * 1024:(o % 2 + 1) * 1024]
                slotB = wbB[:, (o % 2) * 1024:(o % 2 + 1) * 1024]
                wchunk("s1", 0, o, s1d, wcs1, o, JD, slotA,
                       g1done.get(o - 2, wbfree_mm[0]), s, "LWA", "WBA")
                wchunk("s1", 0, 16 + o, s1d, wcs1, 16 + o, JD, slotB, 0, s, "LWB", "WBB")
                p.dwait("tensor", "LWA")
                p.dwait("tensor", "LWB")
                if o == 0:
                    p.wait("tensor", "EV", ev_xln_all)
                for tb in range(TB):
                    i = o * TB + tb
                    pw1, pw2 = pb[2 + i % 2], pb[4 + i % 2]
                    if i >= 2:
                        p.wait("tensor", "ES", es_after[i - 2])
                        p.wait("tensor", "EV", ev_after[i - 2])
                    else:
                        p.wait("tensor", "EV", max(pbfree[2 + i % 2], pbfree[4 + i % 2]))
                    for k in range(JD):
                        p.op("tensor", lambda e, k=k, tb=tb, slotA=slotA, pw1=pw1: e.matmul(
                            out=pw1[:], lhsT=slotA[:, k * 128:(k + 1) * 128],
                            rhs=xln[:, k * S + tb * NT: k * S + (tb + 1) * NT],
                            start=(k == 0), stop=(k == JD - 1)))
                    mm = 0
                    for k in range(JD):
                        mm = p.op("tensor", lambda e, k=k, tb=tb, slotB=slotB, pw2=pw2:
                                  e.matmul(
                            out=pw2[:], lhsT=slotB[:, k * 128:(k + 1) * 128],
                            rhs=xln[:, k * S + tb * NT: k * S + (tb + 1) * NT],
                            start=(k == 0), stop=(k == JD - 1)),
                            "MM" if k == JD - 1 else None)
                    p.wait("scalar", "MM", mm)
                    es_g = p.op("scalar", lambda e, i=i, o=o, pw2=pw2: e.activation(
                        out=abuf[:, (i % 2) * NT:(i % 2 + 1) * NT], in_=pw2[:],
                        func=AF.Gelu, bias=sb1[:, 16 + o:17 + o]), "ES")
                    es_after[i] = es_g
                    p.wait("vector", "MM", mm)
                    p.wait("vector", "ES", es_g)
                    p.op("vector", lambda e, i=i, o=o, pw1=pw1: e.scalar_tensor_tensor(
                        out=abuf[:, (i % 2) * NT:(i % 2 + 1) * NT], in0=pw1[:],
                        scalar=sb1[:, o:o + 1], in1=abuf[:, (i % 2) * NT:(i % 2 + 1) * NT],
                        op0=ALU.add, op1=ALU.mult))
                    ev = p.op("vector", lambda e, i=i, o=o, tb=tb: e.tensor_single_scalar(
                        out=ashr[:, o * S + tb * NT: o * S + (tb + 1) * NT],
                        in_=abuf[:, (i % 2) * NT:(i % 2 + 1) * NT],
                        scalar=sgm[:, o:o + 1], op=ALU.mult), "EV")
                    ev_after[i] = ev
                g1done[o] = mm
            wbfree_mm[0] = p.tally["MM"]
            wbfree_mm[1] = p.tally["MM"]
            ev_ashr_all = p.tally["EV"]
            pbfree[2] = ev_after[16 * TB - 2]
            pbfree[3] = ev_after[16 * TB - 1]
            pbfree_es[2] = es_after[16 * TB - 2]
            pbfree_es[3] = es_after[16 * TB - 1]
            f2done = {}
            for ft in range(16):
                slot = (wbA if ft % 2 == 0 else wbB)[:, 0:1024]
                lsem, wsem = ("LWA", "WBA") if ft % 2 == 0 else ("LWB", "WBB")
                wchunk("s2", 0, ft, s2d, wcs2, ft, 1, slot,
                       f2done.get(ft - 2, wbfree_mm[ft % 2]), s, lsem, wsem)
                p.dwait("tensor", lsem)
                if ft == 0:
                    p.wait("tensor", "EV", ev_ashr_all)
                mm = 0
                for tt in range(8):
                    for dblk in range(2):
                        i = tt * 2 + dblk
                        pw = pb[6 + (i + ft) % 2]
                        p.wait("tensor", "EV", pbfree[6 + (i + ft) % 2])
                        mm = p.op("tensor", lambda e, ft=ft, tt=tt, dblk=dblk, pw=pw,
                                  slot=slot: e.matmul(
                            out=pw[:], lhsT=ashr[:, ft * S + tt * 128: ft * S + (tt + 1) * 128],
                            rhs=slot[:, dblk * NT:(dblk + 1) * NT],
                            start=True, stop=True), "MM")
                        p.wait("vector", "MM", mm)
                        ev = p.op("vector", lambda e, tt=tt, dblk=dblk, pw=pw: e.tensor_add(
                            out=yacc[:, tt * D + dblk * NT: tt * D + (dblk + 1) * NT],
                            in0=yacc[:, tt * D + dblk * NT: tt * D + (dblk + 1) * NT],
                            in1=pw[:]), "EV")
                        pbfree[6 + (i + ft) % 2] = ev
                f2done[ft] = mm
            wbfree_mm[0] = p.tally["MM"]
            wbfree_mm[1] = p.tally["MM"]

            for e_i in range(NE):
                e1done = {}
                for o in range(8):
                    slotA = wbA[:, (o % 2) * 1024:(o % 2 + 1) * 1024]
                    slotB = wbB[:, (o % 2) * 1024:(o % 2 + 1) * 1024]
                    f1e = f1d[e_i]
                    wchunk("f1", e_i, o, f1e, wcf1, e_i * 16 + o, JD, slotA,
                           e1done.get(o - 2, wbfree_mm[0]), s, "LWA", "WBA")
                    wchunk("f1", e_i, 8 + o, f1e, wcf1, e_i * 16 + 8 + o, JD, slotB,
                           0, s, "LWB", "WBB")
                    p.dwait("tensor", "LWA")
                    p.dwait("tensor", "LWB")
                    for tb in range(TB):
                        i = o * TB + tb
                        pw1, pw2 = pb[2 + i % 2], pb[4 + i % 2]
                        if i >= 2:
                            p.wait("tensor", "ES", es_after[i - 2])
                            p.wait("tensor", "EV", ev_after[i - 2])
                        else:
                            p.wait("tensor", "EV", max(pbfree[2 + i % 2], pbfree[4 + i % 2]))
                            p.wait("tensor", "ES", max(pbfree_es[2 + i % 2],
                                                       pbfree_es.get(4 + i % 2, 0)))
                        for k in range(JD):
                            p.op("tensor", lambda e, k=k, tb=tb, slotA=slotA, pw1=pw1:
                                 e.matmul(
                                out=pw1[:], lhsT=slotA[:, k * 128:(k + 1) * 128],
                                rhs=xln[:, k * S + tb * NT: k * S + (tb + 1) * NT],
                                start=(k == 0), stop=(k == JD - 1)))
                        mm = 0
                        for k in range(JD):
                            mm = p.op("tensor", lambda e, k=k, tb=tb, slotB=slotB, pw2=pw2:
                                      e.matmul(
                                out=pw2[:], lhsT=slotB[:, k * 128:(k + 1) * 128],
                                rhs=xln[:, k * S + tb * NT: k * S + (tb + 1) * NT],
                                start=(k == 0), stop=(k == JD - 1)),
                                "MM" if k == JD - 1 else None)
                        p.wait("scalar", "MM", mm)
                        es_g = p.op("scalar", lambda e, i=i, o=o, e_i=e_i, pw2=pw2:
                                    e.activation(
                            out=abuf[:, (i % 2) * NT:(i % 2 + 1) * NT], in_=pw2[:],
                            func=AF.Gelu, bias=fb1[:, e_i * 16 + 8 + o:e_i * 16 + 9 + o]),
                            "ES")
                        es_after[i] = es_g
                        p.wait("vector", "MM", mm)
                        p.wait("vector", "ES", es_g)
                        p.op("vector", lambda e, i=i, o=o, e_i=e_i, pw1=pw1:
                             e.scalar_tensor_tensor(
                            out=abuf[:, (i % 2) * NT:(i % 2 + 1) * NT], in0=pw1[:],
                            scalar=fb1[:, e_i * 16 + o:e_i * 16 + o + 1],
                            in1=abuf[:, (i % 2) * NT:(i % 2 + 1) * NT],
                            op0=ALU.add, op1=ALU.mult))
                        ev = p.op("vector", lambda e, i=i, o=o, tb=tb, e_i=e_i:
                                  e.tensor_single_scalar(
                            out=aexp[:, o * S + tb * NT: o * S + (tb + 1) * NT],
                            in_=abuf[:, (i % 2) * NT:(i % 2 + 1) * NT],
                            scalar=gmc[:, e_i * 8 + o:e_i * 8 + o + 1], op=ALU.mult), "EV")
                        ev_after[i] = ev
                    e1done[o] = mm
                wbfree_mm[0] = p.tally["MM"]
                wbfree_mm[1] = p.tally["MM"]
                ev_aexp_all = p.tally["EV"]
                pbfree[2] = ev_after[8 * TB - 2]
                pbfree[3] = ev_after[8 * TB - 1]
                pbfree_es[2] = es_after[8 * TB - 2]
                pbfree_es[3] = es_after[8 * TB - 1]
                e2done = {}
                for ft in range(8):
                    slot = (wbA if ft % 2 == 0 else wbB)[:, 0:1024]
                    lsem, wsem = ("LWA", "WBA") if ft % 2 == 0 else ("LWB", "WBB")
                    wchunk("f2", e_i, ft, f2d[e_i], wcf2, e_i * 8 + ft, 1, slot,
                           e2done.get(ft - 2, wbfree_mm[ft % 2]), s, lsem, wsem)
                    p.dwait("tensor", lsem)
                    if ft == 0:
                        p.wait("tensor", "EV", ev_aexp_all)
                    mm = 0
                    for tt in range(8):
                        for dblk in range(2):
                            i = tt * 2 + dblk
                            pw = pb[6 + (i + ft) % 2]
                            p.wait("tensor", "EV", pbfree[6 + (i + ft) % 2])
                            if ft < 7:
                                mm = p.op("tensor", lambda e, ft=ft, tt=tt, dblk=dblk, pw=pw,
                                          slot=slot: e.matmul(
                                    out=pw[:],
                                    lhsT=aexp[:, ft * S + tt * 128: ft * S + (tt + 1) * 128],
                                    rhs=slot[:, dblk * NT:(dblk + 1) * NT],
                                    start=True, stop=True), "MM")
                            else:
                                p.op("tensor", lambda e, ft=ft, tt=tt, dblk=dblk, pw=pw,
                                     slot=slot: e.matmul(
                                    out=pw[:],
                                    lhsT=aexp[:, ft * S + tt * 128: ft * S + (tt + 1) * 128],
                                    rhs=slot[:, dblk * NT:(dblk + 1) * NT],
                                    start=True, stop=False))
                                mm = p.op("tensor", lambda e, e_i=e_i, dblk=dblk, pw=pw:
                                          e.matmul(
                                    out=pw[:], lhsT=ones_rb[:],
                                    rhs=fc2b_ap(e_i, dblk),
                                    start=False, stop=True), "MM")
                            p.wait("vector", "MM", mm)
                            ev = p.op("vector", lambda e, tt=tt, dblk=dblk, pw=pw, e_i=e_i:
                                      e.scalar_tensor_tensor(
                                out=yacc[:, tt * D + dblk * NT: tt * D + (dblk + 1) * NT],
                                in0=pw[:], scalar=comb[:, tt * NE + e_i:tt * NE + e_i + 1],
                                in1=yacc[:, tt * D + dblk * NT: tt * D + (dblk + 1) * NT],
                                op0=ALU.mult, op1=ALU.add), "EV")
                            pbfree[6 + (i + ft) % 2] = ev
                    e2done[ft] = mm
                wbfree_mm[0] = p.tally["MM"]
                wbfree_mm[1] = p.tally["MM"]
            ev_yacc_done = p.tally["EV"]
            xln_free_mm = p.tally["MM"]

            xresf = hb_f[:, 0:JD * S]
            p.wait("sync", "MM", wbfree_mm[0])
            p.dwait("sync", "ST")
            lx_res = p.dma("sync", xresf.rearrange("p (j t) -> p j t", j=JD),
                           x[:, s * S:(s + 1) * S].rearrange("(j p) t -> p j t", p=128),
                           "LX")
            for tt in range(8):
                mm_tr = {}
                for j in range(JD):
                    i = tt * JD + j
                    pw = pb[2 + i % 2]
                    p.wait("tensor", "EV", max(pbfree[2 + i % 2], ev_yacc_done))
                    p.wait("tensor", "ES", pbfree_es[2 + i % 2])
                    pbfree_es[2 + i % 2] = 0
                    mm = p.op("tensor", lambda e, tt=tt, j=j, pw=pw: e.transpose(
                        out=pw[:, 0:128],
                        in_=yacc[:, tt * D + j * 128: tt * D + (j + 1) * 128],
                        identity=identb), "MM")
                    p.wait("vector", "MM", mm)
                    if j == 0:
                        p.dwait("vector", "ST")
                        if i == 0:
                            p.dwait("vector", "LX")
                    ev = p.op("vector", lambda e, tt=tt, j=j, pw=pw: e.scalar_tensor_tensor(
                        out=ybuf[:, j * 128:(j + 1) * 128], in0=pw[:, 0:128], scalar=0.5,
                        in1=xresf[:, j * S + tt * 128: j * S + (tt + 1) * 128],
                        op0=ALU.mult, op1=ALU.add), "EV")
                    pbfree[2 + i % 2] = ev
                p.wait("sync", "EV", pbfree[2])
                p.wait("sync", "EV", pbfree[3])
                p.dma("sync", y[:, s * S + tt * 128: s * S + (tt + 1) * 128].rearrange(
                    "(j p) t -> p j t", p=128),
                    ybuf[:, 0:JD * 128].rearrange("p (j t) -> p j t", j=JD), "ST")
            yaccfree_mm = p.tally["MM"]
        p.dwait("sync", "ST")


def build_l1ffn():
    nc = bass.Bass("TRN2", target_bir_lowering=False, detect_race_conditions=RACE_DETECT)
    x = nc.dram_tensor("x", [D, T], F32, kind="ExternalInput").ap()
    y = nc.dram_tensor("y", [D, T], F32, kind="ExternalOutput").ap()
    with contextlib.ExitStack() as es:
        p = Prog(nc)
        p.make_sems(es, ALLSEMS)
        Cx = Ctx(nc, p, es)
        emit_l1ffn(Cx, x, y, wp="", first=True)
        with nc.Block() as block:
            p.replay(block)
    return nc


def layout_l1ffn(inp):
    f = np.float32
    g = lambda k: np.asarray(inp[k], f)
    fc1 = g("L1_ffn_fc1_w")
    fc2 = g("L1_ffn_fc2_w")
    return dict(
        gw=np.ascontiguousarray(g("L1_ffn_gate_w").T.reshape(8, 128, 15)
                                .transpose(1, 0, 2).reshape(128, 120)),
        s1=np.ascontiguousarray(g("L1_ffn_s_fc1_w").T),
        s2=np.ascontiguousarray(g("L1_ffn_s_fc2_w").T),
        f1=np.ascontiguousarray(fc1.transpose(0, 2, 1)),
        f2=np.ascontiguousarray(fc2.transpose(0, 2, 1)),
        fb1=np.ascontiguousarray(g("L1_ffn_fc1_b").reshape(15, 16, 128)
                                 .transpose(2, 0, 1).reshape(128, 240)),
        fb2=np.ascontiguousarray(g("L1_ffn_fc2_b").reshape(1, -1)),
        gm=np.ascontiguousarray(g("L1_ffn_geglu_mult").reshape(15, 8, 128)
                                .transpose(2, 0, 1).reshape(128, 120)),
        sb1=_colmajor(g("L1_ffn_s_fc1_b")),
        sgm=_colmajor(g("L1_ffn_s_geglu_mult")),
        sb2=np.ascontiguousarray(g("L1_ffn_s_fc2_b").reshape(1, 1024)),
        idd=np.eye(128, dtype=f),
    )


_PROGS = {}


def _prog(name):
    if name not in _PROGS:
        _PROGS[name] = {"l0ffn": build_l0ffn, "l0attn": build_l0attn,
                        "l1ffn": build_l1ffn, "l1attn": build_l1attn,
                        "fused": build_fused}[name]()
    return _PROGS[name]


def build_fused():
    nc = bass.Bass("TRN2", target_bir_lowering=False, detect_race_conditions=RACE_DETECT)
    x = nc.dram_tensor("x", [D, T], F32, kind="ExternalInput").ap()
    y = nc.dram_tensor("y", [D, T], F32, kind="ExternalOutput").ap()
    b0 = nc.dram_tensor("b0", [D, T], F32, kind="Internal").ap()
    b1 = nc.dram_tensor("b1", [D, T], F32, kind="Internal").ap()
    with contextlib.ExitStack() as es:
        p = Prog(nc)
        p.make_sems(es, ALLSEMS)
        Cx = Ctx(nc, p, es)
        emit_l0ffn(Cx, x, b0, wp="f0", first=True)
        Cx.barrier()
        emit_l0attn(Cx, b0, b1, wp="a0", first=True)
        Cx.barrier()
        emit_l0ffn(Cx, b1, b0, wp="f0", first=False)
        Cx.barrier()
        emit_l1ffn(Cx, b0, b1, wp="f1", first=True)
        Cx.barrier()
        emit_l1attn(Cx, b1, b0, wp="a1", first=True)
        Cx.barrier()
        emit_l1ffn(Cx, b0, y, wp="f1", first=False)
        with nc.Block() as block:
            p.replay(block)
    return nc


def kernel(**inputs):
    x = np.asarray(inputs["L0_attn_x"], np.float32)
    w = {}
    for pref, lay in (("f0", layout_l0ffn), ("a0", layout_l0attn),
                      ("f1", layout_l1ffn), ("a1", layout_l1attn)):
        for k, v in lay(inputs).items():
            w[f"{pref}_{k}"] = v
    nc = _prog("fused")
    in_maps = [dict(w, x=np.ascontiguousarray(x[b].T)) for b in range(NCORE)]
    res = run_bass_kernel_spmd(nc, in_maps, core_ids=list(range(NCORE)))
    return np.stack([np.asarray(res.results[b]["y"]).T for b in range(NCORE)], axis=0)



# version 03
# speedup vs baseline: 1.1968x; 1.1968x over previous
import contextlib
import numpy as np

import concourse.bass as bass
import concourse.mybir as mybir
from concourse.bass_utils import run_bass_kernel_spmd
from concourse.alu_op_type import AluOpType as ALU

F32 = mybir.dt.float32
BF16 = mybir.dt.bfloat16
AF = mybir.ActivationFunctionType
AX = mybir.AxisListType

ENG = ("sync", "gpsimd", "vector", "scalar", "tensor")
RACE_DETECT = True

T = 4096
D = 1024
JD = D // 128
NT = 512
NCORE = 8


class Prog:

    def __init__(self, nc):
        self.nc = nc
        self.ops = {e: [] for e in ENG}
        self.sems = {}
        self.tally = {}

    def make_sems(self, es, names):
        for n in names:
            self.sems[n] = es.enter_context(self.nc.semaphore(n))
            self.tally[n] = 0

    DMA_SEMS = ("LWA", "LWB", "LX", "WBA", "WBB", "ST", "DB", "DG")

    def wait(self, eng, sem, val):
        assert val <= self.tally[sem], ("wait ahead of tally", eng, sem, val, self.tally[sem])
        if sem in self.DMA_SEMS:
            assert val == self.tally[sem] or val == 0, ("partial DMA wait", eng, sem, val,
                                                        self.tally[sem])
        if val > 0:
            s = self.sems[sem]
            self.ops[eng].append(lambda e, s=s, v=val: e.wait_ge(s, v))

    def dwait(self, eng, sem):
        self.wait(eng, sem, self.tally[sem])

    def op(self, eng, fn, inc=None, n=1):
        if inc is None:
            self.ops[eng].append(fn)
            return None
        self.tally[inc] += n
        s = self.sems[inc]

        def run(e, fn=fn, s=s, n=n):
            fn(e).then_inc(s, n)

        self.ops[eng].append(run)
        return self.tally[inc]

    def dma(self, eng, out, in_, inc):
        return self.op(eng, lambda e, o=out, i=in_: e.dma_start(out=o, in_=i), inc, 16)

    def replay(self, block):
        for eng in ENG:
            dec = getattr(block, eng)
            ops = self.ops[eng]

            def body(e, ops=ops):
                for f in ops:
                    f(e)

            body.__name__ = f"_{eng}"
            dec(body)


class Ctx:

    def __init__(self, nc, p, es):
        self.nc = nc
        self.p = p
        self.es = es
        self._dram = {}
        self._sbuf = {}
        self._state = {}
        sb = lambda name, shape, dt: es.enter_context(nc.sbuf_tensor(name, shape, dt))
        self.xs = sb("xs", [128, JD * NT], F32)
        self.xln = sb("xln", [128, JD * 1024], BF16)
        self.hb = sb("hb", [128, 4 * JD * 1024], BF16)
        self.arena = sb("arena", [128, 4 * JD * 1024], BF16)
        self.wbA = sb("wbA", [128, JD * 1024], BF16)
        self.wbB = sb("wbB", [128, JD * 1024], BF16)
        self.abuf = sb("abuf", [128, 2 * NT], F32)
        self.ones_r = sb("ones_r", [1, 128], F32)
        self.ones_c = sb("ones_c", [128, 1], F32)
        self.rmean = sb("rmean", [1, NT], F32)
        self.rvar = sb("rvar", [1, NT], F32)
        self.rt2 = sb("rt2", [1, NT], F32)
        self.rA = sb("rA", [1, NT], F32)
        self.rB = self.rt2
        self.ceps = sb("ceps", [1, 2], F32)
        self.pb = [es.enter_context(nc.psum_tensor(f"pb{i}", [128, NT], F32))
                   for i in range(8)]
        self.inited = False

    def dram(self, name, shape, dt=F32, kind="ExternalInput"):
        if name not in self._dram:
            self._dram[name] = self.nc.dram_tensor(name, shape, dt, kind=kind).ap()
        return self._dram[name]

    def sb(self, name, shape, dt):
        if name not in self._sbuf:
            self._sbuf[name] = self.es.enter_context(self.nc.sbuf_tensor(name, shape, dt))
        return self._sbuf[name]

    def init_common(self):
        if self.inited:
            return self._state["ev_common"]
        p = self.p
        p.op("vector", lambda e: e.memset(self.ones_r[:], 1.0))
        p.op("vector", lambda e: e.memset(self.ones_c[:], 1.0))
        p.op("vector", lambda e: e.memset(self.ceps[0:1, 0:1], 1e-5))
        p.op("vector", lambda e: e.memset(self.ceps[0:1, 1:2], 1e-6))
        ev = p.op("vector", lambda e: e.nop(), "EV")
        self._state["ev_common"] = ev
        self.inited = True
        return ev

    def barrier(self):
        p = self.p
        mm, ev, es_ = p.tally["MM"], p.tally["EV"], p.tally["ES"]
        for eng in ENG:
            if eng != "tensor":
                p.wait(eng, "MM", mm)
            if eng != "vector":
                p.wait(eng, "EV", ev)
            if eng != "scalar":
                p.wait(eng, "ES", es_)
            for dsem in ("ST", "LX", "LWA", "LWB", "WBA", "WBB", "DB"):
                p.dwait(eng, dsem)


def emit_l0ffn(C, xin, xout, wp="f0", first=True):
    HID = 4096
    JH = HID // 128
    S = 1024
    NS = T // S
    TB = S // NT

    nc = C.nc
    p = C.p
    nm = (wp + "_") if wp else ""
    x, y = xin, xout
    w1t = C.dram(nm + "w1t", [D, HID])
    w2t = C.dram(nm + "w2t", [D, HID])
    h1t = C.dram(nm + "h1t", [HID, HID])
    h2t = C.dram(nm + "h2t", [HID, HID])
    w3t = C.dram(nm + "w3t", [HID, D])
    s1v = C.dram(nm + "s1v", [128, JH])
    s2v = C.dram(nm + "s2v", [128, JH])
    h1sv = C.dram(nm + "h1sv", [128, JH])
    h2sv = C.dram(nm + "h2sv", [128, JH])
    s3v = C.dram(nm + "s3v", [128, JD])
    wc1 = C.dram(nm + "wc1", [JH, 128, JD * 128], BF16, "Internal")
    wc2 = C.dram(nm + "wc2", [JH, 128, JD * 128], BF16, "Internal")
    wc3 = C.dram(nm + "wc3", [JH, 128, JH * 128], BF16, "Internal")
    wc4 = C.dram(nm + "wc4", [JH, 128, JH * 128], BF16, "Internal")
    wc5 = C.dram(nm + "wc5", [JD, 128, JH * 128], BF16, "Internal")

    if True:
        xs = C.xs
        xln = C.xln
        hb = C.hb
        arena = C.arena
        wbA = C.wbA
        wbB = C.wbB
        abuf = C.abuf
        s1c = C.sb(nm + "s1c", [128, JH], F32)
        s2c = C.sb(nm + "s2c", [128, JH], F32)
        h1sc = C.sb(nm + "h1sc", [128, JH], F32)
        h2sc = C.sb(nm + "h2sc", [128, JH], F32)
        s3h = C.sb(nm + "s3h", [128, JD], F32)
        ones_r = C.ones_r
        ones_c = C.ones_c
        rmean, rvar, rt2, rA, rB = C.rmean, C.rvar, C.rt2, C.rA, C.rB
        ceps = C.ceps
        pb = C.pb

        arena_f = arena.bitcast(F32)
        gbuf = arena
        xsq = arena_f[:, 10240:10240 + JD * NT]
        xres = arena_f[:, 0:JD * S]
        ybuf = [arena_f[:, 8192 + i * NT:8192 + (i + 1) * NT] for i in range(2)]

        ev_common = C.init_common()
        state = C._state.setdefault(wp or "l0ffn", {})
        if first:
            for dst, srcd in ((s1c, s1v), (s2c, s2v), (h1sc, h1sv), (h2sc, h2sv),
                              (s3h, s3v)):
                p.dma("sync", dst[:], srcd, "LX")
            p.dwait("vector", "LX")
            p.op("vector", lambda e: e.tensor_scalar_mul(out=s3h[:], in0=s3h[:],
                                                         scalar1=0.5))
            p.op("vector", lambda e: e.drain())
            state["ev_init"] = p.op("vector", lambda e: e.nop(), "EV")
            state["wb_after"] = {}
        ev_init = max(state["ev_init"], ev_common)
        wb_after = state["wb_after"]

        xs_free = (0, ev_init)
        xln_free_mm = 0
        wb_free_mm = 0
        hb_free_mm = 0
        arena_free = (0, 0)
        psAB_free_ev = 0
        psCD_free_ev = 0
        prev_mm_stats = 0

        def wchunk_load(s, tag, o, wsrc, wc, ktiles, slot, free_mm, lsem, wsem):
            if free_mm:
                p.wait("gpsimd", "MM", free_mm)
            if s == 0 and first:
                if (tag, o - 2) in wb_after:
                    p.dwait("gpsimd", wsem)
                view3 = slot.rearrange("p (k o) -> p k o", k=ktiles)
                src = wsrc[:, o * 128:(o + 1) * 128].rearrange("(k p) o -> p k o", p=128)
                p.dma("gpsimd", view3, src, lsem)
                p.dwait("sync", lsem)
                wb_after[(tag, o)] = p.dma("sync", wc[o], slot, wsem)
            else:
                p.dwait("gpsimd", wsem)
                p.dma("gpsimd", slot, wc[o], lsem)

        for s in range(NS):
            ev_xln = {}
            for tb in range(TB):
                t0 = s * S + tb * NT
                p.wait("sync", "MM", xs_free[0])
                p.wait("sync", "EV", xs_free[1])
                src = x[:, t0:t0 + NT].rearrange("(j p) t -> p j t", p=128)
                lx_xs = p.dma("sync", xs[:].rearrange("p (j t) -> p j t", j=JD), src, "LX")
                p.wait("scalar", "MM", prev_mm_stats)
                p.dwait("scalar", "LX")
                es_xsq = p.op("scalar", lambda e: e.activation(out=xsq, in_=xs[:],
                                                               func=AF.Square), "ES")
                p.wait("tensor", "EV", max(ev_init, psAB_free_ev))
                p.dwait("tensor", "LX")
                for j in range(JD):
                    p.op("tensor", lambda e, j=j: e.matmul(
                        out=pb[4][0:1, :], lhsT=ones_c[:], rhs=xs[:, j * NT:(j + 1) * NT],
                        start=(j == 0), stop=(j == JD - 1)))
                p.wait("tensor", "ES", es_xsq)
                mm_stats = 0
                for j in range(JD):
                    mm_stats = p.op("tensor", lambda e, j=j: e.matmul(
                        out=pb[5][0:1, :], lhsT=ones_c[:], rhs=xsq[:, j * NT:(j + 1) * NT],
                        start=(j == 0), stop=(j == JD - 1)),
                        "MM" if j == JD - 1 else None)
                prev_mm_stats = mm_stats
                p.wait("vector", "MM", mm_stats)
                p.op("vector", lambda e: e.tensor_scalar_mul(out=rmean[:], in0=pb[4][0:1, :],
                                                             scalar1=1.0 / D))
                p.op("vector", lambda e: e.tensor_scalar_mul(out=rvar[:], in0=pb[5][0:1, :],
                                                             scalar1=1.0 / D))
                p.op("vector", lambda e: e.tensor_mul(out=rt2[:], in0=rmean[:], in1=rmean[:]))
                psAB_free_ev = p.op("vector", lambda e: e.tensor_sub(out=rvar[:], in0=rvar[:],
                                                                     in1=rt2[:]), "EV")
                p.wait("scalar", "EV", psAB_free_ev)
                p.op("scalar", lambda e: e.activation(out=rt2[:], in_=rvar[:], func=AF.Ln,
                                                      bias=ceps[0:1, 0:1]))
                p.op("scalar", lambda e: e.drain())
                es_rstd = p.op("scalar", lambda e: e.activation(out=rA[:], in_=rt2[:],
                                                                func=AF.Exp, scale=-0.5), "ES")
                p.wait("vector", "ES", es_rstd)
                p.op("vector", lambda e: e.tensor_mul(out=rB[:], in0=rmean[:], in1=rA[:]))
                ev_rB = p.op("vector", lambda e: e.tensor_scalar_mul(out=rB[:], in0=rB[:],
                                                                     scalar1=-1.0), "EV")
                p.wait("tensor", "EV", max(ev_rB, psCD_free_ev))
                p.op("tensor", lambda e: e.matmul(out=pb[6][:], lhsT=ones_r[:], rhs=rA[:],
                                                  start=True, stop=True))
                mm_bcast = p.op("tensor", lambda e: e.matmul(out=pb[7][:], lhsT=ones_r[:],
                                                             rhs=rB[:], start=True, stop=True),
                                "MM")
                p.wait("vector", "MM", max(mm_bcast, xln_free_mm))
                last = 0
                for j in range(JD):
                    p.op("vector", lambda e, j=j: e.tensor_mul(
                        out=abuf[:, 0:NT], in0=xs[:, j * NT:(j + 1) * NT], in1=pb[6][:]))
                    last = p.op("vector", lambda e, j=j, tb=tb: e.tensor_add(
                        out=xln[:, j * S + tb * NT: j * S + (tb + 1) * NT],
                        in0=abuf[:, 0:NT], in1=pb[7][:]), "EV")
                ev_xln[tb] = last
                psCD_free_ev = last
                xs_free = (mm_stats, last)

            g12_done = {}
            es_after_g12 = {}
            ev_after_g12 = {}
            for o in range(JH):
                slot1 = wbA[:, (o % 2) * JD * 128:(o % 2 + 1) * JD * 128]
                slot2 = wbB[:, (o % 2) * JD * 128:(o % 2 + 1) * JD * 128]
                fm = g12_done.get(o - 2, wb_free_mm)
                wchunk_load(s, 1, o, w1t, wc1, JD, slot1, fm, "LWA", "WBA")
                wchunk_load(s, 2, o, w2t, wc2, JD, slot2, 0, "LWB", "WBB")
                p.dwait("tensor", "LWA")
                p.dwait("tensor", "LWB")
                if o == 0:
                    p.wait("tensor", "EV", ev_xln[TB - 1])
                for tb in range(TB):
                    i = o * TB + tb
                    pw1, pw2 = pb[i % 2], pb[2 + i % 2]
                    if i >= 2:
                        p.wait("tensor", "ES", es_after_g12[i - 2])
                        p.wait("tensor", "EV", ev_after_g12[i - 2])
                    for k in range(JD):
                        p.op("tensor", lambda e, k=k, tb=tb, slot1=slot1, pw1=pw1: e.matmul(
                            out=pw1[:], lhsT=slot1[:, k * 128:(k + 1) * 128],
                            rhs=xln[:, k * S + tb * NT: k * S + (tb + 1) * NT],
                            start=(k == 0), stop=(k == JD - 1)))
                    mm = 0
                    for k in range(JD):
                        mm = p.op("tensor", lambda e, k=k, tb=tb, slot2=slot2, pw2=pw2: e.matmul(
                            out=pw2[:], lhsT=slot2[:, k * 128:(k + 1) * 128],
                            rhs=xln[:, k * S + tb * NT: k * S + (tb + 1) * NT],
                            start=(k == 0), stop=(k == JD - 1)),
                            "MM" if k == JD - 1 else None)
                    p.wait("scalar", "MM", mm)
                    es_s = p.op("scalar", lambda e, i=i, o=o, pw1=pw1: e.activation(
                        out=abuf[:, (i % 2) * NT:(i % 2 + 1) * NT], in_=pw1[:], func=AF.Silu,
                        scale=s1c[:, o:o + 1]), "ES")
                    es_after_g12[i] = es_s
                    p.wait("vector", "MM", mm)
                    p.wait("vector", "ES", es_s)
                    if i == 0:
                        p.wait("vector", "MM", hb_free_mm)
                    ev_h = p.op("vector", lambda e, i=i, o=o, tb=tb, pw2=pw2:
                                e.scalar_tensor_tensor(
                        out=hb[:, o * S + tb * NT: o * S + (tb + 1) * NT], in0=pw2[:],
                        scalar=s2c[:, o:o + 1], in1=abuf[:, (i % 2) * NT:(i % 2 + 1) * NT],
                        op0=ALU.mult, op1=ALU.mult), "EV")
                    ev_after_g12[i] = ev_h
                g12_done[o] = mm
            ev_h_all = ev_after_g12[JH * TB - 1]
            mm_g12_done = p.tally["MM"]
            xln_free_mm = mm_g12_done
            wb_free_mm = mm_g12_done

            def big_gemm(s, tag, wsrc, wc, rhs_buf, rhs_S, n_out, ktiles,
                         evac_fn, first_waits):
                done = {}
                es_after = {}
                nonlocal wb_free_mm
                for o in range(n_out):
                    lsem, wsem = ("LWA", "WBA") if o % 2 == 0 else ("LWB", "WBB")
                    slot = (wbA if o % 2 == 0 else wbB)[:, 0:ktiles * 128]
                    fm = done.get(o - 2, wb_free_mm)
                    wchunk_load(s, tag, o, wsrc, wc, ktiles, slot, fm, lsem, wsem)
                    p.dwait("tensor", lsem)
                    if o == 0:
                        for sem, val in first_waits:
                            p.wait("tensor", sem, val)
                    mm = 0
                    for tb in range(TB):
                        i = o * TB + tb
                        pw = pb[i % 4]
                        if i >= 4:
                            sem, val = es_after[i - 4]
                            p.wait("tensor", sem, val)
                        for k in range(ktiles):
                            mm = p.op("tensor", lambda e, k=k, tb=tb, slot=slot, pw=pw,
                                      rhs_buf=rhs_buf, rhs_S=rhs_S: e.matmul(
                                out=pw[:], lhsT=slot[:, k * 128:(k + 1) * 128],
                                rhs=rhs_buf[:, k * rhs_S + tb * NT: k * rhs_S + (tb + 1) * NT],
                                start=(k == 0), stop=(k == ktiles - 1)),
                                "MM" if k == ktiles - 1 else None)
                        es_after[i] = evac_fn(o, tb, pw, i, mm)
                    done[o] = mm
                wb_free_mm = p.tally["MM"]
                return p.tally["MM"]

            af = arena_free

            def evac_g3(o, tb, pw, i, mm, af=af):
                p.wait("scalar", "MM", mm)
                if i == 0:
                    p.wait("scalar", "EV", af[0])
                    p.dwait("scalar", "ST")
                t = p.op("scalar", lambda e, o=o, tb=tb, pw=pw: e.activation(
                    out=gbuf[:, o * S + tb * NT: o * S + (tb + 1) * NT], in_=pw[:],
                    func=AF.Gelu, scale=h1sc[:, o:o + 1]), "ES")
                return ("ES", t)

            mm_g3_done = big_gemm(s, 3, h1t, wc3, hb, S, JH, JH, evac_g3,
                                  [("EV", ev_h_all)])
            es_g3_all = p.tally["ES"]

            def evac_g4(o, tb, pw, i, mm):
                p.wait("scalar", "MM", mm)
                t = p.op("scalar", lambda e, o=o, tb=tb, pw=pw: e.activation(
                    out=hb[:, o * S + tb * NT: o * S + (tb + 1) * NT], in_=pw[:],
                    func=AF.Gelu, scale=h2sc[:, o:o + 1]), "ES")
                return ("ES", t)

            mm_g4_done = big_gemm(s, 4, h2t, wc4, gbuf, S, JH, JH, evac_g4,
                                  [("ES", es_g3_all)])
            es_g4_all = p.tally["ES"]

            p.wait("sync", "MM", mm_g4_done)
            srcx = x[:, s * S:(s + 1) * S].rearrange("(j p) t -> p j t", p=128)
            lx_res = p.dma("sync", xres.rearrange("p (j t) -> p j t", j=JD), srcx, "LX")

            ev_after_g5 = {}
            st_after_g5 = {}

            def evac_g5(o, tb, pw, i, mm, af=af):
                p.wait("vector", "MM", mm)
                p.dwait("vector", "LX")
                p.dwait("vector", "ST")
                t = p.op("vector", lambda e, o=o, tb=tb, i=i, pw=pw: e.scalar_tensor_tensor(
                    out=ybuf[i % 2], in0=pw[:], scalar=s3h[:, o:o + 1],
                    in1=xres[:, o * S + tb * NT: o * S + (tb + 1) * NT],
                    op0=ALU.mult, op1=ALU.add), "EV")
                ev_after_g5[i] = t
                p.wait("sync", "EV", t)
                t0 = s * S + tb * NT
                stt = p.dma("sync", y[o * 128:(o + 1) * 128, t0:t0 + NT], ybuf[i % 2], "ST")
                st_after_g5[i] = stt
                return ("EV", t)

            mm_g5_done = big_gemm(s, 5, w3t, wc5, hb, S, JD, JH, evac_g5,
                                  [("ES", es_g4_all)])
            hb_free_mm = mm_g5_done
            arena_free = (ev_after_g5[JD * TB - 1], st_after_g5[JD * TB - 1])

        p.dwait("sync", "ST")
        p.dwait("sync", "WBA")
        p.dwait("sync", "WBB")


ALLSEMS = ["LWA", "LWB", "LX", "WBA", "WBB", "ST", "DB", "MM", "EV", "ES"]


def build_l0ffn(dbg=False):
    nc = bass.Bass("TRN2", target_bir_lowering=False, detect_race_conditions=RACE_DETECT)
    x = nc.dram_tensor("x", [D, T], F32, kind="ExternalInput").ap()
    y = nc.dram_tensor("y", [D, T], F32, kind="ExternalOutput").ap()
    with contextlib.ExitStack() as es:
        p = Prog(nc)
        p.make_sems(es, ALLSEMS)
        C = Ctx(nc, p, es)
        emit_l0ffn(C, x, y, wp="", first=True)
        with nc.Block() as block:
            p.replay(block)
    return nc



def _colmajor(v):
    return np.ascontiguousarray(np.asarray(v, np.float32).reshape(-1, 128).T)


def layout_l0ffn(inp):
    f = np.float32
    return dict(
        w1t=np.ascontiguousarray(np.asarray(inp["L0_ffn_w1_w"], f).T),
        w2t=np.ascontiguousarray(np.asarray(inp["L0_ffn_w2_w"], f).T),
        h1t=np.ascontiguousarray(np.asarray(inp["L0_ffn_h1_w"], f).T),
        h2t=np.ascontiguousarray(np.asarray(inp["L0_ffn_h2_w"], f).T),
        w3t=np.ascontiguousarray(np.asarray(inp["L0_ffn_w3_w"], f).T),
        s1v=_colmajor(inp["L0_ffn_w1_s"]),
        s2v=_colmajor(inp["L0_ffn_w2_s"]),
        h1sv=_colmajor(inp["L0_ffn_h1_s"]),
        h2sv=_colmajor(inp["L0_ffn_h2_s"]),
        s3v=_colmajor(inp["L0_ffn_w3_s"]),
    )


def _norm_block(p, pb, src3, xsq3, dst3, ones_c, ones_r, rows, ceps, eps_idx,
                tmp, gain, with_mean, prev_sq_mm, psAB_free, psCD_free,
                waits_t=(), waits_s=(), waits_v=(), dst2=None):
    rmean, rvar, rt2, rA, rB = rows
    p.wait("scalar", "MM", prev_sq_mm)
    for sem, val in waits_s:
        p.wait("scalar", sem, val)
    es_xsq = p.op("scalar", lambda e: e.activation(out=xsq3, in_=src3, func=AF.Square), "ES")
    for sem, val in waits_t:
        p.wait("tensor", sem, val)
    p.wait("tensor", "EV", psAB_free)
    if with_mean:
        for j in range(JD):
            p.op("tensor", lambda e, j=j: e.matmul(
                out=pb[4][0:1, :], lhsT=ones_c[:], rhs=src3[:, j, :],
                start=(j == 0), stop=(j == JD - 1)))
    p.wait("tensor", "ES", es_xsq)
    mm_stats = 0
    for j in range(JD):
        mm_stats = p.op("tensor", lambda e, j=j: e.matmul(
            out=pb[5][0:1, :], lhsT=ones_c[:], rhs=xsq3[:, j, :],
            start=(j == 0), stop=(j == JD - 1)), "MM" if j == JD - 1 else None)
    p.wait("vector", "MM", mm_stats)
    for sem, val in waits_v:
        p.wait("vector", sem, val)
    p.op("vector", lambda e: e.tensor_scalar_mul(out=rvar[:], in0=pb[5][0:1, :],
                                                 scalar1=1.0 / D))
    if with_mean:
        p.op("vector", lambda e: e.tensor_scalar_mul(out=rmean[:], in0=pb[4][0:1, :],
                                                     scalar1=1.0 / D))
        p.op("vector", lambda e: e.tensor_mul(out=rt2[:], in0=rmean[:], in1=rmean[:]))
        p.op("vector", lambda e: e.tensor_sub(out=rvar[:], in0=rvar[:], in1=rt2[:]))
    ev_rows = p.op("vector", lambda e: e.nop(), "EV")
    p.wait("scalar", "EV", ev_rows)
    p.op("scalar", lambda e: e.activation(out=rt2[:], in_=rvar[:], func=AF.Ln,
                                          bias=ceps[0:1, eps_idx:eps_idx + 1]))
    p.op("scalar", lambda e: e.drain())
    es_rstd = p.op("scalar", lambda e: e.activation(out=rA[:], in_=rt2[:],
                                                    func=AF.Exp, scale=-0.5), "ES")
    ev_rB = es_rstd
    if with_mean:
        p.wait("vector", "ES", es_rstd)
        p.op("vector", lambda e: e.tensor_mul(out=rB[:], in0=rmean[:], in1=rA[:]))
        ev_rB = p.op("vector", lambda e: e.tensor_scalar_mul(out=rB[:], in0=rB[:],
                                                             scalar1=-1.0), "EV")
    if with_mean:
        p.wait("tensor", "EV", max(ev_rB, psCD_free))
        p.op("tensor", lambda e: e.matmul(out=pb[6][:], lhsT=ones_r[:], rhs=rA[:],
                                          start=True, stop=True))
        mm_bc = p.op("tensor", lambda e: e.matmul(out=pb[7][:], lhsT=ones_r[:], rhs=rB[:],
                                                  start=True, stop=True), "MM")
    else:
        p.wait("tensor", "ES", es_rstd)
        p.wait("tensor", "EV", psCD_free)
        mm_bc = p.op("tensor", lambda e: e.matmul(out=pb[6][:], lhsT=ones_r[:], rhs=rA[:],
                                                  start=True, stop=True), "MM")
    p.wait("vector", "MM", mm_bc)
    last = 0
    for j in range(JD):
        p.op("vector", lambda e, j=j: e.tensor_mul(out=tmp[:], in0=src3[:, j, :],
                                                   in1=pb[6][:]))
        if with_mean:
            last = p.op("vector", lambda e, j=j: e.tensor_add(
                out=dst3[:, j, :], in0=tmp[:], in1=pb[7][:]), "EV")
            if dst2 is not None:
                last = p.op("vector", lambda e, j=j: e.tensor_add(
                    out=dst2[:, j, :], in0=tmp[:], in1=pb[7][:]), "EV")
        else:
            last = p.op("vector", lambda e, j=j: e.tensor_scalar_mul(
                out=dst3[:, j, :], in0=tmp[:], scalar1=gain[:, j:j + 1]), "EV")
    return mm_stats, last


def emit_l0attn(C, xin, xout, wp="a0", first=True):
    H2 = 2048
    JH2 = H2 // 128
    S = 1024
    NS = T // S
    TB_T = T // NT
    TB = S // NT
    CH = 128
    NC = T // CH

    nc = C.nc
    p = C.p
    nm = (wp + "_") if wp else ""
    x, y = xin, xout
    wqt = C.dram(nm + "wqt", [D, D])
    wkt = C.dram(nm + "wkt", [D, D])
    wvt = C.dram(nm + "wvt", [D, D])
    wot = C.dram(nm + "wot", [D, D])
    wgt = C.dram(nm + "wgt", [D, H2])
    wut = C.dram(nm + "wut", [D, H2])
    wdt = C.dram(nm + "wdt", [H2, D])
    n1v = C.dram(nm + "n1v", [128, JD])
    n2v = C.dram(nm + "n2v", [128, JD])
    alv = C.dram(nm + "alv", [128, JD])
    bwv = C.dram(nm + "bwv", [128, JD])
    bbv = C.dram(nm + "bbv", [1, 1])
    msl = C.dram(nm + "msl", [128, 128])
    mli = C.dram(nm + "mli", [128, 128])
    xisc = C.dram(nm + "xisc", [D, T], F32, "Internal")
    qsc = C.dram(nm + "qsc", [JD, 128, T], BF16, "Internal")
    ksc = C.dram(nm + "ksc", [JD, 128, T], BF16, "Internal")
    vsc = C.dram(nm + "vsc", [JD, 128, T], BF16, "Internal")
    osc = C.dram(nm + "osc", [JD, 128, T], BF16, "Internal")
    wco = C.dram(nm + "wco", [JD, 128, JD * 128], BF16, "Internal")
    wcg = C.dram(nm + "wcg", [JH2, 128, JD * 128], BF16, "Internal")
    wcu = C.dram(nm + "wcu", [JH2, 128, JD * 128], BF16, "Internal")
    wcd = C.dram(nm + "wcd", [JD, 128, JH2 * 128], BF16, "Internal")

    if True:
        xs = C.xs
        xln = C.xln
        hb = C.hb
        arena = C.arena
        wbA = C.wbA
        wbB = C.wbB
        abuf = C.abuf
        n1c = C.sb(nm + "n1c", [128, JD], F32)
        n2c = C.sb(nm + "n2c", [128, JD], F32)
        amc = C.sb(nm + "amc", [128, JD], F32)
        alc = C.sb(nm + "alc", [128, JD], F32)
        bwc = C.sb(nm + "bwc", [128, JD], BF16)
        bwf = C.sb(nm + "bwf", [128, JD], F32)
        betc = C.sb(nm + "betc", [128, NC], F32)
        carry = [C.sb(nm + f"carry{j}", [128, 1], F32) for j in range(JD)]
        bbb = C.sb(nm + "bbb", [128, 1], F32)
        bb1 = C.sb(nm + "bb1", [1, 1], F32)
        maskl = C.sb(nm + "maskl", [128, 128], BF16)
        maski = C.sb(nm + "maski", [128, 128], BF16)
        identb = C.sb(nm + "identb", [128, 128], BF16)
        ones_r = C.ones_r
        ones_c = C.ones_c
        rmean, rvar, rt2, rA = C.rmean, C.rvar, C.rt2, C.rA
        rB = rt2
        ceps = C.ceps
        pb = C.pb

        pwr = xln[:, 0:1024]
        pwrT = xln[:, 1024:2048]
        ubuf = xln[:, 2048:3072]
        wtb = xln[:, 3072:3200]
        wtT = xln[:, 3200:3328]
        t1f = xs[:, 0:1024]

        rows = (rmean, rvar, rt2, rA, rB)
        arena_f = arena.bitcast(F32)
        hb_f = hb.bitcast(F32)
        ambc = arena_f[:, 12288:12288 + JD * NT]
        xi_tb = arena_f[:, 0:JD * NT]
        qf = arena[:, 0:1024]
        kf = arena[:, 1024:2048]
        vtm = arena[:, 2048:3072]
        ktm = arena[:, 3072:4096]
        otm = arena[:, 4096:5120]
        ofm = arena[:, 5120:6144]
        vstg = arena[:, 6144:7168]
        Sf = hb_f[:, 0:8192]
        Sbb = hb[:, 16384:24576]
        kstage = xln[:, 0:JD * NT]
        xsq = abuf[:, NT:2 * NT]
        vsq = arena_f[:, 8192:8192 + JD * NT]
        xnew = arena_f[:, 0:JD * S]
        xresC = arena_f[:, JD * S:2 * JD * S]
        mbuf = hb[:, 0:JH2 * S]
        ostripe = xln[:, 0:JD * S]
        h2st = xln[:, 0:JD * S]
        tmpNT = abuf[:, 0:NT]

        C.init_common()

        for dst, srcd in ((n1c, n1v), (n2c, n2v), (alc, alv), (bwf, bwv)):
            p.dma("sync", dst[:], srcd, "LX")
        p.dma("sync", bb1[:], bbv, "LX")
        p.wait("gpsimd", "MM", p.tally["MM"])
        p.dwait("gpsimd", "WBA")
        p.dma("gpsimd", maskl[:], msl, "LWA")
        p.dma("gpsimd", maski[:], mli, "LWA")
        for j in range(JD):
            p.op("vector", lambda e, j=j: e.memset(carry[j][:], 0.0))
        p.dwait("vector", "LWA")
        p.op("vector", lambda e: e.tensor_sub(out=identb[:], in0=maski[:], in1=maskl[:]))
        p.dwait("vector", "LX")
        p.op("vector", lambda e: e.tensor_copy(out=bwc[:], in_=bwf[:]))
        ev_init0 = p.op("vector", lambda e: e.nop(), "EV")
        p.wait("scalar", "EV", ev_init0)
        es_al = p.op("scalar", lambda e: e.activation(out=alc[:], in_=alc[:],
                                                      func=AF.Sigmoid), "ES")
        p.wait("vector", "ES", es_al)
        p.op("vector", lambda e: e.tensor_scalar(out=amc[:], in0=alc[:], scalar1=-1.0,
                                                 scalar2=1.0, op0=ALU.mult, op1=ALU.add))
        p.op("vector", lambda e: e.drain())
        p.op("vector", lambda e: e.memset(abuf[:, 0:NT], 0.0))
        for j in range(JD):
            p.op("vector", lambda e, j=j: e.tensor_scalar(
                out=ambc[:, j * NT:(j + 1) * NT], in0=abuf[:, 0:NT], scalar1=0.0,
                scalar2=amc[:, j:j + 1], op0=ALU.mult, op1=ALU.add))
        p.op("vector", lambda e: e.drain())
        ev_init = p.op("vector", lambda e: e.nop(), "EV")
        p.wait("tensor", "EV", ev_init)
        mm_bb = p.op("tensor", lambda e: e.matmul(out=pb[6][:, 0:1], lhsT=ones_r[:],
                                                  rhs=bb1[:], start=True, stop=True), "MM")
        p.wait("vector", "MM", mm_bb)
        ev_bbb = p.op("vector", lambda e: e.tensor_copy(out=bbb[:], in_=pb[6][:, 0:1]), "EV")

        hb3full = hb.ap().rearrange("p (j t) -> p j t", j=JD)
        xs3 = xs[:].rearrange("p (j t) -> p j t", j=JD)
        vsq3 = vsq.rearrange("p (j t) -> p j t", j=JD)
        xi3 = xi_tb.rearrange("p (j t) -> p j t", j=JD)
        xs_free = (0, ev_init0)
        xi_free = (0, 0)
        psAB_free = ev_bbb
        psCD_free = ev_bbb
        prev_sq = 0
        ev_h = 0
        st_xi = 0
        for tb in range(TB_T):
            t0 = tb * NT
            p.wait("sync", "MM", xs_free[0])
            p.wait("sync", "EV", xs_free[1])
            lx_xs = p.dma("sync", xs3, x[:, t0:t0 + NT].rearrange("(j p) t -> p j t", p=128),
                          "LX")
            if True:
                mm_s1, ev_ln = _norm_block(
                    p, pb, xs3, vsq3, xi3, ones_c, ones_r, rows, ceps, 0, tmpNT, None,
                    True, prev_sq, psAB_free, psCD_free,
                    waits_t=[("LX", lx_xs), ("EV", ev_init), ("MM", xi_free[0])],
                    waits_s=[("LX", lx_xs)],
                    waits_v=[("ST", st_xi)])
                xs_free = (mm_s1, ev_ln)
                prev_sq = mm_s1
                psAB_free = ev_ln
                psCD_free = ev_ln
                ev_h = ev_ln
            if True:
                p.wait("sync", "EV", ev_ln)
                st_xi = p.dma("sync", xisc[:, t0:t0 + NT].rearrange("(j p) t -> p j t", p=128),
                              xi3, "ST")
            if True:
                mm_s2, ev_done = _norm_block(
                    p, pb, xi3, vsq3, hb3full[:, :, t0:t0 + NT], ones_c, ones_r,
                    rows, ceps, 1,
                    tmpNT, n1c, False, mm_s1, ev_ln, ev_ln,
                    waits_t=[("EV", ev_ln)], waits_s=[("EV", ev_ln)])
                prev_sq = mm_s2
                psAB_free = ev_done
                psCD_free = ev_done
                ev_h = ev_done
                xi_free = (mm_s2, ev_done)
        ev_h_all = ev_h

        def proj_matmuls(wslot, o, tb, pw, wait_es):
            if wait_es is not None:
                p.wait("tensor", wait_es[0], wait_es[1])
            mm = 0
            for k in range(JD):
                mm = p.op("tensor", lambda e, k=k, o=o, tb=tb, pw=pw, wslot=wslot: e.matmul(
                    out=pw[:], lhsT=wslot[:, k * 1024 + o * 128: k * 1024 + (o + 1) * 128],
                    rhs=hb[:, k * T + tb * NT: k * T + (tb + 1) * NT],
                    start=(k == 0), stop=(k == JD - 1)), "MM" if k == JD - 1 else None)
            return mm

        p.wait("gpsimd", "EV", ev_init0)
        p.dma("gpsimd", wbA[:].rearrange("p (k o) -> p k o", k=JD),
              wqt.rearrange("(k p) o -> p k o", p=128), "LWA")
        p.dwait("tensor", "LWA")
        p.wait("tensor", "EV", ev_h_all)
        st_slot = {}
        ev_evac = {}
        for tb in range(TB_T):
            for o in range(JD):
                i = tb * JD + o
                pw = pb[i % 4]
                if i >= 4:
                    p.wait("tensor", "EV", ev_evac[i - 4])
                mm = proj_matmuls(wbA, o, tb, pw, None)
                p.wait("vector", "MM", mm)
                if tb > 0:
                    p.dwait("vector", "ST")
                ev = p.op("vector", lambda e, o=o, pw=pw: e.tensor_copy(
                    out=kstage[:, o * NT:(o + 1) * NT], in_=pw[:]), "EV")
                ev_evac[i] = ev
                p.wait("sync", "EV", ev)
                st_slot[(tb, o)] = p.dma("sync", qsc[o, :, tb * NT:(tb + 1) * NT],
                                         kstage[:, o * NT:(o + 1) * NT], "ST")
        mm_q_done = p.tally["MM"]
        st_q_done = p.tally["ST"]

        p.wait("gpsimd", "MM", mm_q_done)
        p.dma("gpsimd", wbB[:].rearrange("p (k o) -> p k o", k=JD),
              wkt.rearrange("(k p) o -> p k o", p=128), "LWB")
        p.dwait("tensor", "LWB")
        p.wait("tensor", "EV", p.tally["EV"])
        ev_evac = {}
        es_sq = {}
        st_slot = {}
        st_floor = st_q_done
        prev_ss_mm = 0
        for tb in range(TB_T):
            mm_o = {}
            for o in range(JD):
                i = tb * JD + o
                pw = pb[i % 4]
                if i >= 4:
                    p.wait("tensor", "EV", ev_evac[i - 4])
                    p.wait("tensor", "ES", es_sq[i - 4])
                mm = proj_matmuls(wbB, o, tb, pw, None)
                mm_o[o] = mm
                p.wait("vector", "MM", mm)
                p.dwait("vector", "ST")
                ev = p.op("vector", lambda e, o=o, pw=pw: e.tensor_copy(
                    out=kstage[:, o * NT:(o + 1) * NT], in_=pw[:]), "EV")
                ev_evac[i] = ev
                if True:
                    p.wait("scalar", "MM", max(mm, prev_ss_mm))
                    p.wait("scalar", "EV", ev)
                    es_sq[i] = p.op("scalar", lambda e, o=o, pw=pw: e.activation(
                        out=vsq3[:, o, :], in_=pw[:], func=AF.Square), "ES")
            if True:
                p.wait("tensor", "ES", es_sq[tb * JD + JD - 1])
                p.wait("tensor", "EV", psAB_free)
                mm_ss = 0
                for o in range(JD):
                    mm_ss = p.op("tensor", lambda e, o=o: e.matmul(
                        out=pb[4][0:1, :], lhsT=ones_c[:], rhs=vsq3[:, o, :],
                        start=(o == 0), stop=(o == JD - 1)), "MM" if o == JD - 1 else None)
                prev_ss_mm = mm_ss
            if True:
                p.wait("vector", "MM", mm_ss)
                psAB_free = p.op("vector", lambda e: e.tensor_single_scalar(
                    out=rvar[:], in_=pb[4][0:1, :], scalar=1e-24, op=ALU.max), "EV")
                p.wait("scalar", "EV", psAB_free)
                p.op("scalar", lambda e: e.activation(out=rt2[:], in_=rvar[:], func=AF.Ln))
                p.op("scalar", lambda e: e.drain())
                es_r = p.op("scalar", lambda e: e.activation(out=rA[:], in_=rt2[:],
                                                             func=AF.Exp, scale=-0.5), "ES")
                p.wait("tensor", "ES", es_r)
                p.wait("tensor", "EV", psCD_free)
                mm_bc = p.op("tensor", lambda e: e.matmul(out=pb[6][:], lhsT=ones_r[:],
                                                          rhs=rA[:], start=True, stop=True), "MM")
            if True:
                p.wait("vector", "MM", mm_bc)
                lastev = 0
                for o in range(JD):
                    lastev = p.op("vector", lambda e, o=o: e.tensor_mul(
                        out=kstage[:, o * NT:(o + 1) * NT],
                        in0=kstage[:, o * NT:(o + 1) * NT], in1=pb[6][:]), "EV")
                    p.wait("sync", "EV", lastev)
                    st_slot[(tb, o)] = p.dma("sync", ksc[o, :, tb * NT:(tb + 1) * NT],
                                             kstage[:, o * NT:(o + 1) * NT], "ST")
                psCD_free = lastev
        mm_k_done = p.tally["MM"]

        p.wait("gpsimd", "MM", mm_k_done)
        p.dma("gpsimd", wbA[:].rearrange("p (k o) -> p k o", k=JD),
              wvt.rearrange("(k p) o -> p k o", p=128), "LWA")
        p.dwait("tensor", "LWA")
        p.wait("tensor", "EV", p.tally["EV"])
        ev_evac = {}
        st_slot = {}
        st_floor = p.tally["ST"]
        for tb in range(TB_T):
            for o in range(JD):
                i = tb * JD + o
                pw = pb[i % 4]
                if i >= 4:
                    p.wait("tensor", "EV", ev_evac[i - 4])
                mm = proj_matmuls(wbA, o, tb, pw, None)
                p.wait("vector", "MM", mm)
                ev = p.op("vector", lambda e, o=o, pw=pw: e.tensor_scalar_mul(
                    out=xs[:, o * NT:(o + 1) * NT], in0=pw[:], scalar1=alc[:, o:o + 1]), "EV")
                ev_evac[i] = ev
            for o in range(JD):
                p.dwait("vector", "ST")
                p.op("vector", lambda e: e.drain())
                p.op("vector", lambda e, o=o: e.tensor_tensor_scan(
                    out=kstage[:, o * NT:(o + 1) * NT].bitcast(F32) if False else
                    xs[:, o * NT:(o + 1) * NT], data0=ambc[:, o * NT:(o + 1) * NT],
                    data1=xs[:, o * NT:(o + 1) * NT], initial=carry[o][:],
                    op0=ALU.mult, op1=ALU.add))
                ev = p.op("vector", lambda e, o=o: e.tensor_copy(
                    out=kstage[:, o * NT:(o + 1) * NT], in_=xs[:, o * NT:(o + 1) * NT]), "EV")
                p.op("vector", lambda e: e.drain())
                p.op("vector", lambda e, o=o: e.tensor_copy(
                    out=carry[o][:], in_=xs[:, (o + 1) * NT - 1:(o + 1) * NT]))
                p.wait("sync", "EV", ev)
                st_slot[(tb, o)] = p.dma("sync", vsc[o, :, tb * NT:(tb + 1) * NT],
                                         kstage[:, o * NT:(o + 1) * NT], "ST")
            p.op("vector", lambda e: e.drain())
        mm_v_done = p.tally["MM"]
        ev_v_done = p.tally["EV"]

        p.wait("tensor", "EV", p.tally["EV"])
        es_bet = {}
        for c in range(NC):
            pw = pb[4 + c % 2]
            if c >= 2:
                p.wait("tensor", "ES", es_bet[c - 2])
            mm = 0
            for k in range(JD):
                mm = p.op("tensor", lambda e, k=k, c=c, pw=pw: e.matmul(
                    out=pw[:, 0:1], lhsT=hb[:, k * T + c * 128: k * T + (c + 1) * 128],
                    rhs=bwc[:, k:k + 1], start=(k == 0), stop=(k == JD - 1)),
                    "MM" if k == JD - 1 else None)
            p.wait("scalar", "MM", mm)
            es_bet[c] = p.op("scalar", lambda e, c=c, pw=pw: e.activation(
                out=betc[:, c:c + 1], in_=pw[:, 0:1], func=AF.Sigmoid, bias=bbb[:]), "ES")
        es_beta_done = p.tally["ES"]

        pb6b = pb[6].bitcast(BF16)
        p.wait("vector", "ES", es_beta_done)
        p.op("vector", lambda e: e.memset(Sf[:, :], 0.0))
        ev_S0 = p.op("vector", lambda e: e.memset(Sbb[:, :], 0.0), "EV")

        sb_ready = ("EV", ev_S0)
        loads_free_mm = 0
        ev_sf_last = 0
        st_o = {}
        for c in range(NC):
            cs = slice(c * CH, (c + 1) * CH)
            p.wait("sync", "MM", loads_free_mm)
            lq = p.dma("sync", qf.rearrange("p (j t) -> p j t", j=JD),
                       qsc[:, :, cs].rearrange("j p t -> p j t"), "LX")
            lk = p.dma("sync", kf.rearrange("p (j t) -> p j t", j=JD),
                       ksc[:, :, cs].rearrange("j p t -> p j t"), "LX")
            lv = p.dma("sync", vstg.rearrange("p (j t) -> p j t", j=JD),
                       vsc[:, :, cs].rearrange("j p t -> p j t"), "LX")
            p.wait("tensor", sb_ready[0], sb_ready[1])
            p.dwait("tensor", "LX")
            mm = 0
            for j in range(JD):
                mm = p.op("tensor", lambda e, j=j: e.transpose(
                    out=pb6b[:, j * 128:(j + 1) * 128], in_=kf[:, j * 128:(j + 1) * 128],
                    identity=identb[:]), "MM" if j == JD - 1 else None)
            mm_trk = mm
            p.wait("vector", "MM", mm_trk)
            ev_ktm = p.op("vector", lambda e: e.tensor_copy(out=ktm[:], in_=pb6b[:]), "EV")
            p.wait("tensor", "EV", ev_ktm)
            for j in range(JD):
                mm = p.op("tensor", lambda e, j=j: e.transpose(
                    out=pb6b[:, j * 128:(j + 1) * 128], in_=vstg[:, j * 128:(j + 1) * 128],
                    identity=identb[:]), "MM" if j == JD - 1 else None)
            mm_trv = mm
            p.wait("vector", "MM", mm_trv)
            ev_vtm = p.op("vector", lambda e: e.tensor_copy(out=vtm[:], in_=pb6b[:]), "EV")
            mm = 0
            for h in range(2):
                for j in range(JD):
                    mm = p.op("tensor", lambda e, h=h, j=j: e.matmul(
                        out=pb[h][:], lhsT=kf[:, j * 128:(j + 1) * 128],
                        rhs=Sbb[:, j * 1024 + h * NT: j * 1024 + (h + 1) * NT],
                        start=(j == 0), stop=(j == JD - 1)),
                        "MM" if (h == 1 and j == JD - 1) else None)
            mm_m1 = mm
            for j in range(JD):
                mm = p.op("tensor", lambda e, j=j: e.matmul(
                    out=pb[4][:, 0:128], lhsT=kf[:, j * 128:(j + 1) * 128],
                    rhs=kf[:, j * 128:(j + 1) * 128],
                    start=(j == 0), stop=(j == JD - 1)), "MM" if j == JD - 1 else None)
            mm_G = mm
            for j in range(JD):
                mm = p.op("tensor", lambda e, j=j: e.matmul(
                    out=pb[5][:, 0:128], lhsT=qf[:, j * 128:(j + 1) * 128],
                    rhs=kf[:, j * 128:(j + 1) * 128],
                    start=(j == 0), stop=(j == JD - 1)), "MM" if j == JD - 1 else None)
            mm_W = mm
            for h in range(2):
                for j in range(JD):
                    p.op("tensor", lambda e, h=h, j=j: e.matmul(
                        out=pb[2 + h][:], lhsT=qf[:, j * 128:(j + 1) * 128],
                        rhs=Sbb[:, j * 1024 + h * NT: j * 1024 + (h + 1) * NT],
                        start=(j == 0), stop=False))
            p.wait("vector", "MM", mm_G)
            ev_A = p.op("vector", lambda e, c=c: e.scalar_tensor_tensor(
                out=pwr[:, 0:128], in0=pb[4][:, 0:128], scalar=betc[:, c:c + 1],
                in1=maskl[:], op0=ALU.mult, op1=ALU.mult), "EV")
            p.wait("vector", "MM", mm_W)
            ev_W = p.op("vector", lambda e: e.tensor_mul(
                out=wtb[:], in0=pb[5][:, 0:128], in1=maski[:]), "EV")
            p.wait("vector", "MM", mm_m1)
            p.wait("vector", "EV", ev_vtm)
            p.op("vector", lambda e: e.tensor_sub(out=t1f[:, 0:NT], in0=vtm[:, 0:NT],
                                                  in1=pb[0][:]))
            p.op("vector", lambda e: e.tensor_sub(out=t1f[:, NT:2 * NT], in0=vtm[:, NT:2 * NT],
                                                  in1=pb[1][:]))
            ev_U0 = p.op("vector", lambda e, c=c: e.tensor_scalar_mul(
                out=ubuf[:], in0=t1f[:], scalar1=betc[:, c:c + 1]), "EV")
            p.wait("tensor", "EV", ev_A)
            ev_P = ev_A
            for m in range(6):
                p.wait("tensor", "EV", ev_P)
                mm_T = p.op("tensor", lambda e, m=m: e.transpose(
                    out=pb6b[:, m * 128:(m + 1) * 128], in_=pwr[:, m * 128:(m + 1) * 128],
                    identity=identb[:]), "MM")
                p.wait("vector", "MM", mm_T)
                ev_T = p.op("vector", lambda e, m=m: e.tensor_copy(
                    out=pwrT[:, m * 128:(m + 1) * 128],
                    in_=pb6b[:, m * 128:(m + 1) * 128]), "EV")
                p.wait("tensor", "EV", ev_T)
                mm_P = p.op("tensor", lambda e, m=m: e.matmul(
                    out=pb[7][:, 0:128], lhsT=pwrT[:, m * 128:(m + 1) * 128],
                    rhs=pwr[:, m * 128:(m + 1) * 128], start=True, stop=True), "MM")
                p.wait("vector", "MM", mm_P)
                ev_P = p.op("vector", lambda e, m=m: e.tensor_copy(
                    out=pwr[:, (m + 1) * 128:(m + 2) * 128], in_=pb[7][:, 0:128]), "EV")
            p.wait("tensor", "EV", ev_P)
            mm_T6 = p.op("tensor", lambda e: e.transpose(
                out=pb6b[:, 6 * 128:7 * 128], in_=pwr[:, 6 * 128:7 * 128],
                identity=identb[:]), "MM")
            p.wait("vector", "MM", mm_T6)
            ev_T6 = p.op("vector", lambda e: e.tensor_copy(
                out=pwrT[:, 6 * 128:7 * 128], in_=pb6b[:, 6 * 128:7 * 128]), "EV")
            ev_app = ev_U0
            for m in [6, 5, 4, 3, 2, 1, 0]:
                p.wait("tensor", "EV", ev_app if m != 6 else max(ev_app, ev_T6))
                if m == 6:
                    p.wait("tensor", "EV", ev_T6)
                mm1 = p.op("tensor", lambda e, m=m: e.matmul(
                    out=pb[7][:], lhsT=pwrT[:, m * 128:(m + 1) * 128],
                    rhs=ubuf[:, 0:NT], start=True, stop=True), "MM")
                mm2 = p.op("tensor", lambda e, m=m: e.matmul(
                    out=pb[0][:], lhsT=pwrT[:, m * 128:(m + 1) * 128],
                    rhs=ubuf[:, NT:2 * NT], start=True, stop=True), "MM")
                op = ALU.add if m > 0 else ALU.subtract
                p.wait("vector", "MM", mm2)
                p.op("vector", lambda e, op=op: e.tensor_tensor(
                    out=ubuf[:, 0:NT], in0=ubuf[:, 0:NT], in1=pb[7][:], op=op))
                ev_app = p.op("vector", lambda e, op=op: e.tensor_tensor(
                    out=ubuf[:, NT:2 * NT], in0=ubuf[:, NT:2 * NT], in1=pb[0][:], op=op), "EV")
            ev_Ufinal = ev_app
            p.wait("tensor", "EV", max(ev_W, ev_T6))
            mm_wt = p.op("tensor", lambda e: e.transpose(
                out=pb6b[:, 7 * 128:8 * 128], in_=wtb[:], identity=identb[:]), "MM")
            p.wait("vector", "MM", mm_wt)
            ev_wtT = p.op("vector", lambda e: e.tensor_copy(
                out=wtT[:], in_=pb6b[:, 7 * 128:8 * 128]), "EV")
            p.wait("tensor", "EV", max(ev_Ufinal, ev_wtT))
            p.op("tensor", lambda e: e.matmul(out=pb[2][:], lhsT=wtT[:], rhs=ubuf[:, 0:NT],
                                              start=False, stop=True))
            mm_O = p.op("tensor", lambda e: e.matmul(out=pb[3][:], lhsT=wtT[:],
                                                     rhs=ubuf[:, NT:2 * NT],
                                                     start=False, stop=True), "MM")
            loads_free_mm = mm_O
            p.wait("vector", "MM", mm_O)
            p.op("vector", lambda e: e.tensor_copy(out=otm[:, 0:NT], in_=pb[2][:]))
            ev_otm = p.op("vector", lambda e: e.tensor_copy(out=otm[:, NT:2 * NT],
                                                            in_=pb[3][:]), "EV")
            p.wait("tensor", "EV", max(ev_otm, ev_wtT))
            for j in range(JD):
                mm = p.op("tensor", lambda e, j=j: e.transpose(
                    out=pb6b[:, j * 128:(j + 1) * 128], in_=otm[:, j * 128:(j + 1) * 128],
                    identity=identb[:]), "MM" if j == JD - 1 else None)
            mm_otr = mm
            p.wait("vector", "MM", mm_otr)
            p.dwait("vector", "ST")
            ev_ofm = p.op("vector", lambda e: e.tensor_copy(out=ofm[:], in_=pb6b[:]), "EV")
            p.wait("sync", "EV", ev_ofm)
            st_o[c] = p.dma("sync", osc[:, :, cs].rearrange("j p t -> p j t"),
                            ofm.rearrange("p (j t) -> p j t", j=JD), "ST")
            ev_sf = {}
            for j in range(JD):
                pwa, pwb = (pb[0], pb[1]) if j % 2 == 0 else (pb[4], pb[5])
                if j >= 2:
                    p.wait("tensor", "EV", ev_sf[j - 2])
                elif j == 0:
                    p.wait("tensor", "EV", ev_Ufinal)
                mm = 0
                for h in range(2):
                    mm = p.op("tensor", lambda e, j=j, h=h, pwa=pwa, pwb=pwb: e.matmul(
                        out=(pwa if h == 0 else pwb)[:], lhsT=ktm[:, j * 128:(j + 1) * 128],
                        rhs=ubuf[:, h * NT:(h + 1) * NT], start=True, stop=True),
                        "MM" if h == 1 else None)
                p.wait("vector", "MM", mm)
                p.op("vector", lambda e, j=j, pwa=pwa: e.tensor_add(
                    out=Sf[:, j * 1024: j * 1024 + NT], in0=Sf[:, j * 1024: j * 1024 + NT],
                    in1=pwa[:]))
                ev_sf[j] = p.op("vector", lambda e, j=j, pwb=pwb: e.tensor_add(
                    out=Sf[:, j * 1024 + NT: (j + 1) * 1024],
                    in0=Sf[:, j * 1024 + NT: (j + 1) * 1024], in1=pwb[:]), "EV")
                p.wait("scalar", "EV", ev_sf[j])
                es_sb = p.op("scalar", lambda e, j=j: e.activation(
                    out=Sbb[:, j * 1024:(j + 1) * 1024], in_=Sf[:, j * 1024:(j + 1) * 1024],
                    func=AF.Copy), "ES")
            sb_ready = ("ES", es_sb)
            ev_sf_last = ev_sf[JD - 1]
        mm_deltadone = p.tally["MM"]
        ev_deltadone = p.tally["EV"]
        es_deltadone = p.tally["ES"]
        st_odone = p.tally["ST"]

        wb_after = {}
        xnew3 = xnew.rearrange("p (j t) -> p j t", j=JD)
        xresC3 = xresC.rearrange("p (j t) -> p j t", j=JD)
        h2st3 = h2st.rearrange("p (j t) -> p j t", j=JD)
        wb_free_mm = mm_deltadone
        wb_free_wb = 0
        ost_free = (mm_deltadone, 0)
        xnew_free_ev = 0
        mbuf_free_mm = 0
        st_y = {}
        prev_sq = 0
        psAB_free = p.tally["EV"]
        psCD_free = p.tally["EV"]
        p.wait("vector", "EV", psAB_free)

        def wchunk(tag, o, wsrc, wc, ktiles, slot, free_mm, s, lsem, wsem):
            if free_mm:
                p.wait("gpsimd", "MM", free_mm)
            if s == 0:
                if (tag, o - 2) in wb_after:
                    p.dwait("gpsimd", wsem)
                view3 = slot.rearrange("p (k o) -> p k o", k=ktiles)
                src = wsrc[:, o * 128:(o + 1) * 128].rearrange("(k p) o -> p k o", p=128)
                p.dma("gpsimd", view3, src, lsem)
                p.dwait("sync", lsem)
                wb_after[(tag, o)] = p.dma("sync", wc[o], slot, wsem)
            else:
                p.dwait("gpsimd", wsem)
                p.dma("gpsimd", slot, wc[o], lsem)

        for s in range(NS):
            p.wait("sync", "MM", ost_free[0])
            p.wait("sync", "EV", xnew_free_ev)
            lo = p.dma("sync", ostripe.rearrange("p (j t) -> p j t", j=JD),
                       osc[:, :, s * S:(s + 1) * S].rearrange("j p t -> p j t"), "LX")
            lxr = p.dma("sync", xresC3, xisc[:, s * S:(s + 1) * S].rearrange(
                "(j p) t -> p j t", p=128), "LX")
            g1_done = {}
            ev_g1 = {}
            for o in range(JD):
                lsem, wsem = ("LWA", "WBA") if o % 2 == 0 else ("LWB", "WBB")
                slot = (wbA if o % 2 == 0 else wbB)[:, 0:JD * 128]
                wchunk(1, o, wot, wco, JD, slot, g1_done.get(o - 2, wb_free_mm), s,
                       lsem, wsem)
                p.dwait("tensor", lsem)
                if o == 0:
                    p.dwait("tensor", "LX")
                mm = 0
                for tb in range(TB):
                    i = o * TB + tb
                    pw = pb[i % 4]
                    if i >= 4:
                        p.wait("tensor", "EV", ev_g1[i - 4])
                    for k in range(JD):
                        mm = p.op("tensor", lambda e, k=k, tb=tb, slot=slot, pw=pw: e.matmul(
                            out=pw[:], lhsT=slot[:, k * 128:(k + 1) * 128],
                            rhs=ostripe[:, k * S + tb * NT: k * S + (tb + 1) * NT],
                            start=(k == 0), stop=(k == JD - 1)),
                            "MM" if k == JD - 1 else None)
                    p.wait("vector", "MM", mm)
                    p.dwait("vector", "LX")
                    if i == 0:
                        p.dwait("vector", "ST")
                    ev_g1[i] = p.op("vector", lambda e, o=o, tb=tb, pw=pw: e.tensor_add(
                        out=xnew[:, o * S + tb * NT: o * S + (tb + 1) * NT],
                        in0=xresC[:, o * S + tb * NT: o * S + (tb + 1) * NT], in1=pw[:]), "EV")
                g1_done[o] = mm
            mm_c1_done = p.tally["MM"]
            ev_c1_all = ev_g1[JD * TB - 1]
            wb_free_mm = mm_c1_done
            p.wait("sync", "EV", ev_c1_all)
            lxr2 = p.dma("sync", xresC3, x[:, s * S:(s + 1) * S].rearrange(
                "(j p) t -> p j t", p=128), "LX")
            vsqC3 = hb_f[:, 8192:8192 + JD * NT].rearrange("p (j t) -> p j t", j=JD)
            ev_h2 = {}
            for tb in range(TB):
                src3 = xnew3[:, :, tb * NT:(tb + 1) * NT]
                dst3 = h2st3[:, :, tb * NT:(tb + 1) * NT]
                mm_stats, ev_done = _norm_block(
                    p, pb, src3, vsqC3, dst3, ones_c, ones_r, rows, ceps, 1, tmpNT, n2c,
                    False, prev_sq, psAB_free, psCD_free,
                    waits_t=[("EV", ev_c1_all), ("MM", mm_c1_done)],
                    waits_s=[("EV", ev_c1_all)])
                prev_sq = mm_stats
                psAB_free = ev_done
                psCD_free = ev_done
                ev_h2[tb] = ev_done
            ev_h2_all = ev_h2[TB - 1]
            es_g3 = {}
            ev_g3 = {}
            g3_done = {}
            for o in range(JH2):
                slotg = wbA[:, (o % 2) * JD * 128:(o % 2 + 1) * JD * 128]
                slotu = wbB[:, (o % 2) * JD * 128:(o % 2 + 1) * JD * 128]
                wchunk(3, o, wgt, wcg, JD, slotg, g3_done.get(o - 2, wb_free_mm), s,
                       "LWA", "WBA")
                wchunk(4, o, wut, wcu, JD, slotu, 0, s, "LWB", "WBB")
                p.dwait("tensor", "LWA")
                p.dwait("tensor", "LWB")
                if o == 0:
                    p.wait("tensor", "EV", ev_h2_all)
                for tb in range(TB):
                    i = o * TB + tb
                    pw1, pw2 = pb[i % 2], pb[2 + i % 2]
                    if i >= 2:
                        p.wait("tensor", "ES", es_g3[i - 2])
                        p.wait("tensor", "EV", ev_g3[i - 2])
                    for k in range(JD):
                        p.op("tensor", lambda e, k=k, tb=tb, slotg=slotg, pw1=pw1: e.matmul(
                            out=pw1[:], lhsT=slotg[:, k * 128:(k + 1) * 128],
                            rhs=h2st[:, k * S + tb * NT: k * S + (tb + 1) * NT],
                            start=(k == 0), stop=(k == JD - 1)))
                    mm = 0
                    for k in range(JD):
                        mm = p.op("tensor", lambda e, k=k, tb=tb, slotu=slotu, pw2=pw2:
                                  e.matmul(
                            out=pw2[:], lhsT=slotu[:, k * 128:(k + 1) * 128],
                            rhs=h2st[:, k * S + tb * NT: k * S + (tb + 1) * NT],
                            start=(k == 0), stop=(k == JD - 1)),
                            "MM" if k == JD - 1 else None)
                    p.wait("scalar", "MM", mm)
                    es_g3[i] = p.op("scalar", lambda e, i=i, pw1=pw1: e.activation(
                        out=abuf[:, (i % 2) * NT:(i % 2 + 1) * NT], in_=pw1[:],
                        func=AF.Silu), "ES")
                    p.wait("vector", "MM", mm)
                    p.wait("vector", "ES", es_g3[i])
                    if i == 0:
                        p.wait("vector", "MM", mbuf_free_mm)
                    ev_g3[i] = p.op("vector", lambda e, i=i, o=o, tb=tb, pw2=pw2:
                                    e.tensor_mul(
                        out=mbuf[:, o * S + tb * NT: o * S + (tb + 1) * NT],
                        in0=abuf[:, (i % 2) * NT:(i % 2 + 1) * NT], in1=pw2[:]), "EV")
                g3_done[o] = mm
            ev_c3_all = ev_g3[JH2 * TB - 1]
            mm_c3_done = p.tally["MM"]
            wb_free_mm = mm_c3_done
            g4_done = {}
            ev_g4 = {}
            for o in range(JD):
                lsem, wsem = ("LWA", "WBA") if o % 2 == 0 else ("LWB", "WBB")
                slot = (wbA if o % 2 == 0 else wbB)[:, 0:JH2 * 128]
                wchunk(5, o, wdt, wcd, JH2, slot, g4_done.get(o - 2, wb_free_mm), s,
                       lsem, wsem)
                p.dwait("tensor", lsem)
                if o == 0:
                    p.wait("tensor", "EV", ev_c3_all)
                for tb in range(TB):
                    i = o * TB + tb
                    pw = pb[i % 4]
                    if i >= 4:
                        p.wait("tensor", "EV", ev_g4[i - 4])
                    mm = 0
                    for k in range(JH2):
                        mm = p.op("tensor", lambda e, k=k, tb=tb, slot=slot, pw=pw: e.matmul(
                            out=pw[:], lhsT=slot[:, k * 128:(k + 1) * 128],
                            rhs=mbuf[:, k * S + tb * NT: k * S + (tb + 1) * NT],
                            start=(k == 0), stop=(k == JH2 - 1)),
                            "MM" if k == JH2 - 1 else None)
                    p.wait("vector", "MM", mm)
                    p.dwait("vector", "LX")
                    p.op("vector", lambda e, o=o, tb=tb, pw=pw: e.tensor_add(
                        out=xnew[:, o * S + tb * NT: o * S + (tb + 1) * NT],
                        in0=xnew[:, o * S + tb * NT: o * S + (tb + 1) * NT], in1=pw[:]))
                    ev_g4[i] = p.op("vector", lambda e, o=o, tb=tb: e.tensor_add(
                        out=xnew[:, o * S + tb * NT: o * S + (tb + 1) * NT],
                        in0=xnew[:, o * S + tb * NT: o * S + (tb + 1) * NT],
                        in1=xresC[:, o * S + tb * NT: o * S + (tb + 1) * NT]), "EV")
                    p.wait("sync", "EV", ev_g4[i])
                    st_y[(s, i)] = p.dma(
                        "sync", y[o * 128:(o + 1) * 128, s * S + tb * NT: s * S + (tb + 1) * NT],
                        xnew[:, o * S + tb * NT: o * S + (tb + 1) * NT], "ST")
                g4_done[o] = mm
            mm_c4_done = p.tally["MM"]
            wb_free_mm = mm_c4_done
            mbuf_free_mm = mm_c4_done
            ost_free = (mm_c3_done, 0)
            xnew_free_ev = p.tally["EV"]

        p.dwait("sync", "ST")
        p.dwait("sync", "WBA")
        p.dwait("sync", "WBB")


def build_l0attn():
    nc = bass.Bass("TRN2", target_bir_lowering=False, detect_race_conditions=RACE_DETECT)
    x = nc.dram_tensor("x", [D, T], F32, kind="ExternalInput").ap()
    y = nc.dram_tensor("y", [D, T], F32, kind="ExternalOutput").ap()
    with contextlib.ExitStack() as es:
        p = Prog(nc)
        p.make_sems(es, ALLSEMS)
        Cx = Ctx(nc, p, es)
        emit_l0attn(Cx, x, y, wp="", first=True)
        with nc.Block() as block:
            p.replay(block)
    return nc


def layout_l0attn(inp):
    f = np.float32
    tril = np.tril(np.ones((128, 128), f), -1)
    trili = np.tril(np.ones((128, 128), f), 0)
    return dict(
        wqt=np.ascontiguousarray(np.asarray(inp["L0_attn_Wq"], f)),
        wkt=np.ascontiguousarray(np.asarray(inp["L0_attn_Wk"], f)),
        wvt=np.ascontiguousarray(np.asarray(inp["L0_attn_Wv"], f)),
        wot=np.ascontiguousarray(np.asarray(inp["L0_attn_Wo"], f)),
        wgt=np.ascontiguousarray(np.asarray(inp["L0_attn_w_gate"], f)),
        wut=np.ascontiguousarray(np.asarray(inp["L0_attn_w_up"], f)),
        wdt=np.ascontiguousarray(np.asarray(inp["L0_attn_w_down"], f)),
        n1v=_colmajor(inp["L0_attn_norm1_w"]),
        n2v=_colmajor(inp["L0_attn_norm2_w"]),
        alv=_colmajor(inp["L0_attn_alpha_logit"]),
        bwv=_colmajor(np.asarray(inp["L0_attn_beta_w"], f)[:, 0]),
        bbv=np.asarray(inp["L0_attn_beta_b"], f).reshape(1, 1),
        msl=tril, mli=trili,
    )


def emit_l1attn(C, xin, xout, wp="a1", first=True):
    NH = 8
    DK = 128
    DC = 32
    DR = 32
    SCALE = 1.0 / float(np.sqrt(DK + DR))

    nc = C.nc
    p = C.p
    nm = (wp + "_") if wp else ""
    x, y = xin, xout
    wc96d = C.dram(nm + "wc96", [D, 96])
    wuqd = C.dram(nm + "wuq", [DC, D])
    wukd = C.dram(nm + "wuk", [DC, D])
    wuvd = C.dram(nm + "wuv", [DC, D])
    wqrd = C.dram(nm + "wqr", [DC, NH * DR])
    wod = C.dram(nm + "wo", [D, D])
    bc96d = C.dram(nm + "bc96", [96, 1])
    buqd = C.dram(nm + "buq", [128, NH])
    bukd = C.dram(nm + "buk", [128, NH])
    buvd = C.dram(nm + "buv", [1, D])
    bqrd = C.dram(nm + "bqr", [DR, NH])
    bod = C.dram(nm + "bo", [128, JD])
    osc = C.dram(nm + "osc", [NH, 128, T], BF16, "Internal")

    if True:
        xs = C.xs
        arena = C.arena
        hbt = C.hb
        hb_f = C.hb.bitcast(F32)
        cq = hbt[0:32, 0:T]
        xitb = hbt[:, 4096:8192]
        akt = hbt[:, 8192:9216]
        ostg = hbt[:, 9216:10240]
        ybuf = hb_f[:, 5120:6144]
        sq = hb_f[:, 6144:6656]
        vsq = hb_f[:, 8192:12288]
        wo = C.wbA
        wc96 = C.wbB[:, 0:JD * 96]
        wuq = C.wbB[0:DC, 768:768 + D]
        wuk = C.wbB[0:DC, 1792:1792 + D]
        wuv = C.wbB[0:DC, 2816:2816 + D]
        wqr = C.wbB[0:DC, 3840:3840 + NH * DR]
        bc96 = C.sb(nm + "bc96s", [96, 1], F32)
        buq = C.sb(nm + "buqs", [128, NH], F32)
        buk = C.sb(nm + "buks", [128, NH], F32)
        buv = C.abuf[0:1, 0:D]
        bqr = C.sb(nm + "bqrs", [DR, NH], F32)
        bo = C.sb(nm + "bos", [128, JD], F32)
        bvh = hb_f[:, 7296:7424]
        ones_r = C.ones_r
        ones_c = C.ones_c
        ones128 = hbt[:, 14848:14976]
        km33 = hb_f[0:33, 7168:7170]
        rowb33 = hb_f[0:33, 6656:7168]
        rmean, rvar, rt2, rA = C.rmean, C.rvar, C.rt2, C.rA
        ceps = C.ceps
        pb = C.pb
        rows = (rmean, rvar, rt2, rA, rt2)

        cslab = arena[0:96, 0:T]
        kr2 = arena[0:33, T:2 * T]
        qh = arena[:, 2 * T:3 * T]
        qr2 = arena[0:33, 3 * T:4 * T]
        kch = arena[:, 4 * T:5 * T]
        v_tm = arena[:, 5 * T:6 * T]
        arena_f = arena.bitcast(F32)
        ostripe = arena[:, 0:JD * 1024]
        xresC = arena_f[:, 4096:4096 + JD * 1024]

        C.init_common()

        p.wait("gpsimd", "MM", p.tally["MM"])
        p.dwait("gpsimd", "WBA")
        p.dwait("gpsimd", "WBB")
        p.dma("gpsimd", wc96.rearrange("p (k o) -> p k o", k=JD),
              wc96d.rearrange("(k p) o -> p k o", p=128), "LWA")
        p.dma("gpsimd", wuq, wuqd, "LWA")
        p.dma("gpsimd", wuk, wukd, "LWA")
        p.dma("gpsimd", wuv, wuvd, "LWA")
        p.dma("gpsimd", wqr, wqrd, "LWA")
        p.dma("gpsimd", wo[:].rearrange("p (k o) -> p k o", k=JD),
              wod.rearrange("(k p) o -> p k o", p=128), "LWA")
        for dst, srcd in ((bc96, bc96d), (buq, buqd), (buk, bukd), (buv, buvd),
                          (bqr, bqrd), (bo, bod)):
            p.dma("sync", dst[:], srcd, "LX")
        p.op("vector", lambda e: e.memset(ones128, 1.0))
        p.dwait("vector", "LX")
        p.op("vector", lambda e: e.tensor_scalar_mul(out=buq[:], in0=buq[:], scalar1=SCALE))
        p.op("vector", lambda e: e.tensor_scalar_mul(out=bqr[:], in0=bqr[:], scalar1=SCALE))
        p.op("vector", lambda e: e.drain())
        ev_init = p.op("vector", lambda e: e.nop(), "EV")
        p.dwait("tensor", "LWA")

        xs3 = xs[:].rearrange("p (j t) -> p j t", j=JD)
        xi3 = xitb.rearrange("p (j t) -> p j t", j=JD)
        vsq3 = vsq.rearrange("p (j t) -> p j t", j=JD)
        xs_free = (0, ev_init)
        psAB_free = ev_init
        psCD_free = ev_init
        prev_sq = 0
        pbfree = {i: 0 for i in range(8)}
        pbfree_es = {}
        ev_cp = {}
        for tb in range(T // NT):
            t0 = tb * NT
            p.wait("sync", "MM", xs_free[0])
            p.wait("sync", "EV", xs_free[1])
            lx_xs = p.dma("sync", xs3, x[:, t0:t0 + NT].rearrange("(j p) t -> p j t", p=128),
                          "LX")
            mm_ln, ev_ln = _norm_block(
                p, pb, xs3, vsq3, xi3, ones_c, ones_r, rows, ceps, 0, sq[:, 0:NT], None,
                True, prev_sq, psAB_free, psCD_free,
                waits_t=[("LX", lx_xs), ("EV", ev_init)], waits_s=[("LX", lx_xs)])
            prev_sq = mm_ln
            psAB_free = ev_ln
            psCD_free = ev_ln
            xs_free = (mm_ln, ev_ln)
            pw = pb[tb % 2]
            p.wait("tensor", "EV", max(pbfree[tb % 2], ev_ln))
            mm = 0
            for k in range(JD):
                mm = p.op("tensor", lambda e, k=k, pw=pw: e.matmul(
                    out=pw[0:96, :], lhsT=wc96[:, k * 96:(k + 1) * 96],
                    rhs=xitb[:, k * NT:(k + 1) * NT],
                    start=(k == 0), stop=(k == JD - 1)), "MM" if k == JD - 1 else None)
            p.wait("vector", "MM", mm)
            ev = p.op("vector", lambda e, tb=tb, pw=pw: e.tensor_scalar_add(
                out=cslab[:, tb * NT:(tb + 1) * NT].to_broadcast([96, NT]) if False
                else cslab[:, tb * NT:(tb + 1) * NT],
                in0=pw[0:96, :], scalar1=bc96[:]), "EV")
            pbfree[tb % 2] = ev
            ev_cp[tb] = ev
        ev_cp_all = ev_cp[T // NT - 1]
        pbfree[4] = psAB_free
        pbfree[5] = psAB_free
        pbfree[6] = psCD_free
        pbfree[7] = psCD_free
        p.wait("sync", "EV", ev_cp_all)
        p.dma("sync", kr2[0:32, :], cslab[32:64, :], "LX")
        p.dma("sync", cq, cslab[64:96, :], "LX")
        p.dwait("vector", "LX")
        ev_kr = p.op("vector", lambda e: e.memset(kr2[32:33, :], 1.0), "EV")

        st_osc = {}
        for h in range(NH):
            hs = slice(h * 128, (h + 1) * 128)
            p.wait("tensor", "EV", max(pbfree[6], ev_kr))
            mm_bv = p.op("tensor", lambda e, h=h: e.matmul(
                out=pb[6][:, 0:128], lhsT=ones_r[:], rhs=buv[0:1, h * 128:(h + 1) * 128],
                start=True, stop=True), "MM")
            p.wait("vector", "MM", mm_bv)
            ev_bvh = p.op("vector", lambda e: e.tensor_copy(out=bvh, in_=pb[6][:, 0:128]),
                          "EV")
            pbfree[6] = ev_bvh
            p.op("vector", lambda e: e.memset(km33[32:33, 0:1], 0.0))
            es_k = {}
            for kb in range(T // NT):
                pw = pb[kb % 2]
                p.wait("tensor", "ES", pbfree_es.get(kb % 2, 0))
                p.wait("tensor", "EV", pbfree[kb % 2])
                mm = p.op("tensor", lambda e, h=h, kb=kb, pw=pw: e.matmul(
                    out=pw[:], lhsT=wuk[:, h * 128:(h + 1) * 128],
                    rhs=cslab[0:32, kb * NT:(kb + 1) * NT], start=True, stop=True), "MM")
                p.wait("scalar", "MM", mm)
                es1 = p.op("scalar", lambda e, h=h, kb=kb, pw=pw: e.activation(
                    out=kch[:, kb * NT:(kb + 1) * NT], in_=pw[:], func=AF.Identity,
                    bias=buk[:, h:h + 1]), "ES")
                pbfree_es[kb % 2] = es1
                es2 = p.op("scalar", lambda e, kb=kb: e.activation(
                    out=sq[:, 0:NT], in_=kch[:, kb * NT:(kb + 1) * NT], func=AF.Square),
                    "ES")
                p.wait("tensor", "ES", es2)
                p.wait("tensor", "EV", pbfree[4])
                mmq = p.op("tensor", lambda e: e.matmul(
                    out=pb[4][32:33, :], lhsT=ones_c[:], rhs=sq[:, 0:NT],
                    start=True, stop=True), "MM")
                p.wait("vector", "MM", mmq)
                p.op("vector", lambda e: e.tensor_reduce(
                    out=km33[32:33, 1:2], in_=pb[4][32:33, :], axis=AX.X, op=ALU.max))
                ev = p.op("vector", lambda e: e.tensor_tensor(
                    out=km33[32:33, 0:1], in0=km33[32:33, 0:1], in1=km33[32:33, 1:2],
                    op=ALU.max), "EV")
                pbfree[4] = ev
            p.wait("scalar", "EV", pbfree[4])
            es_km = p.op("scalar", lambda e: e.activation(
                out=km33[32:33, 0:1], in_=km33[32:33, 0:1], func=AF.Sqrt), "ES")
            for kt in range(T // 128):
                pw = pb[kt % 2]
                p.wait("tensor", "ES", pbfree_es.get(kt % 2, 0))
                p.wait("tensor", "EV", pbfree[kt % 2])
                mm = p.op("tensor", lambda e, h=h, kt=kt, pw=pw: e.matmul(
                    out=pw[:, 0:128], lhsT=cslab[0:32, kt * 128:(kt + 1) * 128],
                    rhs=wuv[:, h * 128:(h + 1) * 128], start=True, stop=True), "MM")
                p.wait("vector", "MM", mm)
                p.wait("vector", "EV", ev_bvh)
                ev = p.op("vector", lambda e, kt=kt, pw=pw: e.tensor_add(
                    out=v_tm[:, kt * 128:(kt + 1) * 128], in0=pw[:, 0:128], in1=bvh),
                    "EV")
                pbfree[kt % 2] = ev
                pbfree_es[kt % 2] = 0
            p.wait("vector", "ES", es_km)
            for qb in range(T // NT):
                pw = pb[qb % 2]
                pw2 = pb[2 + qb % 2]
                p.wait("tensor", "ES", pbfree_es.get(qb % 2, 0))
                p.wait("tensor", "EV", pbfree[qb % 2])
                mm1 = p.op("tensor", lambda e, h=h, qb=qb, pw=pw: e.matmul(
                    out=pw[:], lhsT=wuq[:, h * 128:(h + 1) * 128],
                    rhs=cq[:, qb * NT:(qb + 1) * NT], start=True, stop=True), "MM")
                p.wait("scalar", "MM", mm1)
                es1 = p.op("scalar", lambda e, h=h, qb=qb, pw=pw: e.activation(
                    out=qh[:, qb * NT:(qb + 1) * NT], in_=pw[:], func=AF.Identity,
                    scale=SCALE, bias=buq[:, h:h + 1]), "ES")
                pbfree_es[qb % 2] = es1
                es2 = p.op("scalar", lambda e, qb=qb: e.activation(
                    out=sq[:, 0:NT], in_=qh[:, qb * NT:(qb + 1) * NT], func=AF.Square), "ES")
                p.wait("tensor", "ES", es2)
                p.wait("tensor", "EV", pbfree[4])
                p.op("tensor", lambda e: e.matmul(
                    out=pb[4][32:33, :], lhsT=ones_c[:], rhs=sq[:, 0:NT],
                    start=True, stop=False))
                p.wait("tensor", "ES", pbfree_es.get(2 + qb % 2, 0))
                p.wait("tensor", "EV", pbfree[2 + qb % 2])
                mm2 = p.op("tensor", lambda e, h=h, qb=qb, pw2=pw2: e.matmul(
                    out=pw2[0:32, :], lhsT=wqr[:, h * DR:(h + 1) * DR],
                    rhs=cq[:, qb * NT:(qb + 1) * NT], start=True, stop=True), "MM")
                p.wait("scalar", "MM", mm2)
                es3 = p.op("scalar", lambda e, h=h, qb=qb, pw2=pw2: e.activation(
                    out=qr2[0:32, qb * NT:(qb + 1) * NT], in_=pw2[0:32, :], func=AF.Identity,
                    scale=SCALE, bias=bqr[:, h:h + 1]), "ES")
                pbfree_es[2 + qb % 2] = es3
                es4 = p.op("scalar", lambda e, qb=qb: e.activation(
                    out=sq[0:32, 0:NT], in_=qr2[0:32, qb * NT:(qb + 1) * NT],
                    func=AF.Square), "ES")
                p.wait("tensor", "ES", es4)
                mm3 = p.op("tensor", lambda e: e.matmul(
                    out=pb[4][32:33, :], lhsT=ones_c[0:32, :], rhs=sq[0:32, 0:NT],
                    start=False, stop=True), "MM")
                p.wait("scalar", "MM", mm3)
                es5 = p.op("scalar", lambda e: e.activation(
                    out=rowb33[32:33, :], in_=pb[4][32:33, :], func=AF.Sqrt), "ES")
                p.wait("vector", "ES", es5)
                ev = p.op("vector", lambda e, qb=qb: e.tensor_scalar(
                    out=qr2[32:33, qb * NT:(qb + 1) * NT], in0=rowb33[32:33, :],
                    scalar1=km33[32:33, 0:1], scalar2=-1.0,
                    op0=ALU.mult, op1=ALU.mult), "EV")
                pbfree[4] = ev
            for qb in range(T // NT):
                i_av = 0
                p.wait("tensor", "EV", max(pbfree[2], pbfree[3]))
                es_exp = {}
                mm_av = {}
                for kt in range(T // 128):
                    pw = pb[kt % 2]
                    p.wait("tensor", "ES", pbfree_es.get(kt % 2, 0))
                    p.wait("tensor", "EV", pbfree[kt % 2])
                    if kt >= 2:
                        p.wait("tensor", "MM", mm_av[kt - 2])
                    p.op("tensor", lambda e, h=h, kt=kt, qb=qb, pw=pw: e.matmul(
                        out=pw[:], lhsT=kch[:, kt * 128:(kt + 1) * 128],
                        rhs=qh[:, qb * NT:(qb + 1) * NT], start=True, stop=False))
                    mm = p.op("tensor", lambda e, h=h, kt=kt, qb=qb, pw=pw: e.matmul(
                        out=pw[:], lhsT=kr2[:, kt * 128:(kt + 1) * 128],
                        rhs=qr2[:, qb * NT:(qb + 1) * NT], start=False, stop=True), "MM")
                    p.wait("scalar", "MM", mm)
                    ese = p.op("scalar", lambda e, kt=kt, pw=pw: e.activation(
                        out=akt[:, (kt % 2) * NT:(kt % 2 + 1) * NT], in_=pw[:],
                        func=AF.Exp), "ES")
                    es_exp[kt] = ese
                    pbfree_es[kt % 2] = ese
                    pbfree[kt % 2] = 0
                    p.wait("tensor", "ES", ese)
                    p.op("tensor", lambda e, kt=kt: e.matmul(
                        out=pb[2][:], lhsT=v_tm[:, kt * 128:(kt + 1) * 128],
                        rhs=akt[:, (kt % 2) * NT:(kt % 2 + 1) * NT],
                        start=(kt == 0), stop=(kt == T // 128 - 1)))
                    mm_av[kt] = p.op("tensor", lambda e, kt=kt: e.matmul(
                        out=pb[3][:], lhsT=ones128,
                        rhs=akt[:, (kt % 2) * NT:(kt % 2 + 1) * NT],
                        start=(kt == 0), stop=(kt == T // 128 - 1)), "MM")
                mm_done = mm_av[T // 128 - 1]
                p.wait("vector", "MM", mm_done)
                p.dwait("vector", "ST")
                p.op("vector", lambda e, qb=qb: e.reciprocal(
                    out=ybuf[:, 0:NT], in_=pb[3][:]))
                ev_o = p.op("vector", lambda e, qb=qb: e.tensor_mul(
                    out=ostg[:, (qb % 2) * NT:(qb % 2 + 1) * NT], in0=pb[2][:],
                    in1=ybuf[:, 0:NT]), "EV")
                pbfree[2] = ev_o
                pbfree[3] = ev_o
                p.wait("sync", "EV", ev_o)
                st_osc[(h, qb)] = p.dma(
                    "sync", osc[h, :, qb * NT:(qb + 1) * NT],
                    ostg[:, (qb % 2) * NT:(qb % 2 + 1) * NT], "ST")
        mm_attn_done = p.tally["MM"]

        ost_free = (mm_attn_done, 0)
        ev_g = {}
        for s in range(4):
            S = 1024
            p.wait("sync", "MM", ost_free[0])
            p.wait("sync", "EV", ost_free[1])
            p.dwait("sync", "ST")
            lo = p.dma("sync", ostripe.rearrange("p (j t) -> p j t", j=JD),
                       osc[:, :, s * S:(s + 1) * S].rearrange("j p t -> p j t"), "LX")
            lxr = p.dma("sync", xresC.rearrange("p (j t) -> p j t", j=JD),
                        x[:, s * S:(s + 1) * S].rearrange("(j p) t -> p j t", p=128), "LX")
            for o in range(JD):
                for tb in range(2):
                    i = o * 2 + tb
                    pw = pb[4 + i % 4]
                    if i == 0:
                        p.dwait("tensor", "LX")
                    if i >= 4:
                        p.wait("tensor", "EV", ev_g[(s, i - 4)])
                    elif s > 0:
                        p.wait("tensor", "EV", ev_g[(s - 1, 12 + i)])
                    else:
                        p.wait("tensor", "EV", pbfree[4 + i % 4])
                    mm = 0
                    for k in range(JD):
                        mm = p.op("tensor", lambda e, k=k, o=o, tb=tb, pw=pw, s=s: e.matmul(
                            out=pw[:], lhsT=wo[:, k * 1024 + o * 128: k * 1024 + (o + 1) * 128],
                            rhs=ostripe[:, k * S + tb * NT: k * S + (tb + 1) * NT],
                            start=(k == 0), stop=(k == JD - 1)),
                            "MM" if k == JD - 1 else None)
                    p.wait("vector", "MM", mm)
                    p.dwait("vector", "LX")
                    p.dwait("vector", "ST")
                    ev = p.op("vector", lambda e, o=o, tb=tb, pw=pw, i=i: e.scalar_tensor_tensor(
                        out=ybuf[:, (i % 2) * NT:(i % 2 + 1) * NT], in0=pw[:],
                        scalar=bo[:, o:o + 1],
                        in1=xresC[:, o * S + tb * NT: o * S + (tb + 1) * NT],
                        op0=ALU.add, op1=ALU.add), "EV")
                    ev_g[(s, i)] = ev
                    p.wait("sync", "EV", ev)
                    p.dma("sync", y[o * 128:(o + 1) * 128, s * S + tb * NT: s * S + (tb + 1) * NT],
                          ybuf[:, (i % 2) * NT:(i % 2 + 1) * NT], "ST")
            ost_free = (p.tally["MM"], p.tally["EV"])

        p.dwait("sync", "ST")


def build_l1attn():
    nc = bass.Bass("TRN2", target_bir_lowering=False, detect_race_conditions=RACE_DETECT)
    x = nc.dram_tensor("x", [D, T], F32, kind="ExternalInput").ap()
    y = nc.dram_tensor("y", [D, T], F32, kind="ExternalOutput").ap()
    with contextlib.ExitStack() as es:
        p = Prog(nc)
        p.make_sems(es, ALLSEMS)
        Cx = Ctx(nc, p, es)
        emit_l1attn(Cx, x, y, wp="", first=True)
        with nc.Block() as block:
            p.replay(block)
    return nc


def layout_l1attn(inp):
    f = np.float32
    g = lambda k: np.asarray(inp[k], f)
    wc96 = np.hstack([g("L1_attn_W_DKV"), g("L1_attn_W_KR"), g("L1_attn_W_DQ")])
    bc96 = np.concatenate([g("L1_attn_b_DKV"), g("L1_attn_b_KR"), g("L1_attn_b_DQ")])
    return dict(
        wc96=np.ascontiguousarray(wc96),
        wuq=np.ascontiguousarray(g("L1_attn_W_UQ")),
        wuk=np.ascontiguousarray(g("L1_attn_W_UK")),
        wuv=np.ascontiguousarray(g("L1_attn_W_UV")),
        wqr=np.ascontiguousarray(g("L1_attn_W_QR")),
        wo=np.ascontiguousarray(g("L1_attn_W_O")),
        bc96=np.ascontiguousarray(bc96.reshape(96, 1)),
        buq=np.ascontiguousarray(g("L1_attn_b_UQ").reshape(8, 128).T),
        buk=np.ascontiguousarray(g("L1_attn_b_UK").reshape(8, 128).T),
        buv=np.ascontiguousarray(g("L1_attn_b_UV").reshape(1, 1024)),
        bqr=np.ascontiguousarray(g("L1_attn_b_QR").reshape(8, 32).T),
        bo=_colmajor(g("L1_attn_b_O")),
    )


def emit_l1ffn(C, xin, xout, wp="f1", first=True):
    NE = 15
    F = 1024
    FS = 2048
    S = 1024
    NS = T // S
    TB = S // NT

    nc = C.nc
    p = C.p
    nm = (wp + "_") if wp else ""
    x, y = xin, xout
    gwd = C.dram(nm + "gw", [128, JD * NE])
    s1d = C.dram(nm + "s1", [D, 2 * FS])
    s2d = C.dram(nm + "s2", [FS, D])
    f1d = C.dram(nm + "f1", [NE, D, 2 * F])
    f2d = C.dram(nm + "f2", [NE, F, D])
    fb1d = C.dram(nm + "fb1", [128, NE * 16])
    fb2d = C.dram(nm + "fb2", [1, NE * D])
    gmd = C.dram(nm + "gm", [128, NE * 8])
    sb1d = C.dram(nm + "sb1", [128, 32])
    sgmd = C.dram(nm + "sgm", [128, 16])
    sb2d = C.dram(nm + "sb2", [1, D])
    idd = C.dram(nm + "idd", [128, 128])
    wcs1 = C.dram(nm + "wcs1", [32, 128, JD * 128], BF16, "Internal")
    wcs2 = C.dram(nm + "wcs2", [16, 128, D], BF16, "Internal")
    wcf1 = C.dram(nm + "wcf1", [NE * 16, 128, JD * 128], BF16, "Internal")
    wcf2 = C.dram(nm + "wcf2", [NE * 8, 128, D], BF16, "Internal")

    if True:
        xs = C.xs
        xln = C.xln
        arena_f = C.arena.bitcast(F32)
        hbt = C.hb
        hb_f = C.hb.bitcast(F32)
        yacc = arena_f[:, 0:8 * D]
        ybuf = arena_f[:, 8192:9216]
        vsq = arena_f[:, 9216:13312]
        ssb = arena_f[:, 13312:14336]
        sb2r = arena_f[0:1, 14336:15360]
        ashr = hbt[:, 0:16 * S]
        aexp = hbt[:, 16 * S:24 * S]
        wbA = C.wbA
        wbB = C.wbB
        abuf = C.abuf
        ebias = C.abuf.bitcast(BF16)[0:1, 0:D]
        wbA_f = C.wbA.bitcast(F32)
        identb = wbA_f[:, 3072:3200]
        gatew = wbA_f[:, 3200:3320]
        fb1 = wbA_f[:, 3328:3568]
        gmc = wbA_f[:, 3584:3704]
        comb = arena_f[:, 15360:15480]
        et = arena_f[:, 15480:15600]
        etd = arena_f[:, 15600:15720]
        lgs = arena_f[:, 15720:15840]
        scr = arena_f[:, 15840:15968]
        r1 = arena_f[:, 15968:15976]
        r2 = arena_f[:, 15976:15984]
        combb = C.arena[:, 31968:32088]
        ones_rb = C.sb(nm + "ones_rb", [1, 128], BF16)
        sb1 = C.sb(nm + "sb1c", [128, 32], F32)
        sgm = C.sb(nm + "sgmc", [128, 16], F32)
        ones_r = C.ones_r
        ones_c = C.ones_c
        rmean, rvar, rt2, rA = C.rmean, C.rvar, C.rt2, C.rA
        ceps = C.ceps
        pb = C.pb
        rows = (rmean, rvar, rt2, rA, rt2)

        C.init_common()
        wb_after = {}

        p.wait("gpsimd", "MM", p.tally["MM"])
        p.dwait("gpsimd", "WBA")
        p.dwait("gpsimd", "WBB")
        p.dma("sync", gatew, gwd, "LX")
        p.dma("sync", fb1, fb1d, "LX")
        p.dma("sync", gmc, gmd, "LX")
        for dst, srcd in ((sb1[:], sb1d), (sgm[:], sgmd), (sb2r, sb2d), (identb, idd)):
            p.dma("sync", dst, srcd, "LX")
        p.op("vector", lambda e: e.memset(ones_rb[:], 1.0))
        p.dwait("vector", "LX")
        ev_init = p.op("vector", lambda e: e.nop(), "EV")
        p.wait("tensor", "EV", ev_init)
        p.dwait("tensor", "LWA")
        mm_ssb = 0
        for hblk in range(2):
            mm_ssb = p.op("tensor", lambda e, hblk=hblk: e.matmul(
                out=pb[6 + hblk][:], lhsT=ones_r[:], rhs=sb2r[0:1, hblk * NT:(hblk + 1) * NT],
                start=True, stop=True), "MM")
            p.wait("vector", "MM", mm_ssb)
            p.op("vector", lambda e, hblk=hblk: e.tensor_copy(
                out=ssb[:, hblk * NT:(hblk + 1) * NT], in_=pb[6 + hblk][:]))
        ev_ssbcp = p.op("vector", lambda e: e.nop(), "EV")
        p.wait("tensor", "EV", ev_ssbcp)
        ev_ssb = p.op("vector", lambda e: e.nop(), "EV")

        def wchunk(tag, idx, o, src2d, wc, wcrow, ktiles, slot, free_mm, s, lsem, wsem):
            if free_mm:
                p.wait("gpsimd", "MM", free_mm)
            if s == 0 and first:
                if (tag, idx, o - 2) in wb_after or (tag, idx, o - 4) in wb_after:
                    p.dwait("gpsimd", wsem)
                if ktiles == 1:
                    view3 = slot[:, 0:src2d.shape[1]]
                    srcv = src2d[o * 128:(o + 1) * 128, :]
                else:
                    view3 = slot.rearrange("p (k o) -> p k o", k=ktiles)
                    srcv = src2d[:, o * 128:(o + 1) * 128].rearrange("(k p) o -> p k o", p=128)
                p.dma("gpsimd", view3, srcv, lsem)
                p.dwait("sync", lsem)
                wb_after[(tag, idx, o)] = p.dma("sync", wc[wcrow], slot, wsem)
            else:
                p.dwait("gpsimd", wsem)
                p.dma("gpsimd", slot, wc[wcrow], lsem)

        xlnf = hb_f[:, 8192:8192 + JD * NT]
        xlnf3 = xlnf.rearrange("p (j t) -> p j t", j=JD)
        xs3 = xs[:].rearrange("p (j t) -> p j t", j=JD)
        vsq3 = vsq.rearrange("p (j t) -> p j t", j=JD)
        xln3 = xln[:].rearrange("p (j t) -> p j t", j=JD)
        xs_free = (0, ev_ssb)
        xlnf_free_mm_local = [0]
        psAB_free = ev_ssb
        psCD_free = ev_ssb
        prev_sq = 0
        wbfree_mm = {0: 0, 1: 0}
        pbfree = {i: 0 for i in range(8)}
        pbfree_es = {i: 0 for i in range(8)}
        xln_free_mm = 0
        xlnf_free_mm = 0
        yaccfree_mm = 0
        fslot_free_mm = 0
        st_y = 0
        for s in range(NS):
            ev_xln = {}
            for tb in range(TB):
                t0 = s * S + tb * NT
                p.wait("sync", "MM", xs_free[0])
                p.wait("sync", "EV", xs_free[1])
                lx_xs = p.dma("sync", xs3,
                              x[:, t0:t0 + NT].rearrange("(j p) t -> p j t", p=128), "LX")
                dst3 = xln3[:, :, tb * NT:(tb + 1) * NT]
                mm_ln, ev_ln = _norm_block(
                    p, pb, xs3, vsq3, dst3, ones_c, ones_r, rows, ceps, 0,
                    abuf[:, 0:NT], None, True, prev_sq, psAB_free, psCD_free,
                    waits_t=[("LX", lx_xs), ("EV", ev_init)],
                    waits_s=[("LX", lx_xs)],
                    waits_v=[("MM", max(xln_free_mm, xlnf_free_mm_local[0]))], dst2=xlnf3)
                prev_sq = mm_ln
                psAB_free = ev_ln
                psCD_free = ev_ln
                ev_xln[tb] = ev_ln
                pw = pb[2 + tb % 2]
                p.wait("tensor", "EV", max(pbfree[2 + tb % 2], ev_ln))
                p.wait("tensor", "ES", pbfree_es[2 + tb % 2])
                pbfree_es[2 + tb % 2] = 0
                mm = 0
                for t4 in range(4):
                    for k in range(JD):
                        mm = p.op("tensor", lambda e, t4=t4, k=k, pw=pw: e.matmul(
                            out=pw[:, t4 * 16: t4 * 16 + NE],
                            lhsT=xlnf[:, k * NT + t4 * 128: k * NT + (t4 + 1) * 128],
                            rhs=gatew[:, k * NE:(k + 1) * NE],
                            start=(k == 0), stop=(k == JD - 1)),
                            "MM" if k == JD - 1 else None)
                p.wait("vector", "MM", mm)
                ev = 0
                for t4 in range(4):
                    tt = tb * 4 + t4
                    ev = p.op("vector", lambda e, t4=t4, tt=tt, pw=pw: e.tensor_copy(
                        out=lgs[:, tt * NE:(tt + 1) * NE],
                        in_=pw[:, t4 * 16: t4 * 16 + NE]), "EV")
                pbfree[2 + tb % 2] = ev
                xlnf_free_mm_local[0] = mm
                xs_free = (mm_ln, ev_ln)
            ev_xln_all = ev_xln[TB - 1]
            xlnf_free_mm = xlnf_free_mm_local[0]

            for tt in range(8):
                p.op("vector", lambda e, tt=tt: e.tensor_reduce(
                    out=r1[:, tt:tt + 1], in_=lgs[:, tt * NE:(tt + 1) * NE], axis=AX.X,
                    op=ALU.max))
            _evv = p.op("vector", lambda e: e.nop(), "EV")
            p.wait("vector", "EV", _evv)
            ev_d = 0
            for tt in range(8):
                ev_d = p.op("vector", lambda e, tt=tt: e.tensor_single_scalar(
                    out=etd[:, tt * NE:(tt + 1) * NE], in_=lgs[:, tt * NE:(tt + 1) * NE],
                    scalar=r1[:, tt:tt + 1], op=ALU.subtract), "EV")
            p.wait("scalar", "EV", ev_d)
            es_exp_all = 0
            for tt in range(8):
                es_exp_all = p.op("scalar", lambda e, tt=tt: e.activation(
                    out=et[:, tt * NE:(tt + 1) * NE], in_=etd[:, tt * NE:(tt + 1) * NE],
                    func=AF.Exp), "ES")
            def vedge():
                evv = p.op("vector", lambda e: e.nop(), "EV")
                p.wait("vector", "EV", evv)

            p.wait("vector", "ES", es_exp_all)
            for rnd in range(3):
                for tt in range(8):
                    src = et[:, tt * NE:(tt + 1) * NE] if rnd == 0 else comb[:, tt * NE:(tt + 1) * NE]
                    dstr = (r1 if rnd < 2 else r2)
                    p.op("vector", lambda e, tt=tt, src=src, dstr=dstr: e.tensor_reduce(
                        out=dstr[:, tt:tt + 1], in_=src, axis=AX.X, op=ALU.max))
                if rnd == 2:
                    break
                vedge()
                for tt in range(8):
                    src = et[:, tt * NE:(tt + 1) * NE] if rnd == 0 else comb[:, tt * NE:(tt + 1) * NE]
                    p.op("vector", lambda e, tt=tt, src=src: e.tensor_single_scalar(
                        out=scr[:, tt * 16:tt * 16 + NE], in_=src,
                        scalar=r1[:, tt:tt + 1], op=ALU.subtract))
                for tt in range(8):
                    p.op("vector", lambda e, tt=tt: e.tensor_single_scalar(
                        out=scr[:, tt * 16:tt * 16 + NE], in_=scr[:, tt * 16:tt * 16 + NE],
                        scalar=0.0, op=ALU.is_ge))
                for tt in range(8):
                    src = et[:, tt * NE:(tt + 1) * NE] if rnd == 0 else comb[:, tt * NE:(tt + 1) * NE]
                    p.op("vector", lambda e, tt=tt, src=src: e.scalar_tensor_tensor(
                        out=comb[:, tt * NE:(tt + 1) * NE], in0=scr[:, tt * 16:tt * 16 + NE],
                        scalar=-1e30, in1=src, op0=ALU.mult, op1=ALU.add))
            vedge()
            ev_comb = {}
            for tt in range(8):
                p.op("vector", lambda e, tt=tt: e.tensor_single_scalar(
                    out=scr[:, tt * 16:tt * 16 + NE], in_=et[:, tt * NE:(tt + 1) * NE],
                    scalar=r2[:, tt:tt + 1], op=ALU.subtract))
            for tt in range(8):
                p.op("vector", lambda e, tt=tt: e.tensor_single_scalar(
                    out=scr[:, tt * 16:tt * 16 + NE], in_=scr[:, tt * 16:tt * 16 + NE],
                    scalar=0.0, op=ALU.is_ge))
            for tt in range(8):
                p.op("vector", lambda e, tt=tt: e.tensor_mul(
                    out=comb[:, tt * NE:(tt + 1) * NE], in0=et[:, tt * NE:(tt + 1) * NE],
                    in1=scr[:, tt * 16:tt * 16 + NE]))
            for tt in range(8):
                p.op("vector", lambda e, tt=tt: e.tensor_reduce(
                    out=r1[:, tt:tt + 1], in_=comb[:, tt * NE:(tt + 1) * NE], axis=AX.X,
                    op=ALU.add))
            vedge()
            for tt in range(8):
                p.op("vector", lambda e, tt=tt: e.reciprocal(
                    out=r2[:, tt:tt + 1], in_=r1[:, tt:tt + 1]))
            vedge()
            for tt in range(8):
                p.op("vector", lambda e, tt=tt: e.tensor_single_scalar(
                    out=comb[:, tt * NE:(tt + 1) * NE], in_=comb[:, tt * NE:(tt + 1) * NE],
                    scalar=r2[:, tt:tt + 1], op=ALU.mult))
                ev_comb[tt] = p.op("vector", lambda e, tt=tt: e.tensor_copy(
                    out=combb[:, tt * NE:(tt + 1) * NE],
                    in_=comb[:, tt * NE:(tt + 1) * NE]), "EV")
            ev_comb_all = ev_comb[7]

            p.wait("vector", "MM", yaccfree_mm)
            for tt in range(8):
                for dblk in range(2):
                    p.op("vector", lambda e, tt=tt, dblk=dblk: e.tensor_copy(
                        out=yacc[:, tt * D + dblk * NT: tt * D + (dblk + 1) * NT],
                        in_=ssb[:, dblk * NT:(dblk + 1) * NT]))
            ev_yinit = p.op("vector", lambda e: e.nop(), "EV")

            g1done = {}
            es_after = {}
            ev_after = {}
            for o in range(16):
                slotA = wbA[:, (o % 2) * 1024:(o % 2 + 1) * 1024]
                slotB = wbB[:, (o % 2) * 1024:(o % 2 + 1) * 1024]
                wchunk("s1", 0, o, s1d, wcs1, o, JD, slotA,
                       g1done.get(o - 2, wbfree_mm[0]), s, "LWA", "WBA")
                wchunk("s1", 0, 16 + o, s1d, wcs1, 16 + o, JD, slotB, 0, s, "LWB", "WBB")
                p.dwait("tensor", "LWA")
                p.dwait("tensor", "LWB")
                if o == 0:
                    p.wait("tensor", "EV", ev_xln_all)
                for tb in range(TB):
                    i = o * TB + tb
                    pw1, pw2 = pb[2 + i % 2], pb[4 + i % 2]
                    if i >= 2:
                        p.wait("tensor", "ES", es_after[i - 2])
                        p.wait("tensor", "EV", ev_after[i - 2])
                    else:
                        p.wait("tensor", "EV", max(pbfree[2 + i % 2], pbfree[4 + i % 2]))
                    for k in range(JD):
                        p.op("tensor", lambda e, k=k, tb=tb, slotA=slotA, pw1=pw1: e.matmul(
                            out=pw1[:], lhsT=slotA[:, k * 128:(k + 1) * 128],
                            rhs=xln[:, k * S + tb * NT: k * S + (tb + 1) * NT],
                            start=(k == 0), stop=(k == JD - 1)))
                    mm = 0
                    for k in range(JD):
                        mm = p.op("tensor", lambda e, k=k, tb=tb, slotB=slotB, pw2=pw2:
                                  e.matmul(
                            out=pw2[:], lhsT=slotB[:, k * 128:(k + 1) * 128],
                            rhs=xln[:, k * S + tb * NT: k * S + (tb + 1) * NT],
                            start=(k == 0), stop=(k == JD - 1)),
                            "MM" if k == JD - 1 else None)
                    p.wait("scalar", "MM", mm)
                    es_g = p.op("scalar", lambda e, i=i, o=o, pw2=pw2: e.activation(
                        out=abuf[:, (i % 2) * NT:(i % 2 + 1) * NT], in_=pw2[:],
                        func=AF.Gelu, bias=sb1[:, 16 + o:17 + o]), "ES")
                    es_after[i] = es_g
                    p.wait("vector", "MM", mm)
                    p.wait("vector", "ES", es_g)
                    p.op("vector", lambda e, i=i, o=o, pw1=pw1: e.scalar_tensor_tensor(
                        out=abuf[:, (i % 2) * NT:(i % 2 + 1) * NT], in0=pw1[:],
                        scalar=sb1[:, o:o + 1], in1=abuf[:, (i % 2) * NT:(i % 2 + 1) * NT],
                        op0=ALU.add, op1=ALU.mult))
                    ev = p.op("vector", lambda e, i=i, o=o, tb=tb: e.tensor_single_scalar(
                        out=ashr[:, o * S + tb * NT: o * S + (tb + 1) * NT],
                        in_=abuf[:, (i % 2) * NT:(i % 2 + 1) * NT],
                        scalar=sgm[:, o:o + 1], op=ALU.mult), "EV")
                    ev_after[i] = ev
                g1done[o] = mm
            wbfree_mm[0] = p.tally["MM"]
            wbfree_mm[1] = p.tally["MM"]
            ev_ashr_all = p.tally["EV"]
            pbfree[2] = ev_after[16 * TB - 2]
            pbfree[3] = ev_after[16 * TB - 1]
            pbfree_es[2] = es_after[16 * TB - 2]
            pbfree_es[3] = es_after[16 * TB - 1]
            p.wait("tensor", "ES", es_after[16 * TB - 1])
            ev_prev_grp = 0
            mm_sl = {}
            for grp in range(2):
                p.wait("tensor", "EV", ev_ashr_all if grp == 0 else ev_prev_grp)
                for ft in range(16):
                    slot = (wbA if ft % 2 == 0 else wbB)[:, 0:1024]
                    lsem, wsem = ("LWA", "WBA") if ft % 2 == 0 else ("LWB", "WBB")
                    if grp == 0:
                        wchunk("s2", 0, ft, s2d, wcs2, ft, 1, slot,
                               mm_sl.get((0, ft - 2), wbfree_mm[ft % 2]), s, lsem, wsem)
                    else:
                        p.wait("gpsimd", "MM", mm_sl.get((1, ft - 2), mm_sl[(0, 15)]))
                        p.dwait("gpsimd", wsem)
                        p.dma("gpsimd", slot, wcs2[ft], lsem)
                    p.dwait("tensor", lsem)
                    mm = 0
                    for tile in range(8):
                        tt = grp * 4 + tile // 2
                        dblk = tile % 2
                        mm = p.op("tensor", lambda e, ft=ft, tile=tile, tt=tt, dblk=dblk,
                                  slot=slot: e.matmul(
                            out=pb[tile][:],
                            lhsT=ashr[:, ft * S + tt * 128: ft * S + (tt + 1) * 128],
                            rhs=slot[:, dblk * NT:(dblk + 1) * NT],
                            start=(ft == 0), stop=(ft == 15)),
                            "MM" if tile == 7 else None)
                    mm_sl[(grp, ft)] = mm
                p.wait("vector", "MM", mm_sl[(grp, 15)])
                ev = 0
                for tile in range(8):
                    tt = grp * 4 + tile // 2
                    dblk = tile % 2
                    ev = p.op("vector", lambda e, tile=tile, tt=tt, dblk=dblk: e.tensor_add(
                        out=yacc[:, tt * D + dblk * NT: tt * D + (dblk + 1) * NT],
                        in0=yacc[:, tt * D + dblk * NT: tt * D + (dblk + 1) * NT],
                        in1=pb[tile][:]), "EV")
                    pbfree[tile] = ev
                ev_prev_grp = ev
            wbfree_mm[0] = p.tally["MM"]
            wbfree_mm[1] = p.tally["MM"]

            fslot = [hbt[:, 24576 + f_ * 1024: 24576 + (f_ + 1) * 1024]
                     for f_ in range(8)]
            for e_i in range(NE):
                p.wait("sync", "MM", fslot_free_mm)
                p.dwait("sync", "WBA")
                if s == 0 and first:
                    p.wait("gpsimd", "MM", fslot_free_mm)
                    for f_ in range(8):
                        p.dma("gpsimd", fslot[f_], f2d[e_i][f_ * 128:(f_ + 1) * 128, :], "DB")
                    p.dwait("sync", "DB")
                    for f_ in range(8):
                        p.dma("sync", wcf2[e_i * 8 + f_], fslot[f_], "WBA")
                else:
                    for f_ in range(8):
                        p.dma("sync", fslot[f_], wcf2[e_i * 8 + f_], "DB")
                e1done = {}
                for o in range(8):
                    slotA = wbA[:, (o % 2) * 1024:(o % 2 + 1) * 1024]
                    slotB = wbB[:, (o % 2) * 1024:(o % 2 + 1) * 1024]
                    f1e = f1d[e_i]
                    wchunk("f1", e_i, o, f1e, wcf1, e_i * 16 + o, JD, slotA,
                           e1done.get(o - 2, wbfree_mm[0]), s, "LWA", "WBA")
                    wchunk("f1", e_i, 8 + o, f1e, wcf1, e_i * 16 + 8 + o, JD, slotB,
                           0, s, "LWB", "WBB")
                    p.dwait("tensor", "LWA")
                    p.dwait("tensor", "LWB")
                    for tb in range(TB):
                        i = o * TB + tb
                        pw1, pw2 = pb[2 + i % 2], pb[4 + i % 2]
                        if i >= 2:
                            p.wait("tensor", "ES", es_after[i - 2])
                            p.wait("tensor", "EV", ev_after[i - 2])
                        else:
                            p.wait("tensor", "EV", max(pbfree[2 + i % 2], pbfree[4 + i % 2]))
                            p.wait("tensor", "ES", max(pbfree_es[2 + i % 2],
                                                       pbfree_es.get(4 + i % 2, 0)))
                        for k in range(JD):
                            p.op("tensor", lambda e, k=k, tb=tb, slotA=slotA, pw1=pw1:
                                 e.matmul(
                                out=pw1[:], lhsT=slotA[:, k * 128:(k + 1) * 128],
                                rhs=xln[:, k * S + tb * NT: k * S + (tb + 1) * NT],
                                start=(k == 0), stop=(k == JD - 1)))
                        mm = 0
                        for k in range(JD):
                            mm = p.op("tensor", lambda e, k=k, tb=tb, slotB=slotB, pw2=pw2:
                                      e.matmul(
                                out=pw2[:], lhsT=slotB[:, k * 128:(k + 1) * 128],
                                rhs=xln[:, k * S + tb * NT: k * S + (tb + 1) * NT],
                                start=(k == 0), stop=(k == JD - 1)),
                                "MM" if k == JD - 1 else None)
                        p.wait("scalar", "MM", mm)
                        es_g = p.op("scalar", lambda e, i=i, o=o, e_i=e_i, pw2=pw2:
                                    e.activation(
                            out=abuf[:, (i % 2) * NT:(i % 2 + 1) * NT], in_=pw2[:],
                            func=AF.Gelu, bias=fb1[:, e_i * 16 + 8 + o:e_i * 16 + 9 + o]),
                            "ES")
                        es_after[i] = es_g
                        p.wait("vector", "MM", mm)
                        p.wait("vector", "ES", es_g)
                        p.op("vector", lambda e, i=i, o=o, e_i=e_i, pw1=pw1:
                             e.scalar_tensor_tensor(
                            out=abuf[:, (i % 2) * NT:(i % 2 + 1) * NT], in0=pw1[:],
                            scalar=fb1[:, e_i * 16 + o:e_i * 16 + o + 1],
                            in1=abuf[:, (i % 2) * NT:(i % 2 + 1) * NT],
                            op0=ALU.add, op1=ALU.mult))
                        ev = p.op("vector", lambda e, i=i, o=o, tb=tb, e_i=e_i:
                                  e.tensor_single_scalar(
                            out=aexp[:, o * S + tb * NT: o * S + (tb + 1) * NT],
                            in_=abuf[:, (i % 2) * NT:(i % 2 + 1) * NT],
                            scalar=gmc[:, e_i * 8 + o:e_i * 8 + o + 1], op=ALU.mult), "EV")
                        ev_after[i] = ev
                    e1done[o] = mm
                wbfree_mm[0] = p.tally["MM"]
                wbfree_mm[1] = p.tally["MM"]
                ev_aexp_all = p.tally["EV"]
                pbfree[2] = ev_after[8 * TB - 2]
                pbfree[3] = ev_after[8 * TB - 1]
                pbfree_es[2] = es_after[8 * TB - 2]
                pbfree_es[3] = es_after[8 * TB - 1]
                p.wait("gpsimd", "EV", ev_aexp_all)
                p.dma("gpsimd", ebias, fb2d[0:1, e_i * D:(e_i + 1) * D], "DB")
                p.dwait("tensor", "DB")
                p.wait("tensor", "EV", ev_aexp_all)
                p.wait("tensor", "ES", es_after[8 * TB - 1])
                ev_prev_grp = 0
                for grp in range(2):
                    if grp:
                        p.wait("tensor", "EV", ev_prev_grp)
                    for fi in range(8):
                        for tile in range(8):
                            tt = grp * 4 + tile // 2
                            dblk = tile % 2
                            p.op("tensor", lambda e, fi=fi, tile=tile, tt=tt, dblk=dblk:
                                 e.matmul(
                                out=pb[tile][:],
                                lhsT=aexp[:, fi * S + tt * 128: fi * S + (tt + 1) * 128],
                                rhs=fslot[fi][:, dblk * NT:(dblk + 1) * NT],
                                start=(fi == 0), stop=False))
                    mm = 0
                    for tile in range(8):
                        dblk = tile % 2
                        mm = p.op("tensor", lambda e, tile=tile, dblk=dblk: e.matmul(
                            out=pb[tile][:], lhsT=ones_rb[:],
                            rhs=ebias[0:1, dblk * NT:(dblk + 1) * NT],
                            start=False, stop=True), "MM" if tile == 7 else None)
                    p.wait("vector", "MM", mm)
                    ev = 0
                    for tile in range(8):
                        tt = grp * 4 + tile // 2
                        dblk = tile % 2
                        ev = p.op("vector", lambda e, tile=tile, tt=tt, dblk=dblk, e_i=e_i:
                                  e.scalar_tensor_tensor(
                            out=yacc[:, tt * D + dblk * NT: tt * D + (dblk + 1) * NT],
                            in0=pb[tile][:],
                            scalar=comb[:, tt * NE + e_i:tt * NE + e_i + 1],
                            in1=yacc[:, tt * D + dblk * NT: tt * D + (dblk + 1) * NT],
                            op0=ALU.mult, op1=ALU.add), "EV")
                        pbfree[tile] = ev
                    ev_prev_grp = ev
                fslot_free_mm = p.tally["MM"]
                wbfree_mm[0] = p.tally["MM"]
                wbfree_mm[1] = p.tally["MM"]
            ev_yacc_done = p.tally["EV"]
            xln_free_mm = p.tally["MM"]
            psAB_free = p.tally["EV"]
            psCD_free = psAB_free

            xresf = hb_f[:, 0:JD * S]
            p.wait("sync", "MM", wbfree_mm[0])
            p.dwait("sync", "ST")
            lx_res = p.dma("sync", xresf.rearrange("p (j t) -> p j t", j=JD),
                           x[:, s * S:(s + 1) * S].rearrange("(j p) t -> p j t", p=128),
                           "LX")
            for tt in range(8):
                mm_tr = {}
                for j in range(JD):
                    i = tt * JD + j
                    pw = pb[2 + i % 2]
                    p.wait("tensor", "EV", max(pbfree[2 + i % 2], ev_yacc_done))
                    p.wait("tensor", "ES", pbfree_es[2 + i % 2])
                    pbfree_es[2 + i % 2] = 0
                    mm = p.op("tensor", lambda e, tt=tt, j=j, pw=pw: e.transpose(
                        out=pw[:, 0:128],
                        in_=yacc[:, tt * D + j * 128: tt * D + (j + 1) * 128],
                        identity=identb), "MM")
                    p.wait("vector", "MM", mm)
                    if j == 0:
                        p.dwait("vector", "ST")
                        if i == 0:
                            p.dwait("vector", "LX")
                    ev = p.op("vector", lambda e, tt=tt, j=j, pw=pw: e.scalar_tensor_tensor(
                        out=ybuf[:, j * 128:(j + 1) * 128], in0=pw[:, 0:128], scalar=0.5,
                        in1=xresf[:, j * S + tt * 128: j * S + (tt + 1) * 128],
                        op0=ALU.mult, op1=ALU.add), "EV")
                    pbfree[2 + i % 2] = ev
                p.wait("sync", "EV", pbfree[2])
                p.wait("sync", "EV", pbfree[3])
                p.dma("sync", y[:, s * S + tt * 128: s * S + (tt + 1) * 128].rearrange(
                    "(j p) t -> p j t", p=128),
                    ybuf[:, 0:JD * 128].rearrange("p (j t) -> p j t", j=JD), "ST")
            yaccfree_mm = p.tally["MM"]
        p.dwait("sync", "ST")


def build_l1ffn():
    nc = bass.Bass("TRN2", target_bir_lowering=False, detect_race_conditions=RACE_DETECT)
    x = nc.dram_tensor("x", [D, T], F32, kind="ExternalInput").ap()
    y = nc.dram_tensor("y", [D, T], F32, kind="ExternalOutput").ap()
    with contextlib.ExitStack() as es:
        p = Prog(nc)
        p.make_sems(es, ALLSEMS)
        Cx = Ctx(nc, p, es)
        emit_l1ffn(Cx, x, y, wp="", first=True)
        with nc.Block() as block:
            p.replay(block)
    return nc


def layout_l1ffn(inp):
    f = np.float32
    g = lambda k: np.asarray(inp[k], f)
    fc1 = g("L1_ffn_fc1_w")
    fc2 = g("L1_ffn_fc2_w")
    return dict(
        gw=np.ascontiguousarray(g("L1_ffn_gate_w").T.reshape(8, 128, 15)
                                .transpose(1, 0, 2).reshape(128, 120)),
        s1=np.ascontiguousarray(g("L1_ffn_s_fc1_w").T),
        s2=np.ascontiguousarray(g("L1_ffn_s_fc2_w").T),
        f1=np.ascontiguousarray(fc1.transpose(0, 2, 1)),
        f2=np.ascontiguousarray(fc2.transpose(0, 2, 1)),
        fb1=np.ascontiguousarray(g("L1_ffn_fc1_b").reshape(15, 16, 128)
                                 .transpose(2, 0, 1).reshape(128, 240)),
        fb2=np.ascontiguousarray(g("L1_ffn_fc2_b").reshape(1, -1)),
        gm=np.ascontiguousarray(g("L1_ffn_geglu_mult").reshape(15, 8, 128)
                                .transpose(2, 0, 1).reshape(128, 120)),
        sb1=_colmajor(g("L1_ffn_s_fc1_b")),
        sgm=_colmajor(g("L1_ffn_s_geglu_mult")),
        sb2=np.ascontiguousarray(g("L1_ffn_s_fc2_b").reshape(1, 1024)),
        idd=np.eye(128, dtype=f),
    )


_PROGS = {}


def _prog(name):
    if name not in _PROGS:
        _PROGS[name] = {"l0ffn": build_l0ffn, "l0attn": build_l0attn,
                        "l1ffn": build_l1ffn, "l1attn": build_l1attn,
                        "fused": build_fused}[name]()
    return _PROGS[name]


def build_fused():
    nc = bass.Bass("TRN2", target_bir_lowering=False, detect_race_conditions=RACE_DETECT)
    x = nc.dram_tensor("x", [D, T], F32, kind="ExternalInput").ap()
    y = nc.dram_tensor("y", [D, T], F32, kind="ExternalOutput").ap()
    b0 = nc.dram_tensor("b0", [D, T], F32, kind="Internal").ap()
    b1 = nc.dram_tensor("b1", [D, T], F32, kind="Internal").ap()
    with contextlib.ExitStack() as es:
        p = Prog(nc)
        p.make_sems(es, ALLSEMS)
        Cx = Ctx(nc, p, es)
        emit_l0ffn(Cx, x, b0, wp="f0", first=True)
        Cx.barrier()
        emit_l0attn(Cx, b0, b1, wp="a0", first=True)
        Cx.barrier()
        emit_l0ffn(Cx, b1, b0, wp="f0", first=False)
        Cx.barrier()
        emit_l1ffn(Cx, b0, b1, wp="f1", first=True)
        Cx.barrier()
        emit_l1attn(Cx, b1, b0, wp="a1", first=True)
        Cx.barrier()
        emit_l1ffn(Cx, b0, y, wp="f1", first=False)
        with nc.Block() as block:
            p.replay(block)
    return nc


def kernel(**inputs):
    x = np.asarray(inputs["L0_attn_x"], np.float32)
    w = {}
    for pref, lay in (("f0", layout_l0ffn), ("a0", layout_l0attn),
                      ("f1", layout_l1ffn), ("a1", layout_l1attn)):
        for k, v in lay(inputs).items():
            w[f"{pref}_{k}"] = v
    nc = _prog("fused")
    in_maps = [dict(w, x=np.ascontiguousarray(x[b].T)) for b in range(NCORE)]
    res = run_bass_kernel_spmd(nc, in_maps, core_ids=list(range(NCORE)))
    return np.stack([np.asarray(res.results[b]["y"]).T for b in range(NCORE)], axis=0)



# version 04 (final)
# speedup vs baseline: 1.2795x; 1.0691x over previous
import contextlib
import numpy as np

import concourse.bass as bass
import concourse.mybir as mybir
from concourse.bass_utils import run_bass_kernel_spmd
from concourse.alu_op_type import AluOpType as ALU

F32 = mybir.dt.float32
BF16 = mybir.dt.bfloat16
AF = mybir.ActivationFunctionType
AX = mybir.AxisListType

ENG = ("sync", "gpsimd", "vector", "scalar", "tensor")
RACE_DETECT = True

T = 4096
D = 1024
JD = D // 128
NT = 512
NCORE = 8


class Prog:

    def __init__(self, nc):
        self.nc = nc
        self.ops = {e: [] for e in ENG}
        self.sems = {}
        self.tally = {}

    def make_sems(self, es, names):
        for n in names:
            self.sems[n] = es.enter_context(self.nc.semaphore(n))
            self.tally[n] = 0

    DMA_SEMS = ("LWA", "LWB", "LX", "WBA", "WBB", "ST", "DB", "DG")

    def wait(self, eng, sem, val):
        assert val <= self.tally[sem], ("wait ahead of tally", eng, sem, val, self.tally[sem])
        if sem in self.DMA_SEMS:
            assert val == self.tally[sem] or val == 0, ("partial DMA wait", eng, sem, val,
                                                        self.tally[sem])
        if val > 0:
            s = self.sems[sem]
            self.ops[eng].append(lambda e, s=s, v=val: e.wait_ge(s, v))

    def dwait(self, eng, sem):
        self.wait(eng, sem, self.tally[sem])

    def op(self, eng, fn, inc=None, n=1):
        if inc is None:
            self.ops[eng].append(fn)
            return None
        self.tally[inc] += n
        s = self.sems[inc]

        def run(e, fn=fn, s=s, n=n):
            fn(e).then_inc(s, n)

        self.ops[eng].append(run)
        return self.tally[inc]

    def dma(self, eng, out, in_, inc):
        return self.op(eng, lambda e, o=out, i=in_: e.dma_start(out=o, in_=i), inc, 16)

    def replay(self, block):
        for eng in ENG:
            dec = getattr(block, eng)
            ops = self.ops[eng]

            def body(e, ops=ops):
                for f in ops:
                    f(e)

            body.__name__ = f"_{eng}"
            dec(body)


class Ctx:

    def __init__(self, nc, p, es):
        self.nc = nc
        self.p = p
        self.es = es
        self._dram = {}
        self._sbuf = {}
        self._state = {}
        sb = lambda name, shape, dt: es.enter_context(nc.sbuf_tensor(name, shape, dt))
        self.xs = sb("xs", [128, JD * NT], F32)
        self.xln = sb("xln", [128, JD * 1024], BF16)
        self.hb = sb("hb", [128, 4 * JD * 1024], BF16)
        self.arena = sb("arena", [128, 4 * JD * 1024], BF16)
        self.wbA = sb("wbA", [128, JD * 1024], BF16)
        self.wbB = sb("wbB", [128, JD * 1024], BF16)
        self.abuf = sb("abuf", [128, 2 * NT], F32)
        self.ones_r = sb("ones_r", [1, 128], F32)
        self.ones_c = sb("ones_c", [128, 1], F32)
        self.rmean = sb("rmean", [1, NT], F32)
        self.rvar = sb("rvar", [1, NT], F32)
        self.rt2 = sb("rt2", [1, NT], F32)
        self.rA = sb("rA", [1, NT], F32)
        self.rB = self.rt2
        self.ceps = sb("ceps", [1, 2], F32)
        self.pb = [es.enter_context(nc.psum_tensor(f"pb{i}", [128, NT], F32))
                   for i in range(8)]
        self.inited = False

    def dram(self, name, shape, dt=F32, kind="ExternalInput"):
        if name not in self._dram:
            self._dram[name] = self.nc.dram_tensor(name, shape, dt, kind=kind).ap()
        return self._dram[name]

    def sb(self, name, shape, dt):
        if name not in self._sbuf:
            self._sbuf[name] = self.es.enter_context(self.nc.sbuf_tensor(name, shape, dt))
        return self._sbuf[name]

    def init_common(self):
        if self.inited:
            return self._state["ev_common"]
        p = self.p
        p.op("vector", lambda e: e.memset(self.ones_r[:], 1.0))
        p.op("vector", lambda e: e.memset(self.ones_c[:], 1.0))
        p.op("vector", lambda e: e.memset(self.ceps[0:1, 0:1], 1e-5))
        p.op("vector", lambda e: e.memset(self.ceps[0:1, 1:2], 1e-6))
        ev = p.op("vector", lambda e: e.nop(), "EV")
        self._state["ev_common"] = ev
        self.inited = True
        return ev

    def barrier(self):
        p = self.p
        mm, ev, es_ = p.tally["MM"], p.tally["EV"], p.tally["ES"]
        for eng in ENG:
            if eng != "tensor":
                p.wait(eng, "MM", mm)
            if eng != "vector":
                p.wait(eng, "EV", ev)
            if eng != "scalar":
                p.wait(eng, "ES", es_)
            for dsem in ("ST", "LX", "LWA", "LWB", "WBA", "WBB", "DB"):
                p.dwait(eng, dsem)


def emit_l0ffn(C, xin, xout, wp="f0", first=True):
    HID = 4096
    JH = HID // 128
    S = 1024
    NS = T // S
    TB = S // NT

    nc = C.nc
    p = C.p
    nm = (wp + "_") if wp else ""
    x, y = xin, xout
    w1t = C.dram(nm + "w1t", [D, HID])
    w2t = C.dram(nm + "w2t", [D, HID])
    h1t = C.dram(nm + "h1t", [HID, HID])
    h2t = C.dram(nm + "h2t", [HID, HID])
    w3t = C.dram(nm + "w3t", [HID, D])
    s1v = C.dram(nm + "s1v", [128, JH])
    s2v = C.dram(nm + "s2v", [128, JH])
    h1sv = C.dram(nm + "h1sv", [128, JH])
    h2sv = C.dram(nm + "h2sv", [128, JH])
    s3v = C.dram(nm + "s3v", [128, JD])
    wc1 = C.dram(nm + "wc1", [JH, 128, JD * 128], BF16, "Internal")
    wc2 = C.dram(nm + "wc2", [JH, 128, JD * 128], BF16, "Internal")
    wc3 = C.dram(nm + "wc3", [JH, 128, JH * 128], BF16, "Internal")
    wc4 = C.dram(nm + "wc4", [JH, 128, JH * 128], BF16, "Internal")
    wc5 = C.dram(nm + "wc5", [JD, 128, JH * 128], BF16, "Internal")

    if True:
        xs = C.xs
        xln = C.xln
        hb = C.hb
        arena = C.arena
        wbA = C.wbA
        wbB = C.wbB
        abuf = C.abuf
        s1c = C.sb(nm + "s1c", [128, JH], F32)
        s2c = C.sb(nm + "s2c", [128, JH], F32)
        h1sc = C.sb(nm + "h1sc", [128, JH], F32)
        h2sc = C.sb(nm + "h2sc", [128, JH], F32)
        s3h = C.sb(nm + "s3h", [128, JD], F32)
        ones_r = C.ones_r
        ones_c = C.ones_c
        rmean, rvar, rt2, rA, rB = C.rmean, C.rvar, C.rt2, C.rA, C.rB
        ceps = C.ceps
        pb = C.pb

        arena_f = arena.bitcast(F32)
        gbuf = arena
        xsq = arena_f[:, 10240:10240 + JD * NT]
        xres = arena_f[:, 0:JD * S]
        ybuf = [arena_f[:, 8192 + i * NT:8192 + (i + 1) * NT] for i in range(2)]

        ev_common = C.init_common()
        state = C._state.setdefault(wp or "l0ffn", {})
        if first:
            for dst, srcd in ((s1c, s1v), (s2c, s2v), (h1sc, h1sv), (h2sc, h2sv),
                              (s3h, s3v)):
                p.dma("sync", dst[:], srcd, "LX")
            p.dwait("vector", "LX")
            p.op("vector", lambda e: e.tensor_scalar_mul(out=s3h[:], in0=s3h[:],
                                                         scalar1=0.5))
            p.op("vector", lambda e: e.drain())
            state["ev_init"] = p.op("vector", lambda e: e.nop(), "EV")
            state["wb_after"] = {}
        ev_init = max(state["ev_init"], ev_common)
        wb_after = state["wb_after"]

        xs_free = (0, ev_init)
        xln_free_mm = 0
        wb_free_mm = 0
        hb_free_mm = 0
        arena_free = (0, 0)
        psAB_free_ev = 0
        psCD_free_ev = 0
        prev_mm_stats = 0

        def wchunk_load(s, tag, o, wsrc, wc, ktiles, slot, free_mm, lsem, wsem):
            if free_mm:
                p.wait("gpsimd", "MM", free_mm)
            if s == 0 and first:
                if (tag, o - 2) in wb_after:
                    p.dwait("gpsimd", wsem)
                view3 = slot.rearrange("p (k o) -> p k o", k=ktiles)
                src = wsrc[:, o * 128:(o + 1) * 128].rearrange("(k p) o -> p k o", p=128)
                p.dma("gpsimd", view3, src, lsem)
                p.dwait("sync", lsem)
                wb_after[(tag, o)] = p.dma("sync", wc[o], slot, wsem)
            else:
                p.dwait("gpsimd", wsem)
                p.dma("gpsimd", slot, wc[o], lsem)

        for s in range(NS):
            ev_xln = {}
            for tb in range(TB):
                t0 = s * S + tb * NT
                p.wait("sync", "MM", xs_free[0])
                p.wait("sync", "EV", xs_free[1])
                src = x[:, t0:t0 + NT].rearrange("(j p) t -> p j t", p=128)
                lx_xs = p.dma("sync", xs[:].rearrange("p (j t) -> p j t", j=JD), src, "LX")
                p.wait("scalar", "MM", prev_mm_stats)
                p.dwait("scalar", "LX")
                es_xsq = p.op("scalar", lambda e: e.activation(out=xsq, in_=xs[:],
                                                               func=AF.Square), "ES")
                p.wait("tensor", "EV", max(ev_init, psAB_free_ev))
                p.dwait("tensor", "LX")
                for j in range(JD):
                    p.op("tensor", lambda e, j=j: e.matmul(
                        out=pb[4][0:1, :], lhsT=ones_c[:], rhs=xs[:, j * NT:(j + 1) * NT],
                        start=(j == 0), stop=(j == JD - 1)))
                p.wait("tensor", "ES", es_xsq)
                mm_stats = 0
                for j in range(JD):
                    mm_stats = p.op("tensor", lambda e, j=j: e.matmul(
                        out=pb[5][0:1, :], lhsT=ones_c[:], rhs=xsq[:, j * NT:(j + 1) * NT],
                        start=(j == 0), stop=(j == JD - 1)),
                        "MM" if j == JD - 1 else None)
                prev_mm_stats = mm_stats
                p.wait("vector", "MM", mm_stats)
                p.op("vector", lambda e: e.tensor_scalar_mul(out=rmean[:], in0=pb[4][0:1, :],
                                                             scalar1=1.0 / D))
                p.op("vector", lambda e: e.tensor_scalar_mul(out=rvar[:], in0=pb[5][0:1, :],
                                                             scalar1=1.0 / D))
                p.op("vector", lambda e: e.tensor_mul(out=rt2[:], in0=rmean[:], in1=rmean[:]))
                psAB_free_ev = p.op("vector", lambda e: e.tensor_sub(out=rvar[:], in0=rvar[:],
                                                                     in1=rt2[:]), "EV")
                p.wait("scalar", "EV", psAB_free_ev)
                p.op("scalar", lambda e: e.activation(out=rt2[:], in_=rvar[:], func=AF.Ln,
                                                      bias=ceps[0:1, 0:1]))
                p.op("scalar", lambda e: e.drain())
                es_rstd = p.op("scalar", lambda e: e.activation(out=rA[:], in_=rt2[:],
                                                                func=AF.Exp, scale=-0.5), "ES")
                p.wait("vector", "ES", es_rstd)
                p.op("vector", lambda e: e.tensor_mul(out=rB[:], in0=rmean[:], in1=rA[:]))
                ev_rB = p.op("vector", lambda e: e.tensor_scalar_mul(out=rB[:], in0=rB[:],
                                                                     scalar1=-1.0), "EV")
                p.wait("tensor", "EV", max(ev_rB, psCD_free_ev))
                p.op("tensor", lambda e: e.matmul(out=pb[6][:], lhsT=ones_r[:], rhs=rA[:],
                                                  start=True, stop=True))
                mm_bcast = p.op("tensor", lambda e: e.matmul(out=pb[7][:], lhsT=ones_r[:],
                                                             rhs=rB[:], start=True, stop=True),
                                "MM")
                p.wait("vector", "MM", max(mm_bcast, xln_free_mm))
                last = 0
                for j in range(JD):
                    p.op("vector", lambda e, j=j: e.tensor_mul(
                        out=abuf[:, 0:NT], in0=xs[:, j * NT:(j + 1) * NT], in1=pb[6][:]))
                    last = p.op("vector", lambda e, j=j, tb=tb: e.tensor_add(
                        out=xln[:, j * S + tb * NT: j * S + (tb + 1) * NT],
                        in0=abuf[:, 0:NT], in1=pb[7][:]), "EV")
                ev_xln[tb] = last
                psCD_free_ev = last
                xs_free = (mm_stats, last)

            g12_done = {}
            es_after_g12 = {}
            ev_after_g12 = {}
            for o in range(JH):
                slot1 = wbA[:, (o % 2) * JD * 128:(o % 2 + 1) * JD * 128]
                slot2 = wbB[:, (o % 2) * JD * 128:(o % 2 + 1) * JD * 128]
                fm = g12_done.get(o - 2, wb_free_mm)
                wchunk_load(s, 1, o, w1t, wc1, JD, slot1, fm, "LWA", "WBA")
                wchunk_load(s, 2, o, w2t, wc2, JD, slot2, 0, "LWB", "WBB")
                p.dwait("tensor", "LWA")
                p.dwait("tensor", "LWB")
                if o == 0:
                    p.wait("tensor", "EV", ev_xln[TB - 1])
                for tb in range(TB):
                    i = o * TB + tb
                    pw1, pw2 = pb[i % 2], pb[2 + i % 2]
                    if i >= 2:
                        p.wait("tensor", "ES", es_after_g12[i - 2])
                        p.wait("tensor", "EV", ev_after_g12[i - 2])
                    for k in range(JD):
                        p.op("tensor", lambda e, k=k, tb=tb, slot1=slot1, pw1=pw1: e.matmul(
                            out=pw1[:], lhsT=slot1[:, k * 128:(k + 1) * 128],
                            rhs=xln[:, k * S + tb * NT: k * S + (tb + 1) * NT],
                            start=(k == 0), stop=(k == JD - 1)))
                    mm = 0
                    for k in range(JD):
                        mm = p.op("tensor", lambda e, k=k, tb=tb, slot2=slot2, pw2=pw2: e.matmul(
                            out=pw2[:], lhsT=slot2[:, k * 128:(k + 1) * 128],
                            rhs=xln[:, k * S + tb * NT: k * S + (tb + 1) * NT],
                            start=(k == 0), stop=(k == JD - 1)),
                            "MM" if k == JD - 1 else None)
                    p.wait("scalar", "MM", mm)
                    es_s = p.op("scalar", lambda e, i=i, o=o, pw1=pw1: e.activation(
                        out=abuf[:, (i % 2) * NT:(i % 2 + 1) * NT], in_=pw1[:], func=AF.Silu,
                        scale=s1c[:, o:o + 1]), "ES")
                    es_after_g12[i] = es_s
                    p.wait("vector", "MM", mm)
                    p.wait("vector", "ES", es_s)
                    if i == 0:
                        p.wait("vector", "MM", hb_free_mm)
                    ev_h = p.op("vector", lambda e, i=i, o=o, tb=tb, pw2=pw2:
                                e.scalar_tensor_tensor(
                        out=hb[:, o * S + tb * NT: o * S + (tb + 1) * NT], in0=pw2[:],
                        scalar=s2c[:, o:o + 1], in1=abuf[:, (i % 2) * NT:(i % 2 + 1) * NT],
                        op0=ALU.mult, op1=ALU.mult), "EV")
                    ev_after_g12[i] = ev_h
                g12_done[o] = mm
            ev_h_all = ev_after_g12[JH * TB - 1]
            mm_g12_done = p.tally["MM"]
            xln_free_mm = mm_g12_done
            wb_free_mm = mm_g12_done

            def big_gemm(s, tag, wsrc, wc, rhs_buf, rhs_S, n_out, ktiles,
                         evac_fn, first_waits):
                done = {}
                es_after = {}
                nonlocal wb_free_mm
                for o in range(n_out):
                    lsem, wsem = ("LWA", "WBA") if o % 2 == 0 else ("LWB", "WBB")
                    slot = (wbA if o % 2 == 0 else wbB)[:, 0:ktiles * 128]
                    fm = done.get(o - 2, wb_free_mm)
                    wchunk_load(s, tag, o, wsrc, wc, ktiles, slot, fm, lsem, wsem)
                    p.dwait("tensor", lsem)
                    if o == 0:
                        for sem, val in first_waits:
                            p.wait("tensor", sem, val)
                    mm = 0
                    for tb in range(TB):
                        i = o * TB + tb
                        pw = pb[i % 4]
                        if i >= 4:
                            sem, val = es_after[i - 4]
                            p.wait("tensor", sem, val)
                        for k in range(ktiles):
                            mm = p.op("tensor", lambda e, k=k, tb=tb, slot=slot, pw=pw,
                                      rhs_buf=rhs_buf, rhs_S=rhs_S: e.matmul(
                                out=pw[:], lhsT=slot[:, k * 128:(k + 1) * 128],
                                rhs=rhs_buf[:, k * rhs_S + tb * NT: k * rhs_S + (tb + 1) * NT],
                                start=(k == 0), stop=(k == ktiles - 1)),
                                "MM" if k == ktiles - 1 else None)
                        es_after[i] = evac_fn(o, tb, pw, i, mm)
                    done[o] = mm
                wb_free_mm = p.tally["MM"]
                return p.tally["MM"]

            af = arena_free

            def evac_g3(o, tb, pw, i, mm, af=af):
                p.wait("scalar", "MM", mm)
                if i == 0:
                    p.wait("scalar", "EV", af[0])
                    p.dwait("scalar", "ST")
                t = p.op("scalar", lambda e, o=o, tb=tb, pw=pw: e.activation(
                    out=gbuf[:, o * S + tb * NT: o * S + (tb + 1) * NT], in_=pw[:],
                    func=AF.Gelu, scale=h1sc[:, o:o + 1]), "ES")
                return ("ES", t)

            mm_g3_done = big_gemm(s, 3, h1t, wc3, hb, S, JH, JH, evac_g3,
                                  [("EV", ev_h_all)])
            es_g3_all = p.tally["ES"]

            def evac_g4(o, tb, pw, i, mm):
                p.wait("scalar", "MM", mm)
                t = p.op("scalar", lambda e, o=o, tb=tb, pw=pw: e.activation(
                    out=hb[:, o * S + tb * NT: o * S + (tb + 1) * NT], in_=pw[:],
                    func=AF.Gelu, scale=h2sc[:, o:o + 1]), "ES")
                return ("ES", t)

            mm_g4_done = big_gemm(s, 4, h2t, wc4, gbuf, S, JH, JH, evac_g4,
                                  [("ES", es_g3_all)])
            es_g4_all = p.tally["ES"]

            p.wait("sync", "MM", mm_g4_done)
            srcx = x[:, s * S:(s + 1) * S].rearrange("(j p) t -> p j t", p=128)
            lx_res = p.dma("sync", xres.rearrange("p (j t) -> p j t", j=JD), srcx, "LX")

            ev_after_g5 = {}
            st_after_g5 = {}

            def evac_g5(o, tb, pw, i, mm, af=af):
                p.wait("vector", "MM", mm)
                p.dwait("vector", "LX")
                p.dwait("vector", "ST")
                t = p.op("vector", lambda e, o=o, tb=tb, i=i, pw=pw: e.scalar_tensor_tensor(
                    out=ybuf[i % 2], in0=pw[:], scalar=s3h[:, o:o + 1],
                    in1=xres[:, o * S + tb * NT: o * S + (tb + 1) * NT],
                    op0=ALU.mult, op1=ALU.add), "EV")
                ev_after_g5[i] = t
                p.wait("sync", "EV", t)
                t0 = s * S + tb * NT
                stt = p.dma("sync", y[o * 128:(o + 1) * 128, t0:t0 + NT], ybuf[i % 2], "ST")
                st_after_g5[i] = stt
                return ("EV", t)

            mm_g5_done = big_gemm(s, 5, w3t, wc5, hb, S, JD, JH, evac_g5,
                                  [("ES", es_g4_all)])
            hb_free_mm = mm_g5_done
            arena_free = (ev_after_g5[JD * TB - 1], st_after_g5[JD * TB - 1])

        p.dwait("sync", "ST")
        p.dwait("sync", "WBA")
        p.dwait("sync", "WBB")


ALLSEMS = ["LWA", "LWB", "LX", "WBA", "WBB", "ST", "DB", "MM", "EV", "ES"]


def build_l0ffn(dbg=False):
    nc = bass.Bass("TRN2", target_bir_lowering=False, detect_race_conditions=RACE_DETECT)
    x = nc.dram_tensor("x", [D, T], F32, kind="ExternalInput").ap()
    y = nc.dram_tensor("y", [D, T], F32, kind="ExternalOutput").ap()
    with contextlib.ExitStack() as es:
        p = Prog(nc)
        p.make_sems(es, ALLSEMS)
        C = Ctx(nc, p, es)
        emit_l0ffn(C, x, y, wp="", first=True)
        with nc.Block() as block:
            p.replay(block)
    return nc



def _colmajor(v):
    return np.ascontiguousarray(np.asarray(v, np.float32).reshape(-1, 128).T)


def layout_l0ffn(inp):
    f = np.float32
    return dict(
        w1t=np.ascontiguousarray(np.asarray(inp["L0_ffn_w1_w"], f).T),
        w2t=np.ascontiguousarray(np.asarray(inp["L0_ffn_w2_w"], f).T),
        h1t=np.ascontiguousarray(np.asarray(inp["L0_ffn_h1_w"], f).T),
        h2t=np.ascontiguousarray(np.asarray(inp["L0_ffn_h2_w"], f).T),
        w3t=np.ascontiguousarray(np.asarray(inp["L0_ffn_w3_w"], f).T),
        s1v=_colmajor(inp["L0_ffn_w1_s"]),
        s2v=_colmajor(inp["L0_ffn_w2_s"]),
        h1sv=_colmajor(inp["L0_ffn_h1_s"]),
        h2sv=_colmajor(inp["L0_ffn_h2_s"]),
        s3v=_colmajor(inp["L0_ffn_w3_s"]),
    )


def _norm_block(p, pb, src3, xsq3, dst3, ones_c, ones_r, rows, ceps, eps_idx,
                tmp, gain, with_mean, prev_sq_mm, psAB_free, psCD_free,
                waits_t=(), waits_s=(), waits_v=(), dst2=None):
    rmean, rvar, rt2, rA, rB = rows
    p.wait("scalar", "MM", prev_sq_mm)
    for sem, val in waits_s:
        p.wait("scalar", sem, val)
    es_xsq = p.op("scalar", lambda e: e.activation(out=xsq3, in_=src3, func=AF.Square), "ES")
    for sem, val in waits_t:
        p.wait("tensor", sem, val)
    p.wait("tensor", "EV", psAB_free)
    if with_mean:
        for j in range(JD):
            p.op("tensor", lambda e, j=j: e.matmul(
                out=pb[4][0:1, :], lhsT=ones_c[:], rhs=src3[:, j, :],
                start=(j == 0), stop=(j == JD - 1)))
    p.wait("tensor", "ES", es_xsq)
    mm_stats = 0
    for j in range(JD):
        mm_stats = p.op("tensor", lambda e, j=j: e.matmul(
            out=pb[5][0:1, :], lhsT=ones_c[:], rhs=xsq3[:, j, :],
            start=(j == 0), stop=(j == JD - 1)), "MM" if j == JD - 1 else None)
    p.wait("vector", "MM", mm_stats)
    for sem, val in waits_v:
        p.wait("vector", sem, val)
    p.op("vector", lambda e: e.tensor_scalar_mul(out=rvar[:], in0=pb[5][0:1, :],
                                                 scalar1=1.0 / D))
    if with_mean:
        p.op("vector", lambda e: e.tensor_scalar_mul(out=rmean[:], in0=pb[4][0:1, :],
                                                     scalar1=1.0 / D))
        p.op("vector", lambda e: e.tensor_mul(out=rt2[:], in0=rmean[:], in1=rmean[:]))
        p.op("vector", lambda e: e.tensor_sub(out=rvar[:], in0=rvar[:], in1=rt2[:]))
    ev_rows = p.op("vector", lambda e: e.nop(), "EV")
    p.wait("scalar", "EV", ev_rows)
    p.op("scalar", lambda e: e.activation(out=rt2[:], in_=rvar[:], func=AF.Ln,
                                          bias=ceps[0:1, eps_idx:eps_idx + 1]))
    p.op("scalar", lambda e: e.drain())
    es_rstd = p.op("scalar", lambda e: e.activation(out=rA[:], in_=rt2[:],
                                                    func=AF.Exp, scale=-0.5), "ES")
    ev_rB = es_rstd
    if with_mean:
        p.wait("vector", "ES", es_rstd)
        p.op("vector", lambda e: e.tensor_mul(out=rB[:], in0=rmean[:], in1=rA[:]))
        ev_rB = p.op("vector", lambda e: e.tensor_scalar_mul(out=rB[:], in0=rB[:],
                                                             scalar1=-1.0), "EV")
    if with_mean:
        p.wait("tensor", "EV", max(ev_rB, psCD_free))
        p.op("tensor", lambda e: e.matmul(out=pb[6][:], lhsT=ones_r[:], rhs=rA[:],
                                          start=True, stop=True))
        mm_bc = p.op("tensor", lambda e: e.matmul(out=pb[7][:], lhsT=ones_r[:], rhs=rB[:],
                                                  start=True, stop=True), "MM")
    else:
        p.wait("tensor", "ES", es_rstd)
        p.wait("tensor", "EV", psCD_free)
        mm_bc = p.op("tensor", lambda e: e.matmul(out=pb[6][:], lhsT=ones_r[:], rhs=rA[:],
                                                  start=True, stop=True), "MM")
    p.wait("vector", "MM", mm_bc)
    last = 0
    for j in range(JD):
        p.op("vector", lambda e, j=j: e.tensor_mul(out=tmp[:], in0=src3[:, j, :],
                                                   in1=pb[6][:]))
        if with_mean:
            last = p.op("vector", lambda e, j=j: e.tensor_add(
                out=dst3[:, j, :], in0=tmp[:], in1=pb[7][:]), "EV")
            if dst2 is not None:
                last = p.op("vector", lambda e, j=j: e.tensor_add(
                    out=dst2[:, j, :], in0=tmp[:], in1=pb[7][:]), "EV")
        else:
            last = p.op("vector", lambda e, j=j: e.tensor_scalar_mul(
                out=dst3[:, j, :], in0=tmp[:], scalar1=gain[:, j:j + 1]), "EV")
    return mm_stats, last


def emit_l0attn(C, xin, xout, wp="a0", first=True):
    H2 = 2048
    JH2 = H2 // 128
    S = 1024
    NS = T // S
    TB_T = T // NT
    TB = S // NT
    CH = 128
    NC = T // CH

    nc = C.nc
    p = C.p
    nm = (wp + "_") if wp else ""
    x, y = xin, xout
    wqt = C.dram(nm + "wqt", [D, D])
    wkt = C.dram(nm + "wkt", [D, D])
    wvt = C.dram(nm + "wvt", [D, D])
    wot = C.dram(nm + "wot", [D, D])
    wgt = C.dram(nm + "wgt", [D, H2])
    wut = C.dram(nm + "wut", [D, H2])
    wdt = C.dram(nm + "wdt", [H2, D])
    n1v = C.dram(nm + "n1v", [128, JD])
    n2v = C.dram(nm + "n2v", [128, JD])
    alv = C.dram(nm + "alv", [128, JD])
    bwv = C.dram(nm + "bwv", [128, JD])
    bbv = C.dram(nm + "bbv", [1, 1])
    msl = C.dram(nm + "msl", [128, 128])
    mli = C.dram(nm + "mli", [128, 128])
    xisc = C.dram(nm + "xisc", [D, T], F32, "Internal")
    qsc = C.dram(nm + "qsc", [JD, 128, T], BF16, "Internal")
    ksc = C.dram(nm + "ksc", [JD, 128, T], BF16, "Internal")
    vsc = C.dram(nm + "vsc", [JD, 128, T], BF16, "Internal")
    osc = C.dram(nm + "osc", [JD, 128, T], BF16, "Internal")
    wco = C.dram(nm + "wco", [JD, 128, JD * 128], BF16, "Internal")
    wcg = C.dram(nm + "wcg", [JH2, 128, JD * 128], BF16, "Internal")
    wcu = C.dram(nm + "wcu", [JH2, 128, JD * 128], BF16, "Internal")
    wcd = C.dram(nm + "wcd", [JD, 128, JH2 * 128], BF16, "Internal")

    if True:
        xs = C.xs
        xln = C.xln
        hb = C.hb
        arena = C.arena
        wbA = C.wbA
        wbB = C.wbB
        abuf = C.abuf
        n1c = C.sb(nm + "n1c", [128, JD], F32)
        n2c = C.sb(nm + "n2c", [128, JD], F32)
        amc = C.sb(nm + "amc", [128, JD], F32)
        alc = C.sb(nm + "alc", [128, JD], F32)
        bwc = C.sb(nm + "bwc", [128, JD], BF16)
        bwf = C.sb(nm + "bwf", [128, JD], F32)
        betc = C.sb(nm + "betc", [128, NC], F32)
        carry = [C.sb(nm + f"carry{j}", [128, 1], F32) for j in range(JD)]
        bbb = C.sb(nm + "bbb", [128, 1], F32)
        bb1 = C.sb(nm + "bb1", [1, 1], F32)
        maskl = C.sb(nm + "maskl", [128, 128], BF16)
        maski = C.sb(nm + "maski", [128, 128], BF16)
        identb = C.sb(nm + "identb", [128, 128], BF16)
        ones_r = C.ones_r
        ones_c = C.ones_c
        rmean, rvar, rt2, rA = C.rmean, C.rvar, C.rt2, C.rA
        rB = rt2
        ceps = C.ceps
        pb = C.pb

        pwr = xln[:, 0:1024]
        pwrT = xln[:, 1024:2048]
        ubuf = xln[:, 2048:3072]
        wtb = xln[:, 3072:3200]
        wtT = xln[:, 3200:3328]
        t1f = xs[:, 0:1024]

        rows = (rmean, rvar, rt2, rA, rB)
        arena_f = arena.bitcast(F32)
        hb_f = hb.bitcast(F32)
        ambc = arena_f[:, 12288:12288 + JD * NT]
        xi_tb = arena_f[:, 0:JD * NT]
        qf = arena[:, 0:1024]
        kf = arena[:, 1024:2048]
        vtm = arena[:, 2048:3072]
        ktm = arena[:, 3072:4096]
        otm = arena[:, 4096:5120]
        ofm = arena[:, 5120:6144]
        vstg = arena[:, 6144:7168]
        Sf = hb_f[:, 0:8192]
        Sbb = hb[:, 16384:24576]
        kstage = xln[:, 0:JD * NT]
        xsq = abuf[:, NT:2 * NT]
        vsq = arena_f[:, 8192:8192 + JD * NT]
        xnew = arena_f[:, 0:JD * S]
        xresC = arena_f[:, JD * S:2 * JD * S]
        mbuf = hb[:, 0:JH2 * S]
        ostripe = xln[:, 0:JD * S]
        h2st = xln[:, 0:JD * S]
        tmpNT = abuf[:, 0:NT]

        C.init_common()

        for dst, srcd in ((n1c, n1v), (n2c, n2v), (alc, alv), (bwf, bwv)):
            p.dma("sync", dst[:], srcd, "LX")
        p.dma("sync", bb1[:], bbv, "LX")
        p.wait("gpsimd", "MM", p.tally["MM"])
        p.dwait("gpsimd", "WBA")
        p.dma("gpsimd", maskl[:], msl, "LWA")
        p.dma("gpsimd", maski[:], mli, "LWA")
        for j in range(JD):
            p.op("vector", lambda e, j=j: e.memset(carry[j][:], 0.0))
        p.dwait("vector", "LWA")
        p.op("vector", lambda e: e.tensor_sub(out=identb[:], in0=maski[:], in1=maskl[:]))
        p.dwait("vector", "LX")
        p.op("vector", lambda e: e.tensor_copy(out=bwc[:], in_=bwf[:]))
        ev_init0 = p.op("vector", lambda e: e.nop(), "EV")
        p.wait("scalar", "EV", ev_init0)
        es_al = p.op("scalar", lambda e: e.activation(out=alc[:], in_=alc[:],
                                                      func=AF.Sigmoid), "ES")
        p.wait("vector", "ES", es_al)
        p.op("vector", lambda e: e.tensor_scalar(out=amc[:], in0=alc[:], scalar1=-1.0,
                                                 scalar2=1.0, op0=ALU.mult, op1=ALU.add))
        p.op("vector", lambda e: e.drain())
        p.op("vector", lambda e: e.memset(abuf[:, 0:NT], 0.0))
        for j in range(JD):
            p.op("vector", lambda e, j=j: e.tensor_scalar(
                out=ambc[:, j * NT:(j + 1) * NT], in0=abuf[:, 0:NT], scalar1=0.0,
                scalar2=amc[:, j:j + 1], op0=ALU.mult, op1=ALU.add))
        p.op("vector", lambda e: e.drain())
        ev_init = p.op("vector", lambda e: e.nop(), "EV")
        p.wait("tensor", "EV", ev_init)
        mm_bb = p.op("tensor", lambda e: e.matmul(out=pb[6][:, 0:1], lhsT=ones_r[:],
                                                  rhs=bb1[:], start=True, stop=True), "MM")
        p.wait("vector", "MM", mm_bb)
        ev_bbb = p.op("vector", lambda e: e.tensor_copy(out=bbb[:], in_=pb[6][:, 0:1]), "EV")

        hb3full = hb.ap().rearrange("p (j t) -> p j t", j=JD)
        xs3 = xs[:].rearrange("p (j t) -> p j t", j=JD)
        vsq3 = vsq.rearrange("p (j t) -> p j t", j=JD)
        xi3 = xi_tb.rearrange("p (j t) -> p j t", j=JD)
        xs_free = (0, ev_init0)
        xi_free = (0, 0)
        psAB_free = ev_bbb
        psCD_free = ev_bbb
        prev_sq = 0
        ev_h = 0
        st_xi = 0
        for tb in range(TB_T):
            t0 = tb * NT
            p.wait("sync", "MM", xs_free[0])
            p.wait("sync", "EV", xs_free[1])
            lx_xs = p.dma("sync", xs3, x[:, t0:t0 + NT].rearrange("(j p) t -> p j t", p=128),
                          "LX")
            if True:
                mm_s1, ev_ln = _norm_block(
                    p, pb, xs3, vsq3, xi3, ones_c, ones_r, rows, ceps, 0, tmpNT, None,
                    True, prev_sq, psAB_free, psCD_free,
                    waits_t=[("LX", lx_xs), ("EV", ev_init), ("MM", xi_free[0])],
                    waits_s=[("LX", lx_xs)],
                    waits_v=[("ST", st_xi)])
                xs_free = (mm_s1, ev_ln)
                prev_sq = mm_s1
                psAB_free = ev_ln
                psCD_free = ev_ln
                ev_h = ev_ln
            if True:
                p.wait("sync", "EV", ev_ln)
                st_xi = p.dma("sync", xisc[:, t0:t0 + NT].rearrange("(j p) t -> p j t", p=128),
                              xi3, "ST")
            if True:
                mm_s2, ev_done = _norm_block(
                    p, pb, xi3, vsq3, hb3full[:, :, t0:t0 + NT], ones_c, ones_r,
                    rows, ceps, 1,
                    tmpNT, n1c, False, mm_s1, ev_ln, ev_ln,
                    waits_t=[("EV", ev_ln)], waits_s=[("EV", ev_ln)])
                prev_sq = mm_s2
                psAB_free = ev_done
                psCD_free = ev_done
                ev_h = ev_done
                xi_free = (mm_s2, ev_done)
        ev_h_all = ev_h

        def proj_matmuls(wslot, o, tb, pw, wait_es):
            if wait_es is not None:
                p.wait("tensor", wait_es[0], wait_es[1])
            mm = 0
            for k in range(JD):
                mm = p.op("tensor", lambda e, k=k, o=o, tb=tb, pw=pw, wslot=wslot: e.matmul(
                    out=pw[:], lhsT=wslot[:, k * 1024 + o * 128: k * 1024 + (o + 1) * 128],
                    rhs=hb[:, k * T + tb * NT: k * T + (tb + 1) * NT],
                    start=(k == 0), stop=(k == JD - 1)), "MM" if k == JD - 1 else None)
            return mm

        p.wait("gpsimd", "EV", ev_init0)
        p.dma("gpsimd", wbA[:].rearrange("p (k o) -> p k o", k=JD),
              wqt.rearrange("(k p) o -> p k o", p=128), "LWA")
        p.dwait("tensor", "LWA")
        p.wait("tensor", "EV", ev_h_all)
        st_slot = {}
        ev_evac = {}
        for tb in range(TB_T):
            for o in range(JD):
                i = tb * JD + o
                pw = pb[i % 4]
                if i >= 4:
                    p.wait("tensor", "EV", ev_evac[i - 4])
                mm = proj_matmuls(wbA, o, tb, pw, None)
                p.wait("vector", "MM", mm)
                if tb > 0:
                    p.dwait("vector", "ST")
                ev = p.op("vector", lambda e, o=o, pw=pw: e.tensor_copy(
                    out=kstage[:, o * NT:(o + 1) * NT], in_=pw[:]), "EV")
                ev_evac[i] = ev
                p.wait("sync", "EV", ev)
                st_slot[(tb, o)] = p.dma("sync", qsc[o, :, tb * NT:(tb + 1) * NT],
                                         kstage[:, o * NT:(o + 1) * NT], "ST")
        mm_q_done = p.tally["MM"]
        st_q_done = p.tally["ST"]

        p.wait("gpsimd", "MM", mm_q_done)
        p.dma("gpsimd", wbB[:].rearrange("p (k o) -> p k o", k=JD),
              wkt.rearrange("(k p) o -> p k o", p=128), "LWB")
        p.dwait("tensor", "LWB")
        p.wait("tensor", "EV", p.tally["EV"])
        ev_evac = {}
        es_sq = {}
        st_slot = {}
        st_floor = st_q_done
        prev_ss_mm = 0
        for tb in range(TB_T):
            mm_o = {}
            for o in range(JD):
                i = tb * JD + o
                pw = pb[i % 4]
                if i >= 4:
                    p.wait("tensor", "EV", ev_evac[i - 4])
                    p.wait("tensor", "ES", es_sq[i - 4])
                mm = proj_matmuls(wbB, o, tb, pw, None)
                mm_o[o] = mm
                p.wait("vector", "MM", mm)
                p.dwait("vector", "ST")
                ev = p.op("vector", lambda e, o=o, pw=pw: e.tensor_copy(
                    out=kstage[:, o * NT:(o + 1) * NT], in_=pw[:]), "EV")
                ev_evac[i] = ev
                if True:
                    p.wait("scalar", "MM", max(mm, prev_ss_mm))
                    p.wait("scalar", "EV", ev)
                    es_sq[i] = p.op("scalar", lambda e, o=o, pw=pw: e.activation(
                        out=vsq3[:, o, :], in_=pw[:], func=AF.Square), "ES")
            if True:
                p.wait("tensor", "ES", es_sq[tb * JD + JD - 1])
                p.wait("tensor", "EV", psAB_free)
                mm_ss = 0
                for o in range(JD):
                    mm_ss = p.op("tensor", lambda e, o=o: e.matmul(
                        out=pb[4][0:1, :], lhsT=ones_c[:], rhs=vsq3[:, o, :],
                        start=(o == 0), stop=(o == JD - 1)), "MM" if o == JD - 1 else None)
                prev_ss_mm = mm_ss
            if True:
                p.wait("vector", "MM", mm_ss)
                psAB_free = p.op("vector", lambda e: e.tensor_single_scalar(
                    out=rvar[:], in_=pb[4][0:1, :], scalar=1e-24, op=ALU.max), "EV")
                p.wait("scalar", "EV", psAB_free)
                p.op("scalar", lambda e: e.activation(out=rt2[:], in_=rvar[:], func=AF.Ln))
                p.op("scalar", lambda e: e.drain())
                es_r = p.op("scalar", lambda e: e.activation(out=rA[:], in_=rt2[:],
                                                             func=AF.Exp, scale=-0.5), "ES")
                p.wait("tensor", "ES", es_r)
                p.wait("tensor", "EV", psCD_free)
                mm_bc = p.op("tensor", lambda e: e.matmul(out=pb[6][:], lhsT=ones_r[:],
                                                          rhs=rA[:], start=True, stop=True), "MM")
            if True:
                p.wait("vector", "MM", mm_bc)
                lastev = 0
                for o in range(JD):
                    lastev = p.op("vector", lambda e, o=o: e.tensor_mul(
                        out=kstage[:, o * NT:(o + 1) * NT],
                        in0=kstage[:, o * NT:(o + 1) * NT], in1=pb[6][:]), "EV")
                    p.wait("sync", "EV", lastev)
                    st_slot[(tb, o)] = p.dma("sync", ksc[o, :, tb * NT:(tb + 1) * NT],
                                             kstage[:, o * NT:(o + 1) * NT], "ST")
                psCD_free = lastev
        mm_k_done = p.tally["MM"]

        p.wait("gpsimd", "MM", mm_k_done)
        p.dma("gpsimd", wbA[:].rearrange("p (k o) -> p k o", k=JD),
              wvt.rearrange("(k p) o -> p k o", p=128), "LWA")
        p.dwait("tensor", "LWA")
        p.wait("tensor", "EV", p.tally["EV"])
        ev_evac = {}
        st_slot = {}
        st_floor = p.tally["ST"]
        for tb in range(TB_T):
            for o in range(JD):
                i = tb * JD + o
                pw = pb[i % 4]
                if i >= 4:
                    p.wait("tensor", "EV", ev_evac[i - 4])
                mm = proj_matmuls(wbA, o, tb, pw, None)
                p.wait("vector", "MM", mm)
                ev = p.op("vector", lambda e, o=o, pw=pw: e.tensor_scalar_mul(
                    out=xs[:, o * NT:(o + 1) * NT], in0=pw[:], scalar1=alc[:, o:o + 1]), "EV")
                ev_evac[i] = ev
            for o in range(JD):
                p.dwait("vector", "ST")
                p.op("vector", lambda e: e.drain())
                p.op("vector", lambda e, o=o: e.tensor_tensor_scan(
                    out=kstage[:, o * NT:(o + 1) * NT].bitcast(F32) if False else
                    xs[:, o * NT:(o + 1) * NT], data0=ambc[:, o * NT:(o + 1) * NT],
                    data1=xs[:, o * NT:(o + 1) * NT], initial=carry[o][:],
                    op0=ALU.mult, op1=ALU.add))
                ev = p.op("vector", lambda e, o=o: e.tensor_copy(
                    out=kstage[:, o * NT:(o + 1) * NT], in_=xs[:, o * NT:(o + 1) * NT]), "EV")
                p.op("vector", lambda e: e.drain())
                p.op("vector", lambda e, o=o: e.tensor_copy(
                    out=carry[o][:], in_=xs[:, (o + 1) * NT - 1:(o + 1) * NT]))
                p.wait("sync", "EV", ev)
                st_slot[(tb, o)] = p.dma("sync", vsc[o, :, tb * NT:(tb + 1) * NT],
                                         kstage[:, o * NT:(o + 1) * NT], "ST")
            p.op("vector", lambda e: e.drain())
        mm_v_done = p.tally["MM"]
        ev_v_done = p.tally["EV"]

        p.wait("tensor", "EV", p.tally["EV"])
        es_bet = {}
        for c in range(NC):
            pw = pb[4 + c % 2]
            if c >= 2:
                p.wait("tensor", "ES", es_bet[c - 2])
            mm = 0
            for k in range(JD):
                mm = p.op("tensor", lambda e, k=k, c=c, pw=pw: e.matmul(
                    out=pw[:, 0:1], lhsT=hb[:, k * T + c * 128: k * T + (c + 1) * 128],
                    rhs=bwc[:, k:k + 1], start=(k == 0), stop=(k == JD - 1)),
                    "MM" if k == JD - 1 else None)
            p.wait("scalar", "MM", mm)
            es_bet[c] = p.op("scalar", lambda e, c=c, pw=pw: e.activation(
                out=betc[:, c:c + 1], in_=pw[:, 0:1], func=AF.Sigmoid, bias=bbb[:]), "ES")
        es_beta_done = p.tally["ES"]

        pb6b = pb[6].bitcast(BF16)
        p.wait("vector", "ES", es_beta_done)
        p.op("vector", lambda e: e.memset(Sf[:, :], 0.0))
        ev_S0 = p.op("vector", lambda e: e.memset(Sbb[:, :], 0.0), "EV")

        sb_ready = ("EV", ev_S0)
        loads_free_mm = 0
        ev_sf_last = 0
        st_o = {}
        for c in range(NC):
            cs = slice(c * CH, (c + 1) * CH)
            p.wait("sync", "MM", loads_free_mm)
            lq = p.dma("sync", qf.rearrange("p (j t) -> p j t", j=JD),
                       qsc[:, :, cs].rearrange("j p t -> p j t"), "LX")
            lk = p.dma("sync", kf.rearrange("p (j t) -> p j t", j=JD),
                       ksc[:, :, cs].rearrange("j p t -> p j t"), "LX")
            lv = p.dma("sync", vstg.rearrange("p (j t) -> p j t", j=JD),
                       vsc[:, :, cs].rearrange("j p t -> p j t"), "LX")
            p.wait("tensor", sb_ready[0], sb_ready[1])
            p.dwait("tensor", "LX")
            mm = 0
            for j in range(JD):
                mm = p.op("tensor", lambda e, j=j: e.transpose(
                    out=pb6b[:, j * 128:(j + 1) * 128], in_=kf[:, j * 128:(j + 1) * 128],
                    identity=identb[:]), "MM" if j == JD - 1 else None)
            mm_trk = mm
            p.wait("vector", "MM", mm_trk)
            ev_ktm = p.op("vector", lambda e: e.tensor_copy(out=ktm[:], in_=pb6b[:]), "EV")
            p.wait("tensor", "EV", ev_ktm)
            for j in range(JD):
                mm = p.op("tensor", lambda e, j=j: e.transpose(
                    out=pb6b[:, j * 128:(j + 1) * 128], in_=vstg[:, j * 128:(j + 1) * 128],
                    identity=identb[:]), "MM" if j == JD - 1 else None)
            mm_trv = mm
            p.wait("vector", "MM", mm_trv)
            ev_vtm = p.op("vector", lambda e: e.tensor_copy(out=vtm[:], in_=pb6b[:]), "EV")
            mm = 0
            for h in range(2):
                for j in range(JD):
                    mm = p.op("tensor", lambda e, h=h, j=j: e.matmul(
                        out=pb[h][:], lhsT=kf[:, j * 128:(j + 1) * 128],
                        rhs=Sbb[:, j * 1024 + h * NT: j * 1024 + (h + 1) * NT],
                        start=(j == 0), stop=(j == JD - 1)),
                        "MM" if (h == 1 and j == JD - 1) else None)
            mm_m1 = mm
            for j in range(JD):
                mm = p.op("tensor", lambda e, j=j: e.matmul(
                    out=pb[4][:, 0:128], lhsT=kf[:, j * 128:(j + 1) * 128],
                    rhs=kf[:, j * 128:(j + 1) * 128],
                    start=(j == 0), stop=(j == JD - 1)), "MM" if j == JD - 1 else None)
            mm_G = mm
            for j in range(JD):
                mm = p.op("tensor", lambda e, j=j: e.matmul(
                    out=pb[5][:, 0:128], lhsT=qf[:, j * 128:(j + 1) * 128],
                    rhs=kf[:, j * 128:(j + 1) * 128],
                    start=(j == 0), stop=(j == JD - 1)), "MM" if j == JD - 1 else None)
            mm_W = mm
            for h in range(2):
                for j in range(JD):
                    p.op("tensor", lambda e, h=h, j=j: e.matmul(
                        out=pb[2 + h][:], lhsT=qf[:, j * 128:(j + 1) * 128],
                        rhs=Sbb[:, j * 1024 + h * NT: j * 1024 + (h + 1) * NT],
                        start=(j == 0), stop=False))
            p.wait("vector", "MM", mm_G)
            ev_A = p.op("vector", lambda e, c=c: e.scalar_tensor_tensor(
                out=pwr[:, 0:128], in0=pb[4][:, 0:128], scalar=betc[:, c:c + 1],
                in1=maskl[:], op0=ALU.mult, op1=ALU.mult), "EV")
            p.wait("vector", "MM", mm_W)
            ev_W = p.op("vector", lambda e: e.tensor_mul(
                out=wtb[:], in0=pb[5][:, 0:128], in1=maski[:]), "EV")
            p.wait("vector", "MM", mm_m1)
            p.wait("vector", "EV", ev_vtm)
            p.op("vector", lambda e: e.tensor_sub(out=t1f[:, 0:NT], in0=vtm[:, 0:NT],
                                                  in1=pb[0][:]))
            p.op("vector", lambda e: e.tensor_sub(out=t1f[:, NT:2 * NT], in0=vtm[:, NT:2 * NT],
                                                  in1=pb[1][:]))
            ev_U0 = p.op("vector", lambda e, c=c: e.tensor_scalar_mul(
                out=ubuf[:], in0=t1f[:], scalar1=betc[:, c:c + 1]), "EV")
            p.wait("tensor", "EV", ev_A)
            ev_P = ev_A
            for m in range(6):
                p.wait("tensor", "EV", ev_P)
                mm_T = p.op("tensor", lambda e, m=m: e.transpose(
                    out=pb6b[:, m * 128:(m + 1) * 128], in_=pwr[:, m * 128:(m + 1) * 128],
                    identity=identb[:]), "MM")
                p.wait("vector", "MM", mm_T)
                ev_T = p.op("vector", lambda e, m=m: e.tensor_copy(
                    out=pwrT[:, m * 128:(m + 1) * 128],
                    in_=pb6b[:, m * 128:(m + 1) * 128]), "EV")
                p.wait("tensor", "EV", ev_T)
                mm_P = p.op("tensor", lambda e, m=m: e.matmul(
                    out=pb[7][:, 0:128], lhsT=pwrT[:, m * 128:(m + 1) * 128],
                    rhs=pwr[:, m * 128:(m + 1) * 128], start=True, stop=True), "MM")
                p.wait("vector", "MM", mm_P)
                ev_P = p.op("vector", lambda e, m=m: e.tensor_copy(
                    out=pwr[:, (m + 1) * 128:(m + 2) * 128], in_=pb[7][:, 0:128]), "EV")
            p.wait("tensor", "EV", ev_P)
            mm_T6 = p.op("tensor", lambda e: e.transpose(
                out=pb6b[:, 6 * 128:7 * 128], in_=pwr[:, 6 * 128:7 * 128],
                identity=identb[:]), "MM")
            p.wait("vector", "MM", mm_T6)
            ev_T6 = p.op("vector", lambda e: e.tensor_copy(
                out=pwrT[:, 6 * 128:7 * 128], in_=pb6b[:, 6 * 128:7 * 128]), "EV")
            ev_app = ev_U0
            for m in [6, 5, 4, 3, 2, 1, 0]:
                p.wait("tensor", "EV", ev_app if m != 6 else max(ev_app, ev_T6))
                if m == 6:
                    p.wait("tensor", "EV", ev_T6)
                mm1 = p.op("tensor", lambda e, m=m: e.matmul(
                    out=pb[7][:], lhsT=pwrT[:, m * 128:(m + 1) * 128],
                    rhs=ubuf[:, 0:NT], start=True, stop=True), "MM")
                mm2 = p.op("tensor", lambda e, m=m: e.matmul(
                    out=pb[0][:], lhsT=pwrT[:, m * 128:(m + 1) * 128],
                    rhs=ubuf[:, NT:2 * NT], start=True, stop=True), "MM")
                op = ALU.add if m > 0 else ALU.subtract
                p.wait("vector", "MM", mm2)
                p.op("vector", lambda e, op=op: e.tensor_tensor(
                    out=ubuf[:, 0:NT], in0=ubuf[:, 0:NT], in1=pb[7][:], op=op))
                ev_app = p.op("vector", lambda e, op=op: e.tensor_tensor(
                    out=ubuf[:, NT:2 * NT], in0=ubuf[:, NT:2 * NT], in1=pb[0][:], op=op), "EV")
            ev_Ufinal = ev_app
            p.wait("tensor", "EV", max(ev_W, ev_T6))
            mm_wt = p.op("tensor", lambda e: e.transpose(
                out=pb6b[:, 7 * 128:8 * 128], in_=wtb[:], identity=identb[:]), "MM")
            p.wait("vector", "MM", mm_wt)
            ev_wtT = p.op("vector", lambda e: e.tensor_copy(
                out=wtT[:], in_=pb6b[:, 7 * 128:8 * 128]), "EV")
            p.wait("tensor", "EV", max(ev_Ufinal, ev_wtT))
            p.op("tensor", lambda e: e.matmul(out=pb[2][:], lhsT=wtT[:], rhs=ubuf[:, 0:NT],
                                              start=False, stop=True))
            mm_O = p.op("tensor", lambda e: e.matmul(out=pb[3][:], lhsT=wtT[:],
                                                     rhs=ubuf[:, NT:2 * NT],
                                                     start=False, stop=True), "MM")
            loads_free_mm = mm_O
            p.wait("vector", "MM", mm_O)
            p.op("vector", lambda e: e.tensor_copy(out=otm[:, 0:NT], in_=pb[2][:]))
            ev_otm = p.op("vector", lambda e: e.tensor_copy(out=otm[:, NT:2 * NT],
                                                            in_=pb[3][:]), "EV")
            p.wait("tensor", "EV", max(ev_otm, ev_wtT))
            for j in range(JD):
                mm = p.op("tensor", lambda e, j=j: e.transpose(
                    out=pb6b[:, j * 128:(j + 1) * 128], in_=otm[:, j * 128:(j + 1) * 128],
                    identity=identb[:]), "MM" if j == JD - 1 else None)
            mm_otr = mm
            p.wait("vector", "MM", mm_otr)
            p.dwait("vector", "ST")
            ev_ofm = p.op("vector", lambda e: e.tensor_copy(out=ofm[:], in_=pb6b[:]), "EV")
            p.wait("sync", "EV", ev_ofm)
            st_o[c] = p.dma("sync", osc[:, :, cs].rearrange("j p t -> p j t"),
                            ofm.rearrange("p (j t) -> p j t", j=JD), "ST")
            ev_sf = {}
            for j in range(JD):
                pwa, pwb = (pb[0], pb[1]) if j % 2 == 0 else (pb[4], pb[5])
                if j >= 2:
                    p.wait("tensor", "EV", ev_sf[j - 2])
                elif j == 0:
                    p.wait("tensor", "EV", ev_Ufinal)
                mm = 0
                for h in range(2):
                    mm = p.op("tensor", lambda e, j=j, h=h, pwa=pwa, pwb=pwb: e.matmul(
                        out=(pwa if h == 0 else pwb)[:], lhsT=ktm[:, j * 128:(j + 1) * 128],
                        rhs=ubuf[:, h * NT:(h + 1) * NT], start=True, stop=True),
                        "MM" if h == 1 else None)
                p.wait("vector", "MM", mm)
                p.op("vector", lambda e, j=j, pwa=pwa: e.tensor_add(
                    out=Sf[:, j * 1024: j * 1024 + NT], in0=Sf[:, j * 1024: j * 1024 + NT],
                    in1=pwa[:]))
                ev_sf[j] = p.op("vector", lambda e, j=j, pwb=pwb: e.tensor_add(
                    out=Sf[:, j * 1024 + NT: (j + 1) * 1024],
                    in0=Sf[:, j * 1024 + NT: (j + 1) * 1024], in1=pwb[:]), "EV")
                p.wait("scalar", "EV", ev_sf[j])
                es_sb = p.op("scalar", lambda e, j=j: e.activation(
                    out=Sbb[:, j * 1024:(j + 1) * 1024], in_=Sf[:, j * 1024:(j + 1) * 1024],
                    func=AF.Copy), "ES")
            sb_ready = ("ES", es_sb)
            ev_sf_last = ev_sf[JD - 1]
        mm_deltadone = p.tally["MM"]
        ev_deltadone = p.tally["EV"]
        es_deltadone = p.tally["ES"]
        st_odone = p.tally["ST"]

        wb_after = {}
        xnew3 = xnew.rearrange("p (j t) -> p j t", j=JD)
        xresC3 = xresC.rearrange("p (j t) -> p j t", j=JD)
        h2st3 = h2st.rearrange("p (j t) -> p j t", j=JD)
        wb_free_mm = mm_deltadone
        wb_free_wb = 0
        ost_free = (mm_deltadone, 0)
        xnew_free_ev = 0
        mbuf_free_mm = 0
        st_y = {}
        prev_sq = 0
        psAB_free = p.tally["EV"]
        psCD_free = p.tally["EV"]
        p.wait("vector", "EV", psAB_free)

        def wchunk(tag, o, wsrc, wc, ktiles, slot, free_mm, s, lsem, wsem):
            if free_mm:
                p.wait("gpsimd", "MM", free_mm)
            if s == 0:
                if (tag, o - 2) in wb_after:
                    p.dwait("gpsimd", wsem)
                view3 = slot.rearrange("p (k o) -> p k o", k=ktiles)
                src = wsrc[:, o * 128:(o + 1) * 128].rearrange("(k p) o -> p k o", p=128)
                p.dma("gpsimd", view3, src, lsem)
                p.dwait("sync", lsem)
                wb_after[(tag, o)] = p.dma("sync", wc[o], slot, wsem)
            else:
                p.dwait("gpsimd", wsem)
                p.dma("gpsimd", slot, wc[o], lsem)

        for s in range(NS):
            p.wait("sync", "MM", ost_free[0])
            p.wait("sync", "EV", xnew_free_ev)
            lo = p.dma("sync", ostripe.rearrange("p (j t) -> p j t", j=JD),
                       osc[:, :, s * S:(s + 1) * S].rearrange("j p t -> p j t"), "LX")
            lxr = p.dma("sync", xresC3, xisc[:, s * S:(s + 1) * S].rearrange(
                "(j p) t -> p j t", p=128), "LX")
            g1_done = {}
            ev_g1 = {}
            for o in range(JD):
                lsem, wsem = ("LWA", "WBA") if o % 2 == 0 else ("LWB", "WBB")
                slot = (wbA if o % 2 == 0 else wbB)[:, 0:JD * 128]
                wchunk(1, o, wot, wco, JD, slot, g1_done.get(o - 2, wb_free_mm), s,
                       lsem, wsem)
                p.dwait("tensor", lsem)
                if o == 0:
                    p.dwait("tensor", "LX")
                mm = 0
                for tb in range(TB):
                    i = o * TB + tb
                    pw = pb[i % 4]
                    if i >= 4:
                        p.wait("tensor", "EV", ev_g1[i - 4])
                    for k in range(JD):
                        mm = p.op("tensor", lambda e, k=k, tb=tb, slot=slot, pw=pw: e.matmul(
                            out=pw[:], lhsT=slot[:, k * 128:(k + 1) * 128],
                            rhs=ostripe[:, k * S + tb * NT: k * S + (tb + 1) * NT],
                            start=(k == 0), stop=(k == JD - 1)),
                            "MM" if k == JD - 1 else None)
                    p.wait("vector", "MM", mm)
                    p.dwait("vector", "LX")
                    if i == 0:
                        p.dwait("vector", "ST")
                    ev_g1[i] = p.op("vector", lambda e, o=o, tb=tb, pw=pw: e.tensor_add(
                        out=xnew[:, o * S + tb * NT: o * S + (tb + 1) * NT],
                        in0=xresC[:, o * S + tb * NT: o * S + (tb + 1) * NT], in1=pw[:]), "EV")
                g1_done[o] = mm
            mm_c1_done = p.tally["MM"]
            ev_c1_all = ev_g1[JD * TB - 1]
            wb_free_mm = mm_c1_done
            p.wait("sync", "EV", ev_c1_all)
            lxr2 = p.dma("sync", xresC3, x[:, s * S:(s + 1) * S].rearrange(
                "(j p) t -> p j t", p=128), "LX")
            vsqC3 = hb_f[:, 8192:8192 + JD * NT].rearrange("p (j t) -> p j t", j=JD)
            ev_h2 = {}
            for tb in range(TB):
                src3 = xnew3[:, :, tb * NT:(tb + 1) * NT]
                dst3 = h2st3[:, :, tb * NT:(tb + 1) * NT]
                mm_stats, ev_done = _norm_block(
                    p, pb, src3, vsqC3, dst3, ones_c, ones_r, rows, ceps, 1, tmpNT, n2c,
                    False, prev_sq, psAB_free, psCD_free,
                    waits_t=[("EV", ev_c1_all), ("MM", mm_c1_done)],
                    waits_s=[("EV", ev_c1_all)])
                prev_sq = mm_stats
                psAB_free = ev_done
                psCD_free = ev_done
                ev_h2[tb] = ev_done
            ev_h2_all = ev_h2[TB - 1]
            es_g3 = {}
            ev_g3 = {}
            g3_done = {}
            for o in range(JH2):
                slotg = wbA[:, (o % 2) * JD * 128:(o % 2 + 1) * JD * 128]
                slotu = wbB[:, (o % 2) * JD * 128:(o % 2 + 1) * JD * 128]
                wchunk(3, o, wgt, wcg, JD, slotg, g3_done.get(o - 2, wb_free_mm), s,
                       "LWA", "WBA")
                wchunk(4, o, wut, wcu, JD, slotu, 0, s, "LWB", "WBB")
                p.dwait("tensor", "LWA")
                p.dwait("tensor", "LWB")
                if o == 0:
                    p.wait("tensor", "EV", ev_h2_all)
                for tb in range(TB):
                    i = o * TB + tb
                    pw1, pw2 = pb[i % 2], pb[2 + i % 2]
                    if i >= 2:
                        p.wait("tensor", "ES", es_g3[i - 2])
                        p.wait("tensor", "EV", ev_g3[i - 2])
                    for k in range(JD):
                        p.op("tensor", lambda e, k=k, tb=tb, slotg=slotg, pw1=pw1: e.matmul(
                            out=pw1[:], lhsT=slotg[:, k * 128:(k + 1) * 128],
                            rhs=h2st[:, k * S + tb * NT: k * S + (tb + 1) * NT],
                            start=(k == 0), stop=(k == JD - 1)))
                    mm = 0
                    for k in range(JD):
                        mm = p.op("tensor", lambda e, k=k, tb=tb, slotu=slotu, pw2=pw2:
                                  e.matmul(
                            out=pw2[:], lhsT=slotu[:, k * 128:(k + 1) * 128],
                            rhs=h2st[:, k * S + tb * NT: k * S + (tb + 1) * NT],
                            start=(k == 0), stop=(k == JD - 1)),
                            "MM" if k == JD - 1 else None)
                    p.wait("scalar", "MM", mm)
                    es_g3[i] = p.op("scalar", lambda e, i=i, pw1=pw1: e.activation(
                        out=abuf[:, (i % 2) * NT:(i % 2 + 1) * NT], in_=pw1[:],
                        func=AF.Silu), "ES")
                    p.wait("vector", "MM", mm)
                    p.wait("vector", "ES", es_g3[i])
                    if i == 0:
                        p.wait("vector", "MM", mbuf_free_mm)
                    ev_g3[i] = p.op("vector", lambda e, i=i, o=o, tb=tb, pw2=pw2:
                                    e.tensor_mul(
                        out=mbuf[:, o * S + tb * NT: o * S + (tb + 1) * NT],
                        in0=abuf[:, (i % 2) * NT:(i % 2 + 1) * NT], in1=pw2[:]), "EV")
                g3_done[o] = mm
            ev_c3_all = ev_g3[JH2 * TB - 1]
            mm_c3_done = p.tally["MM"]
            wb_free_mm = mm_c3_done
            g4_done = {}
            ev_g4 = {}
            for o in range(JD):
                lsem, wsem = ("LWA", "WBA") if o % 2 == 0 else ("LWB", "WBB")
                slot = (wbA if o % 2 == 0 else wbB)[:, 0:JH2 * 128]
                wchunk(5, o, wdt, wcd, JH2, slot, g4_done.get(o - 2, wb_free_mm), s,
                       lsem, wsem)
                p.dwait("tensor", lsem)
                if o == 0:
                    p.wait("tensor", "EV", ev_c3_all)
                for tb in range(TB):
                    i = o * TB + tb
                    pw = pb[i % 4]
                    if i >= 4:
                        p.wait("tensor", "EV", ev_g4[i - 4])
                    mm = 0
                    for k in range(JH2):
                        mm = p.op("tensor", lambda e, k=k, tb=tb, slot=slot, pw=pw: e.matmul(
                            out=pw[:], lhsT=slot[:, k * 128:(k + 1) * 128],
                            rhs=mbuf[:, k * S + tb * NT: k * S + (tb + 1) * NT],
                            start=(k == 0), stop=(k == JH2 - 1)),
                            "MM" if k == JH2 - 1 else None)
                    p.wait("vector", "MM", mm)
                    p.dwait("vector", "LX")
                    p.op("vector", lambda e, o=o, tb=tb, pw=pw: e.tensor_add(
                        out=xnew[:, o * S + tb * NT: o * S + (tb + 1) * NT],
                        in0=xnew[:, o * S + tb * NT: o * S + (tb + 1) * NT], in1=pw[:]))
                    ev_g4[i] = p.op("vector", lambda e, o=o, tb=tb: e.tensor_add(
                        out=xnew[:, o * S + tb * NT: o * S + (tb + 1) * NT],
                        in0=xnew[:, o * S + tb * NT: o * S + (tb + 1) * NT],
                        in1=xresC[:, o * S + tb * NT: o * S + (tb + 1) * NT]), "EV")
                    p.wait("sync", "EV", ev_g4[i])
                    st_y[(s, i)] = p.dma(
                        "sync", y[o * 128:(o + 1) * 128, s * S + tb * NT: s * S + (tb + 1) * NT],
                        xnew[:, o * S + tb * NT: o * S + (tb + 1) * NT], "ST")
                g4_done[o] = mm
            mm_c4_done = p.tally["MM"]
            wb_free_mm = mm_c4_done
            mbuf_free_mm = mm_c4_done
            ost_free = (mm_c3_done, 0)
            xnew_free_ev = p.tally["EV"]

        p.dwait("sync", "ST")
        p.dwait("sync", "WBA")
        p.dwait("sync", "WBB")


def build_l0attn():
    nc = bass.Bass("TRN2", target_bir_lowering=False, detect_race_conditions=RACE_DETECT)
    x = nc.dram_tensor("x", [D, T], F32, kind="ExternalInput").ap()
    y = nc.dram_tensor("y", [D, T], F32, kind="ExternalOutput").ap()
    with contextlib.ExitStack() as es:
        p = Prog(nc)
        p.make_sems(es, ALLSEMS)
        Cx = Ctx(nc, p, es)
        emit_l0attn(Cx, x, y, wp="", first=True)
        with nc.Block() as block:
            p.replay(block)
    return nc


def layout_l0attn(inp):
    f = np.float32
    tril = np.tril(np.ones((128, 128), f), -1)
    trili = np.tril(np.ones((128, 128), f), 0)
    return dict(
        wqt=np.ascontiguousarray(np.asarray(inp["L0_attn_Wq"], f)),
        wkt=np.ascontiguousarray(np.asarray(inp["L0_attn_Wk"], f)),
        wvt=np.ascontiguousarray(np.asarray(inp["L0_attn_Wv"], f)),
        wot=np.ascontiguousarray(np.asarray(inp["L0_attn_Wo"], f)),
        wgt=np.ascontiguousarray(np.asarray(inp["L0_attn_w_gate"], f)),
        wut=np.ascontiguousarray(np.asarray(inp["L0_attn_w_up"], f)),
        wdt=np.ascontiguousarray(np.asarray(inp["L0_attn_w_down"], f)),
        n1v=_colmajor(inp["L0_attn_norm1_w"]),
        n2v=_colmajor(inp["L0_attn_norm2_w"]),
        alv=_colmajor(inp["L0_attn_alpha_logit"]),
        bwv=_colmajor(np.asarray(inp["L0_attn_beta_w"], f)[:, 0]),
        bbv=np.asarray(inp["L0_attn_beta_b"], f).reshape(1, 1),
        msl=tril, mli=trili,
    )


def emit_l1attn(C, xin, xout, wp="a1", first=True):
    NH = 8
    DK = 128
    DC = 32
    DR = 32
    SCALE = 1.0 / float(np.sqrt(DK + DR))

    nc = C.nc
    p = C.p
    nm = (wp + "_") if wp else ""
    x, y = xin, xout
    wc96d = C.dram(nm + "wc96", [D, 96])
    wuqd = C.dram(nm + "wuq", [DC, D])
    wukd = C.dram(nm + "wuk", [DC, D])
    wuvd = C.dram(nm + "wuv", [DC, D])
    wqrd = C.dram(nm + "wqr", [DC, NH * DR])
    wod = C.dram(nm + "wo", [D, D])
    bc96d = C.dram(nm + "bc96", [96, 1])
    buqd = C.dram(nm + "buq", [128, NH])
    bukd = C.dram(nm + "buk", [128, NH])
    buvd = C.dram(nm + "buv", [1, D])
    bqrd = C.dram(nm + "bqr", [DR, NH])
    bod = C.dram(nm + "bo", [128, JD])
    osc = C.dram(nm + "osc", [NH, 128, T], BF16, "Internal")

    if True:
        xs = C.xs
        arena = C.arena
        hbt = C.hb
        hb_f = C.hb.bitcast(F32)
        cq = hbt[0:32, 0:T]
        xitb = hbt[:, 4096:8192]
        akt = hbt[:, 8192:9216]
        ostg = hbt[:, 9216:10240]
        ybuf = hb_f[:, 5120:6144]
        sq = hb_f[:, 6144:6656]
        vsq = hb_f[:, 8192:12288]
        wo = C.wbA
        wc96 = C.wbB[:, 0:JD * 96]
        wuq = C.wbB[0:DC, 768:768 + D]
        wuk = C.wbB[0:DC, 1792:1792 + D]
        wuv = C.wbB[0:DC, 2816:2816 + D]
        wqr = C.wbB[0:DC, 3840:3840 + NH * DR]
        bc96 = C.sb(nm + "bc96s", [96, 1], F32)
        buq = C.sb(nm + "buqs", [128, NH], F32)
        buk = C.sb(nm + "buks", [128, NH], F32)
        buv = C.abuf[0:1, 0:D]
        bqr = C.sb(nm + "bqrs", [DR, NH], F32)
        bo = C.sb(nm + "bos", [128, JD], F32)
        bvh = hb_f[:, 7296:7424]
        ones_r = C.ones_r
        ones_c = C.ones_c
        ones128 = hbt[:, 14848:14976]
        km33 = hb_f[0:33, 7168:7170]
        rowb33 = hb_f[0:33, 6656:7168]
        rmean, rvar, rt2, rA = C.rmean, C.rvar, C.rt2, C.rA
        ceps = C.ceps
        pb = C.pb
        rows = (rmean, rvar, rt2, rA, rt2)

        cslab = arena[0:96, 0:T]
        kr2 = arena[0:33, T:2 * T]
        qh = arena[:, 2 * T:3 * T]
        qr2 = arena[0:33, 3 * T:4 * T]
        kch = arena[:, 4 * T:5 * T]
        v_tm = arena[:, 5 * T:6 * T]
        arena_f = arena.bitcast(F32)
        ostripe = arena[:, 0:JD * 1024]
        xresC = arena_f[:, 4096:4096 + JD * 1024]

        C.init_common()

        p.wait("gpsimd", "MM", p.tally["MM"])
        p.dwait("gpsimd", "WBA")
        p.dwait("gpsimd", "WBB")
        p.dma("gpsimd", wc96.rearrange("p (k o) -> p k o", k=JD),
              wc96d.rearrange("(k p) o -> p k o", p=128), "LWA")
        p.dma("gpsimd", wuq, wuqd, "LWA")
        p.dma("gpsimd", wuk, wukd, "LWA")
        p.dma("gpsimd", wuv, wuvd, "LWA")
        p.dma("gpsimd", wqr, wqrd, "LWA")
        p.dma("gpsimd", wo[:].rearrange("p (k o) -> p k o", k=JD),
              wod.rearrange("(k p) o -> p k o", p=128), "LWA")
        for dst, srcd in ((bc96, bc96d), (buq, buqd), (buk, bukd), (buv, buvd),
                          (bqr, bqrd), (bo, bod)):
            p.dma("sync", dst[:], srcd, "LX")
        p.op("vector", lambda e: e.memset(ones128, 1.0))
        p.dwait("vector", "LX")
        p.op("vector", lambda e: e.tensor_scalar_mul(out=buq[:], in0=buq[:], scalar1=SCALE))
        p.op("vector", lambda e: e.tensor_scalar_mul(out=bqr[:], in0=bqr[:], scalar1=SCALE))
        p.op("vector", lambda e: e.drain())
        ev_init = p.op("vector", lambda e: e.nop(), "EV")
        p.dwait("tensor", "LWA")

        xs3 = xs[:].rearrange("p (j t) -> p j t", j=JD)
        xi3 = xitb.rearrange("p (j t) -> p j t", j=JD)
        vsq3 = vsq.rearrange("p (j t) -> p j t", j=JD)
        xs_free = (0, ev_init)
        psAB_free = ev_init
        psCD_free = ev_init
        prev_sq = 0
        pbfree = {i: 0 for i in range(8)}
        pbfree_es = {}
        ev_cp = {}
        for tb in range(T // NT):
            t0 = tb * NT
            p.wait("sync", "MM", xs_free[0])
            p.wait("sync", "EV", xs_free[1])
            lx_xs = p.dma("sync", xs3, x[:, t0:t0 + NT].rearrange("(j p) t -> p j t", p=128),
                          "LX")
            mm_ln, ev_ln = _norm_block(
                p, pb, xs3, vsq3, xi3, ones_c, ones_r, rows, ceps, 0, sq[:, 0:NT], None,
                True, prev_sq, psAB_free, psCD_free,
                waits_t=[("LX", lx_xs), ("EV", ev_init)], waits_s=[("LX", lx_xs)])
            prev_sq = mm_ln
            psAB_free = ev_ln
            psCD_free = ev_ln
            xs_free = (mm_ln, ev_ln)
            pw = pb[tb % 2]
            p.wait("tensor", "EV", max(pbfree[tb % 2], ev_ln))
            mm = 0
            for k in range(JD):
                mm = p.op("tensor", lambda e, k=k, pw=pw: e.matmul(
                    out=pw[0:96, :], lhsT=wc96[:, k * 96:(k + 1) * 96],
                    rhs=xitb[:, k * NT:(k + 1) * NT],
                    start=(k == 0), stop=(k == JD - 1)), "MM" if k == JD - 1 else None)
            p.wait("vector", "MM", mm)
            ev = p.op("vector", lambda e, tb=tb, pw=pw: e.tensor_scalar_add(
                out=cslab[:, tb * NT:(tb + 1) * NT].to_broadcast([96, NT]) if False
                else cslab[:, tb * NT:(tb + 1) * NT],
                in0=pw[0:96, :], scalar1=bc96[:]), "EV")
            pbfree[tb % 2] = ev
            ev_cp[tb] = ev
        ev_cp_all = ev_cp[T // NT - 1]
        pbfree[4] = psAB_free
        pbfree[5] = psAB_free
        pbfree[6] = psCD_free
        pbfree[7] = psCD_free
        p.wait("sync", "EV", ev_cp_all)
        p.dma("sync", kr2[0:32, :], cslab[32:64, :], "LX")
        p.dma("sync", cq, cslab[64:96, :], "LX")
        p.dwait("vector", "LX")
        ev_kr = p.op("vector", lambda e: e.memset(kr2[32:33, :], 1.0), "EV")

        st_osc = {}
        for h in range(NH):
            hs = slice(h * 128, (h + 1) * 128)
            p.wait("tensor", "EV", max(pbfree[6], ev_kr))
            mm_bv = p.op("tensor", lambda e, h=h: e.matmul(
                out=pb[6][:, 0:128], lhsT=ones_r[:], rhs=buv[0:1, h * 128:(h + 1) * 128],
                start=True, stop=True), "MM")
            p.wait("vector", "MM", mm_bv)
            ev_bvh = p.op("vector", lambda e: e.tensor_copy(out=bvh, in_=pb[6][:, 0:128]),
                          "EV")
            pbfree[6] = ev_bvh
            p.op("vector", lambda e: e.memset(km33[32:33, 0:1], 0.0))
            es_k = {}
            for kb in range(T // NT):
                pw = pb[kb % 2]
                p.wait("tensor", "ES", pbfree_es.get(kb % 2, 0))
                p.wait("tensor", "EV", pbfree[kb % 2])
                mm = p.op("tensor", lambda e, h=h, kb=kb, pw=pw: e.matmul(
                    out=pw[:], lhsT=wuk[:, h * 128:(h + 1) * 128],
                    rhs=cslab[0:32, kb * NT:(kb + 1) * NT], start=True, stop=True), "MM")
                p.wait("scalar", "MM", mm)
                es1 = p.op("scalar", lambda e, h=h, kb=kb, pw=pw: e.activation(
                    out=kch[:, kb * NT:(kb + 1) * NT], in_=pw[:], func=AF.Identity,
                    bias=buk[:, h:h + 1]), "ES")
                pbfree_es[kb % 2] = es1
                es2 = p.op("scalar", lambda e, kb=kb: e.activation(
                    out=sq[:, 0:NT], in_=kch[:, kb * NT:(kb + 1) * NT], func=AF.Square),
                    "ES")
                p.wait("tensor", "ES", es2)
                p.wait("tensor", "EV", pbfree[4])
                mmq = p.op("tensor", lambda e: e.matmul(
                    out=pb[4][32:33, :], lhsT=ones_c[:], rhs=sq[:, 0:NT],
                    start=True, stop=True), "MM")
                p.wait("vector", "MM", mmq)
                p.op("vector", lambda e: e.tensor_reduce(
                    out=km33[32:33, 1:2], in_=pb[4][32:33, :], axis=AX.X, op=ALU.max))
                ev = p.op("vector", lambda e: e.tensor_tensor(
                    out=km33[32:33, 0:1], in0=km33[32:33, 0:1], in1=km33[32:33, 1:2],
                    op=ALU.max), "EV")
                pbfree[4] = ev
            p.wait("scalar", "EV", pbfree[4])
            es_km = p.op("scalar", lambda e: e.activation(
                out=km33[32:33, 0:1], in_=km33[32:33, 0:1], func=AF.Sqrt), "ES")
            for kt in range(T // 128):
                pw = pb[kt % 2]
                p.wait("tensor", "ES", pbfree_es.get(kt % 2, 0))
                p.wait("tensor", "EV", pbfree[kt % 2])
                mm = p.op("tensor", lambda e, h=h, kt=kt, pw=pw: e.matmul(
                    out=pw[:, 0:128], lhsT=cslab[0:32, kt * 128:(kt + 1) * 128],
                    rhs=wuv[:, h * 128:(h + 1) * 128], start=True, stop=True), "MM")
                p.wait("vector", "MM", mm)
                p.wait("vector", "EV", ev_bvh)
                ev = p.op("vector", lambda e, kt=kt, pw=pw: e.tensor_add(
                    out=v_tm[:, kt * 128:(kt + 1) * 128], in0=pw[:, 0:128], in1=bvh),
                    "EV")
                pbfree[kt % 2] = ev
                pbfree_es[kt % 2] = 0
            p.wait("vector", "ES", es_km)
            for qb in range(T // NT):
                pw = pb[qb % 2]
                pw2 = pb[2 + qb % 2]
                p.wait("tensor", "ES", pbfree_es.get(qb % 2, 0))
                p.wait("tensor", "EV", pbfree[qb % 2])
                mm1 = p.op("tensor", lambda e, h=h, qb=qb, pw=pw: e.matmul(
                    out=pw[:], lhsT=wuq[:, h * 128:(h + 1) * 128],
                    rhs=cq[:, qb * NT:(qb + 1) * NT], start=True, stop=True), "MM")
                p.wait("scalar", "MM", mm1)
                es1 = p.op("scalar", lambda e, h=h, qb=qb, pw=pw: e.activation(
                    out=qh[:, qb * NT:(qb + 1) * NT], in_=pw[:], func=AF.Identity,
                    scale=SCALE, bias=buq[:, h:h + 1]), "ES")
                pbfree_es[qb % 2] = es1
                es2 = p.op("scalar", lambda e, qb=qb: e.activation(
                    out=sq[:, 0:NT], in_=qh[:, qb * NT:(qb + 1) * NT], func=AF.Square), "ES")
                p.wait("tensor", "ES", es2)
                p.wait("tensor", "EV", pbfree[4])
                p.op("tensor", lambda e: e.matmul(
                    out=pb[4][32:33, :], lhsT=ones_c[:], rhs=sq[:, 0:NT],
                    start=True, stop=False))
                p.wait("tensor", "ES", pbfree_es.get(2 + qb % 2, 0))
                p.wait("tensor", "EV", pbfree[2 + qb % 2])
                mm2 = p.op("tensor", lambda e, h=h, qb=qb, pw2=pw2: e.matmul(
                    out=pw2[0:32, :], lhsT=wqr[:, h * DR:(h + 1) * DR],
                    rhs=cq[:, qb * NT:(qb + 1) * NT], start=True, stop=True), "MM")
                p.wait("scalar", "MM", mm2)
                es3 = p.op("scalar", lambda e, h=h, qb=qb, pw2=pw2: e.activation(
                    out=qr2[0:32, qb * NT:(qb + 1) * NT], in_=pw2[0:32, :], func=AF.Identity,
                    scale=SCALE, bias=bqr[:, h:h + 1]), "ES")
                pbfree_es[2 + qb % 2] = es3
                es4 = p.op("scalar", lambda e, qb=qb: e.activation(
                    out=sq[0:32, 0:NT], in_=qr2[0:32, qb * NT:(qb + 1) * NT],
                    func=AF.Square), "ES")
                p.wait("tensor", "ES", es4)
                mm3 = p.op("tensor", lambda e: e.matmul(
                    out=pb[4][32:33, :], lhsT=ones_c[0:32, :], rhs=sq[0:32, 0:NT],
                    start=False, stop=True), "MM")
                p.wait("scalar", "MM", mm3)
                es5 = p.op("scalar", lambda e: e.activation(
                    out=rowb33[32:33, :], in_=pb[4][32:33, :], func=AF.Sqrt), "ES")
                p.wait("vector", "ES", es5)
                ev = p.op("vector", lambda e, qb=qb: e.tensor_scalar(
                    out=qr2[32:33, qb * NT:(qb + 1) * NT], in0=rowb33[32:33, :],
                    scalar1=km33[32:33, 0:1], scalar2=-1.0,
                    op0=ALU.mult, op1=ALU.mult), "EV")
                pbfree[4] = ev
            NKT = T // 128
            for qb in range(T // NT):
                p.wait("tensor", "EV", max(pbfree[2], pbfree[3]))
                es_exp = {}
                mm_av = {}
                for kt in range(NKT + 1):
                    if kt < NKT:
                        pw = pb[kt % 2]
                        p.wait("tensor", "ES", pbfree_es.get(kt % 2, 0))
                        p.wait("tensor", "EV", pbfree[kt % 2])
                        p.op("tensor", lambda e, h=h, kt=kt, qb=qb, pw=pw: e.matmul(
                            out=pw[:], lhsT=kch[:, kt * 128:(kt + 1) * 128],
                            rhs=qh[:, qb * NT:(qb + 1) * NT], start=True, stop=False))
                        mm = p.op("tensor", lambda e, h=h, kt=kt, qb=qb, pw=pw: e.matmul(
                            out=pw[:], lhsT=kr2[:, kt * 128:(kt + 1) * 128],
                            rhs=qr2[:, qb * NT:(qb + 1) * NT], start=False, stop=True), "MM")
                        p.wait("scalar", "MM", mm)
                        ese = p.op("scalar", lambda e, kt=kt, pw=pw: e.activation(
                            out=akt[:, (kt % 2) * NT:(kt % 2 + 1) * NT], in_=pw[:],
                            func=AF.Exp), "ES")
                        es_exp[kt] = ese
                        pbfree_es[kt % 2] = ese
                        pbfree[kt % 2] = 0
                    if kt >= 1:
                        ka = kt - 1
                        p.wait("tensor", "ES", es_exp[ka])
                        p.op("tensor", lambda e, ka=ka: e.matmul(
                            out=pb[2][:], lhsT=v_tm[:, ka * 128:(ka + 1) * 128],
                            rhs=akt[:, (ka % 2) * NT:(ka % 2 + 1) * NT],
                            start=(ka == 0), stop=(ka == NKT - 1)))
                        mm_av[ka] = p.op("tensor", lambda e, ka=ka: e.matmul(
                            out=pb[3][:], lhsT=ones128,
                            rhs=akt[:, (ka % 2) * NT:(ka % 2 + 1) * NT],
                            start=(ka == 0), stop=(ka == NKT - 1)), "MM")
                mm_done = mm_av[NKT - 1]
                p.wait("vector", "MM", mm_done)
                p.dwait("vector", "ST")
                p.op("vector", lambda e, qb=qb: e.reciprocal(
                    out=ybuf[:, 0:NT], in_=pb[3][:]))
                ev_o = p.op("vector", lambda e, qb=qb: e.tensor_mul(
                    out=ostg[:, (qb % 2) * NT:(qb % 2 + 1) * NT], in0=pb[2][:],
                    in1=ybuf[:, 0:NT]), "EV")
                pbfree[2] = ev_o
                pbfree[3] = ev_o
                p.wait("sync", "EV", ev_o)
                st_osc[(h, qb)] = p.dma(
                    "sync", osc[h, :, qb * NT:(qb + 1) * NT],
                    ostg[:, (qb % 2) * NT:(qb % 2 + 1) * NT], "ST")
        mm_attn_done = p.tally["MM"]

        ost_free = (mm_attn_done, 0)
        ev_g = {}
        for s in range(4):
            S = 1024
            p.wait("sync", "MM", ost_free[0])
            p.wait("sync", "EV", ost_free[1])
            p.dwait("sync", "ST")
            lo = p.dma("sync", ostripe.rearrange("p (j t) -> p j t", j=JD),
                       osc[:, :, s * S:(s + 1) * S].rearrange("j p t -> p j t"), "LX")
            lxr = p.dma("sync", xresC.rearrange("p (j t) -> p j t", j=JD),
                        x[:, s * S:(s + 1) * S].rearrange("(j p) t -> p j t", p=128), "LX")
            for o in range(JD):
                for tb in range(2):
                    i = o * 2 + tb
                    pw = pb[4 + i % 4]
                    if i == 0:
                        p.dwait("tensor", "LX")
                    if i >= 4:
                        p.wait("tensor", "EV", ev_g[(s, i - 4)])
                    elif s > 0:
                        p.wait("tensor", "EV", ev_g[(s - 1, 12 + i)])
                    else:
                        p.wait("tensor", "EV", pbfree[4 + i % 4])
                    mm = 0
                    for k in range(JD):
                        mm = p.op("tensor", lambda e, k=k, o=o, tb=tb, pw=pw, s=s: e.matmul(
                            out=pw[:], lhsT=wo[:, k * 1024 + o * 128: k * 1024 + (o + 1) * 128],
                            rhs=ostripe[:, k * S + tb * NT: k * S + (tb + 1) * NT],
                            start=(k == 0), stop=(k == JD - 1)),
                            "MM" if k == JD - 1 else None)
                    p.wait("vector", "MM", mm)
                    p.dwait("vector", "LX")
                    p.dwait("vector", "ST")
                    ev = p.op("vector", lambda e, o=o, tb=tb, pw=pw, i=i: e.scalar_tensor_tensor(
                        out=ybuf[:, (i % 2) * NT:(i % 2 + 1) * NT], in0=pw[:],
                        scalar=bo[:, o:o + 1],
                        in1=xresC[:, o * S + tb * NT: o * S + (tb + 1) * NT],
                        op0=ALU.add, op1=ALU.add), "EV")
                    ev_g[(s, i)] = ev
                    p.wait("sync", "EV", ev)
                    p.dma("sync", y[o * 128:(o + 1) * 128, s * S + tb * NT: s * S + (tb + 1) * NT],
                          ybuf[:, (i % 2) * NT:(i % 2 + 1) * NT], "ST")
            ost_free = (p.tally["MM"], p.tally["EV"])

        p.dwait("sync", "ST")


def build_l1attn():
    nc = bass.Bass("TRN2", target_bir_lowering=False, detect_race_conditions=RACE_DETECT)
    x = nc.dram_tensor("x", [D, T], F32, kind="ExternalInput").ap()
    y = nc.dram_tensor("y", [D, T], F32, kind="ExternalOutput").ap()
    with contextlib.ExitStack() as es:
        p = Prog(nc)
        p.make_sems(es, ALLSEMS)
        Cx = Ctx(nc, p, es)
        emit_l1attn(Cx, x, y, wp="", first=True)
        with nc.Block() as block:
            p.replay(block)
    return nc


def layout_l1attn(inp):
    f = np.float32
    g = lambda k: np.asarray(inp[k], f)
    wc96 = np.hstack([g("L1_attn_W_DKV"), g("L1_attn_W_KR"), g("L1_attn_W_DQ")])
    bc96 = np.concatenate([g("L1_attn_b_DKV"), g("L1_attn_b_KR"), g("L1_attn_b_DQ")])
    return dict(
        wc96=np.ascontiguousarray(wc96),
        wuq=np.ascontiguousarray(g("L1_attn_W_UQ")),
        wuk=np.ascontiguousarray(g("L1_attn_W_UK")),
        wuv=np.ascontiguousarray(g("L1_attn_W_UV")),
        wqr=np.ascontiguousarray(g("L1_attn_W_QR")),
        wo=np.ascontiguousarray(g("L1_attn_W_O")),
        bc96=np.ascontiguousarray(bc96.reshape(96, 1)),
        buq=np.ascontiguousarray(g("L1_attn_b_UQ").reshape(8, 128).T),
        buk=np.ascontiguousarray(g("L1_attn_b_UK").reshape(8, 128).T),
        buv=np.ascontiguousarray(g("L1_attn_b_UV").reshape(1, 1024)),
        bqr=np.ascontiguousarray(g("L1_attn_b_QR").reshape(8, 32).T),
        bo=_colmajor(g("L1_attn_b_O")),
    )


def emit_l1ffn(C, xin, xout, wp="f1", first=True):
    NE = 15
    F = 1024
    FS = 2048
    S = 1024
    NS = T // S
    TB = S // NT

    nc = C.nc
    p = C.p
    nm = (wp + "_") if wp else ""
    x, y = xin, xout
    gwd = C.dram(nm + "gw", [128, JD * NE])
    s1d = C.dram(nm + "s1", [D, 2 * FS])
    s2d = C.dram(nm + "s2", [FS, D])
    f1d = C.dram(nm + "f1", [NE, D, 2 * F])
    f2d = C.dram(nm + "f2", [NE, F, D])
    fb1d = C.dram(nm + "fb1", [128, NE * 16])
    fb2d = C.dram(nm + "fb2", [1, NE * D])
    gmd = C.dram(nm + "gm", [128, NE * 8])
    sb1d = C.dram(nm + "sb1", [128, 32])
    sgmd = C.dram(nm + "sgm", [128, 16])
    sb2d = C.dram(nm + "sb2", [1, D])
    idd = C.dram(nm + "idd", [128, 128])
    wcs1 = C.dram(nm + "wcs1", [32, 128, JD * 128], BF16, "Internal")
    wcs2 = C.dram(nm + "wcs2", [16, 128, D], BF16, "Internal")
    wcf1 = C.dram(nm + "wcf1", [NE * 16, 128, JD * 128], BF16, "Internal")
    wcf2 = C.dram(nm + "wcf2", [NE * 8, 128, D], BF16, "Internal")

    if True:
        xs = C.xs
        xln = C.xln
        arena_f = C.arena.bitcast(F32)
        hbt = C.hb
        hb_f = C.hb.bitcast(F32)
        yacc = arena_f[:, 0:8 * D]
        ybuf = arena_f[:, 8192:9216]
        vsq = arena_f[:, 9216:13312]
        ssb = arena_f[:, 13312:14336]
        sb2r = arena_f[0:1, 14336:15360]
        ashr = hbt[:, 0:16 * S]
        aexp = hbt[:, 16 * S:24 * S]
        wbA = C.wbA
        wbB = C.wbB
        abuf = C.abuf
        ebias = C.abuf.bitcast(BF16)[0:1, 0:D]
        wbA_f = C.wbA.bitcast(F32)
        identb = wbA_f[:, 3072:3200]
        gatew = wbA_f[:, 3200:3320]
        fb1 = wbA_f[:, 3328:3568]
        gmc = wbA_f[:, 3584:3704]
        comb = arena_f[:, 15360:15480]
        et = arena_f[:, 15480:15600]
        etd = arena_f[:, 15600:15720]
        lgs = arena_f[:, 15720:15840]
        scr = arena_f[:, 15840:15968]
        r1 = arena_f[:, 15968:15976]
        r2 = arena_f[:, 15976:15984]
        combb = C.arena[:, 31968:32088]
        ones_rb = C.sb(nm + "ones_rb", [1, 128], BF16)
        sb1 = C.sb(nm + "sb1c", [128, 32], F32)
        sgm = C.sb(nm + "sgmc", [128, 16], F32)
        ones_r = C.ones_r
        ones_c = C.ones_c
        rmean, rvar, rt2, rA = C.rmean, C.rvar, C.rt2, C.rA
        ceps = C.ceps
        pb = C.pb
        rows = (rmean, rvar, rt2, rA, rt2)

        C.init_common()
        wb_after = {}

        p.wait("gpsimd", "MM", p.tally["MM"])
        p.dwait("gpsimd", "WBA")
        p.dwait("gpsimd", "WBB")
        p.dma("sync", gatew, gwd, "LX")
        p.dma("sync", fb1, fb1d, "LX")
        p.dma("sync", gmc, gmd, "LX")
        for dst, srcd in ((sb1[:], sb1d), (sgm[:], sgmd), (sb2r, sb2d), (identb, idd)):
            p.dma("sync", dst, srcd, "LX")
        p.op("vector", lambda e: e.memset(ones_rb[:], 1.0))
        p.dwait("vector", "LX")
        ev_init = p.op("vector", lambda e: e.nop(), "EV")
        p.wait("tensor", "EV", ev_init)
        p.dwait("tensor", "LWA")
        mm_ssb = 0
        for hblk in range(2):
            mm_ssb = p.op("tensor", lambda e, hblk=hblk: e.matmul(
                out=pb[6 + hblk][:], lhsT=ones_r[:], rhs=sb2r[0:1, hblk * NT:(hblk + 1) * NT],
                start=True, stop=True), "MM")
            p.wait("vector", "MM", mm_ssb)
            p.op("vector", lambda e, hblk=hblk: e.tensor_copy(
                out=ssb[:, hblk * NT:(hblk + 1) * NT], in_=pb[6 + hblk][:]))
        ev_ssbcp = p.op("vector", lambda e: e.nop(), "EV")
        p.wait("tensor", "EV", ev_ssbcp)
        ev_ssb = p.op("vector", lambda e: e.nop(), "EV")

        def wchunk(tag, idx, o, src2d, wc, wcrow, ktiles, slot, free_mm, s, lsem, wsem):
            if free_mm:
                p.wait("gpsimd", "MM", free_mm)
            if s == 0 and first:
                if (tag, idx, o - 2) in wb_after or (tag, idx, o - 4) in wb_after:
                    p.dwait("gpsimd", wsem)
                if ktiles == 1:
                    view3 = slot[:, 0:src2d.shape[1]]
                    srcv = src2d[o * 128:(o + 1) * 128, :]
                else:
                    view3 = slot.rearrange("p (k o) -> p k o", k=ktiles)
                    srcv = src2d[:, o * 128:(o + 1) * 128].rearrange("(k p) o -> p k o", p=128)
                p.dma("gpsimd", view3, srcv, lsem)
                p.dwait("sync", lsem)
                wb_after[(tag, idx, o)] = p.dma("sync", wc[wcrow], slot, wsem)
            else:
                p.dwait("gpsimd", wsem)
                p.dma("gpsimd", slot, wc[wcrow], lsem)

        xlnf = hb_f[:, 8192:8192 + JD * NT]
        xlnf3 = xlnf.rearrange("p (j t) -> p j t", j=JD)
        xs3 = xs[:].rearrange("p (j t) -> p j t", j=JD)
        vsq3 = vsq.rearrange("p (j t) -> p j t", j=JD)
        xln3 = xln[:].rearrange("p (j t) -> p j t", j=JD)
        xs_free = (0, ev_ssb)
        xlnf_free_mm_local = [0]
        psAB_free = ev_ssb
        psCD_free = ev_ssb
        prev_sq = 0
        wbfree_mm = {0: 0, 1: 0}
        pbfree = {i: 0 for i in range(8)}
        pbfree_es = {i: 0 for i in range(8)}
        xln_free_mm = 0
        xlnf_free_mm = 0
        yaccfree_mm = 0
        fslot_free_mm = 0
        st_y = 0
        for s in range(NS):
            ev_xln = {}
            for tb in range(TB):
                t0 = s * S + tb * NT
                p.wait("sync", "MM", xs_free[0])
                p.wait("sync", "EV", xs_free[1])
                lx_xs = p.dma("sync", xs3,
                              x[:, t0:t0 + NT].rearrange("(j p) t -> p j t", p=128), "LX")
                dst3 = xln3[:, :, tb * NT:(tb + 1) * NT]
                mm_ln, ev_ln = _norm_block(
                    p, pb, xs3, vsq3, dst3, ones_c, ones_r, rows, ceps, 0,
                    abuf[:, 0:NT], None, True, prev_sq, psAB_free, psCD_free,
                    waits_t=[("LX", lx_xs), ("EV", ev_init)],
                    waits_s=[("LX", lx_xs)],
                    waits_v=[("MM", max(xln_free_mm, xlnf_free_mm_local[0]))], dst2=xlnf3)
                prev_sq = mm_ln
                psAB_free = ev_ln
                psCD_free = ev_ln
                ev_xln[tb] = ev_ln
                pw = pb[2 + tb % 2]
                p.wait("tensor", "EV", max(pbfree[2 + tb % 2], ev_ln))
                p.wait("tensor", "ES", pbfree_es[2 + tb % 2])
                pbfree_es[2 + tb % 2] = 0
                mm = 0
                for t4 in range(4):
                    for k in range(JD):
                        mm = p.op("tensor", lambda e, t4=t4, k=k, pw=pw: e.matmul(
                            out=pw[:, t4 * 16: t4 * 16 + NE],
                            lhsT=xlnf[:, k * NT + t4 * 128: k * NT + (t4 + 1) * 128],
                            rhs=gatew[:, k * NE:(k + 1) * NE],
                            start=(k == 0), stop=(k == JD - 1)),
                            "MM" if k == JD - 1 else None)
                p.wait("vector", "MM", mm)
                ev = 0
                for t4 in range(4):
                    tt = tb * 4 + t4
                    ev = p.op("vector", lambda e, t4=t4, tt=tt, pw=pw: e.tensor_copy(
                        out=lgs[:, tt * NE:(tt + 1) * NE],
                        in_=pw[:, t4 * 16: t4 * 16 + NE]), "EV")
                pbfree[2 + tb % 2] = ev
                xlnf_free_mm_local[0] = mm
                xs_free = (mm_ln, ev_ln)
            ev_xln_all = ev_xln[TB - 1]
            xlnf_free_mm = xlnf_free_mm_local[0]

            for tt in range(8):
                p.op("vector", lambda e, tt=tt: e.tensor_reduce(
                    out=r1[:, tt:tt + 1], in_=lgs[:, tt * NE:(tt + 1) * NE], axis=AX.X,
                    op=ALU.max))
            _evv = p.op("vector", lambda e: e.nop(), "EV")
            p.wait("vector", "EV", _evv)
            ev_d = 0
            for tt in range(8):
                ev_d = p.op("vector", lambda e, tt=tt: e.tensor_single_scalar(
                    out=etd[:, tt * NE:(tt + 1) * NE], in_=lgs[:, tt * NE:(tt + 1) * NE],
                    scalar=r1[:, tt:tt + 1], op=ALU.subtract), "EV")
            p.wait("scalar", "EV", ev_d)
            es_exp_all = 0
            for tt in range(8):
                es_exp_all = p.op("scalar", lambda e, tt=tt: e.activation(
                    out=et[:, tt * NE:(tt + 1) * NE], in_=etd[:, tt * NE:(tt + 1) * NE],
                    func=AF.Exp), "ES")
            def vedge():
                evv = p.op("vector", lambda e: e.nop(), "EV")
                p.wait("vector", "EV", evv)

            p.wait("vector", "ES", es_exp_all)
            for rnd in range(3):
                for tt in range(8):
                    src = et[:, tt * NE:(tt + 1) * NE] if rnd == 0 else comb[:, tt * NE:(tt + 1) * NE]
                    dstr = (r1 if rnd < 2 else r2)
                    p.op("vector", lambda e, tt=tt, src=src, dstr=dstr: e.tensor_reduce(
                        out=dstr[:, tt:tt + 1], in_=src, axis=AX.X, op=ALU.max))
                if rnd == 2:
                    break
                vedge()
                for tt in range(8):
                    src = et[:, tt * NE:(tt + 1) * NE] if rnd == 0 else comb[:, tt * NE:(tt + 1) * NE]
                    p.op("vector", lambda e, tt=tt, src=src: e.tensor_single_scalar(
                        out=scr[:, tt * 16:tt * 16 + NE], in_=src,
                        scalar=r1[:, tt:tt + 1], op=ALU.subtract))
                for tt in range(8):
                    p.op("vector", lambda e, tt=tt: e.tensor_single_scalar(
                        out=scr[:, tt * 16:tt * 16 + NE], in_=scr[:, tt * 16:tt * 16 + NE],
                        scalar=0.0, op=ALU.is_ge))
                for tt in range(8):
                    src = et[:, tt * NE:(tt + 1) * NE] if rnd == 0 else comb[:, tt * NE:(tt + 1) * NE]
                    p.op("vector", lambda e, tt=tt, src=src: e.scalar_tensor_tensor(
                        out=comb[:, tt * NE:(tt + 1) * NE], in0=scr[:, tt * 16:tt * 16 + NE],
                        scalar=-1e30, in1=src, op0=ALU.mult, op1=ALU.add))
            vedge()
            ev_comb = {}
            for tt in range(8):
                p.op("vector", lambda e, tt=tt: e.tensor_single_scalar(
                    out=scr[:, tt * 16:tt * 16 + NE], in_=et[:, tt * NE:(tt + 1) * NE],
                    scalar=r2[:, tt:tt + 1], op=ALU.subtract))
            for tt in range(8):
                p.op("vector", lambda e, tt=tt: e.tensor_single_scalar(
                    out=scr[:, tt * 16:tt * 16 + NE], in_=scr[:, tt * 16:tt * 16 + NE],
                    scalar=0.0, op=ALU.is_ge))
            for tt in range(8):
                p.op("vector", lambda e, tt=tt: e.tensor_mul(
                    out=comb[:, tt * NE:(tt + 1) * NE], in0=et[:, tt * NE:(tt + 1) * NE],
                    in1=scr[:, tt * 16:tt * 16 + NE]))
            for tt in range(8):
                p.op("vector", lambda e, tt=tt: e.tensor_reduce(
                    out=r1[:, tt:tt + 1], in_=comb[:, tt * NE:(tt + 1) * NE], axis=AX.X,
                    op=ALU.add))
            vedge()
            for tt in range(8):
                p.op("vector", lambda e, tt=tt: e.reciprocal(
                    out=r2[:, tt:tt + 1], in_=r1[:, tt:tt + 1]))
            vedge()
            for tt in range(8):
                p.op("vector", lambda e, tt=tt: e.tensor_single_scalar(
                    out=comb[:, tt * NE:(tt + 1) * NE], in_=comb[:, tt * NE:(tt + 1) * NE],
                    scalar=r2[:, tt:tt + 1], op=ALU.mult))
                ev_comb[tt] = p.op("vector", lambda e, tt=tt: e.tensor_copy(
                    out=combb[:, tt * NE:(tt + 1) * NE],
                    in_=comb[:, tt * NE:(tt + 1) * NE]), "EV")
            ev_comb_all = ev_comb[7]

            p.wait("vector", "MM", yaccfree_mm)
            for tt in range(8):
                for dblk in range(2):
                    p.op("vector", lambda e, tt=tt, dblk=dblk: e.tensor_copy(
                        out=yacc[:, tt * D + dblk * NT: tt * D + (dblk + 1) * NT],
                        in_=ssb[:, dblk * NT:(dblk + 1) * NT]))
            ev_yinit = p.op("vector", lambda e: e.nop(), "EV")

            g1done = {}
            es_after = {}
            ev_after = {}
            for o in range(16):
                slotA = wbA[:, (o % 2) * 1024:(o % 2 + 1) * 1024]
                slotB = wbB[:, (o % 2) * 1024:(o % 2 + 1) * 1024]
                wchunk("s1", 0, o, s1d, wcs1, o, JD, slotA,
                       g1done.get(o - 2, wbfree_mm[0]), s, "LWA", "WBA")
                wchunk("s1", 0, 16 + o, s1d, wcs1, 16 + o, JD, slotB, 0, s, "LWB", "WBB")
                p.dwait("tensor", "LWA")
                p.dwait("tensor", "LWB")
                if o == 0:
                    p.wait("tensor", "EV", ev_xln_all)
                for tb in range(TB):
                    i = o * TB + tb
                    pw1, pw2 = pb[2 + i % 2], pb[4 + i % 2]
                    if i >= 2:
                        p.wait("tensor", "ES", es_after[i - 2])
                        p.wait("tensor", "EV", ev_after[i - 2])
                    else:
                        p.wait("tensor", "EV", max(pbfree[2 + i % 2], pbfree[4 + i % 2]))
                    for k in range(JD):
                        p.op("tensor", lambda e, k=k, tb=tb, slotA=slotA, pw1=pw1: e.matmul(
                            out=pw1[:], lhsT=slotA[:, k * 128:(k + 1) * 128],
                            rhs=xln[:, k * S + tb * NT: k * S + (tb + 1) * NT],
                            start=(k == 0), stop=(k == JD - 1)))
                    mm = 0
                    for k in range(JD):
                        mm = p.op("tensor", lambda e, k=k, tb=tb, slotB=slotB, pw2=pw2:
                                  e.matmul(
                            out=pw2[:], lhsT=slotB[:, k * 128:(k + 1) * 128],
                            rhs=xln[:, k * S + tb * NT: k * S + (tb + 1) * NT],
                            start=(k == 0), stop=(k == JD - 1)),
                            "MM" if k == JD - 1 else None)
                    p.wait("scalar", "MM", mm)
                    es_g = p.op("scalar", lambda e, i=i, o=o, pw2=pw2: e.activation(
                        out=abuf[:, (i % 2) * NT:(i % 2 + 1) * NT], in_=pw2[:],
                        func=AF.Gelu, bias=sb1[:, 16 + o:17 + o]), "ES")
                    es_after[i] = es_g
                    p.wait("vector", "MM", mm)
                    p.wait("vector", "ES", es_g)
                    p.op("vector", lambda e, i=i, o=o, pw1=pw1: e.scalar_tensor_tensor(
                        out=abuf[:, (i % 2) * NT:(i % 2 + 1) * NT], in0=pw1[:],
                        scalar=sb1[:, o:o + 1], in1=abuf[:, (i % 2) * NT:(i % 2 + 1) * NT],
                        op0=ALU.add, op1=ALU.mult))
                    ev = p.op("vector", lambda e, i=i, o=o, tb=tb: e.tensor_single_scalar(
                        out=ashr[:, o * S + tb * NT: o * S + (tb + 1) * NT],
                        in_=abuf[:, (i % 2) * NT:(i % 2 + 1) * NT],
                        scalar=sgm[:, o:o + 1], op=ALU.mult), "EV")
                    ev_after[i] = ev
                g1done[o] = mm
            wbfree_mm[0] = p.tally["MM"]
            wbfree_mm[1] = p.tally["MM"]
            ev_ashr_all = p.tally["EV"]
            pbfree[2] = ev_after[16 * TB - 2]
            pbfree[3] = ev_after[16 * TB - 1]
            pbfree_es[2] = es_after[16 * TB - 2]
            pbfree_es[3] = es_after[16 * TB - 1]
            p.wait("tensor", "ES", es_after[16 * TB - 1])
            ev_prev_grp = 0
            mm_sl = {}
            for grp in range(2):
                p.wait("tensor", "EV", ev_ashr_all if grp == 0 else ev_prev_grp)
                for ft in range(16):
                    slot = (wbA if ft % 2 == 0 else wbB)[:, 0:1024]
                    lsem, wsem = ("LWA", "WBA") if ft % 2 == 0 else ("LWB", "WBB")
                    if grp == 0:
                        wchunk("s2", 0, ft, s2d, wcs2, ft, 1, slot,
                               mm_sl.get((0, ft - 2), wbfree_mm[ft % 2]), s, lsem, wsem)
                    else:
                        p.wait("gpsimd", "MM", mm_sl.get((1, ft - 2), mm_sl[(0, 15)]))
                        p.dwait("gpsimd", wsem)
                        p.dma("gpsimd", slot, wcs2[ft], lsem)
                    p.dwait("tensor", lsem)
                    mm = 0
                    for tile in range(8):
                        tt = grp * 4 + tile // 2
                        dblk = tile % 2
                        mm = p.op("tensor", lambda e, ft=ft, tile=tile, tt=tt, dblk=dblk,
                                  slot=slot: e.matmul(
                            out=pb[tile][:],
                            lhsT=ashr[:, ft * S + tt * 128: ft * S + (tt + 1) * 128],
                            rhs=slot[:, dblk * NT:(dblk + 1) * NT],
                            start=(ft == 0), stop=(ft == 15)),
                            "MM" if tile == 7 else None)
                    mm_sl[(grp, ft)] = mm
                p.wait("vector", "MM", mm_sl[(grp, 15)])
                ev = 0
                for tile in range(8):
                    tt = grp * 4 + tile // 2
                    dblk = tile % 2
                    ev = p.op("vector", lambda e, tile=tile, tt=tt, dblk=dblk: e.tensor_add(
                        out=yacc[:, tt * D + dblk * NT: tt * D + (dblk + 1) * NT],
                        in0=yacc[:, tt * D + dblk * NT: tt * D + (dblk + 1) * NT],
                        in1=pb[tile][:]), "EV")
                    pbfree[tile] = ev
                ev_prev_grp = ev
            wbfree_mm[0] = p.tally["MM"]
            wbfree_mm[1] = p.tally["MM"]

            fslot = [hbt[:, 24576 + f_ * 1024: 24576 + (f_ + 1) * 1024]
                     for f_ in range(8)]
            for e_i in range(NE):
                p.wait("sync", "MM", fslot_free_mm)
                p.dwait("sync", "WBA")
                if s == 0 and first:
                    p.wait("gpsimd", "MM", fslot_free_mm)
                    for f_ in range(8):
                        p.dma("gpsimd", fslot[f_], f2d[e_i][f_ * 128:(f_ + 1) * 128, :], "DB")
                    p.dwait("sync", "DB")
                    for f_ in range(8):
                        p.dma("sync", wcf2[e_i * 8 + f_], fslot[f_], "WBA")
                else:
                    for f_ in range(8):
                        p.dma("sync", fslot[f_], wcf2[e_i * 8 + f_], "DB")
                e1done = {}
                for o in range(8):
                    slotA = wbA[:, (o % 2) * 1024:(o % 2 + 1) * 1024]
                    slotB = wbB[:, (o % 2) * 1024:(o % 2 + 1) * 1024]
                    f1e = f1d[e_i]
                    wchunk("f1", e_i, o, f1e, wcf1, e_i * 16 + o, JD, slotA,
                           e1done.get(o - 2, wbfree_mm[0]), s, "LWA", "WBA")
                    wchunk("f1", e_i, 8 + o, f1e, wcf1, e_i * 16 + 8 + o, JD, slotB,
                           0, s, "LWB", "WBB")
                    p.dwait("tensor", "LWA")
                    p.dwait("tensor", "LWB")
                    for tb in range(TB):
                        i = o * TB + tb
                        pw1, pw2 = pb[2 + i % 2], pb[4 + i % 2]
                        if i >= 2:
                            p.wait("tensor", "ES", es_after[i - 2])
                            p.wait("tensor", "EV", ev_after[i - 2])
                        else:
                            p.wait("tensor", "EV", max(pbfree[2 + i % 2], pbfree[4 + i % 2]))
                            p.wait("tensor", "ES", max(pbfree_es[2 + i % 2],
                                                       pbfree_es.get(4 + i % 2, 0)))
                        for k in range(JD):
                            p.op("tensor", lambda e, k=k, tb=tb, slotA=slotA, pw1=pw1:
                                 e.matmul(
                                out=pw1[:], lhsT=slotA[:, k * 128:(k + 1) * 128],
                                rhs=xln[:, k * S + tb * NT: k * S + (tb + 1) * NT],
                                start=(k == 0), stop=(k == JD - 1)))
                        mm = 0
                        for k in range(JD):
                            mm = p.op("tensor", lambda e, k=k, tb=tb, slotB=slotB, pw2=pw2:
                                      e.matmul(
                                out=pw2[:], lhsT=slotB[:, k * 128:(k + 1) * 128],
                                rhs=xln[:, k * S + tb * NT: k * S + (tb + 1) * NT],
                                start=(k == 0), stop=(k == JD - 1)),
                                "MM" if k == JD - 1 else None)
                        p.wait("scalar", "MM", mm)
                        es_g = p.op("scalar", lambda e, i=i, o=o, e_i=e_i, pw2=pw2:
                                    e.activation(
                            out=abuf[:, (i % 2) * NT:(i % 2 + 1) * NT], in_=pw2[:],
                            func=AF.Gelu, bias=fb1[:, e_i * 16 + 8 + o:e_i * 16 + 9 + o]),
                            "ES")
                        es_after[i] = es_g
                        p.wait("vector", "MM", mm)
                        p.wait("vector", "ES", es_g)
                        p.op("vector", lambda e, i=i, o=o, e_i=e_i, pw1=pw1:
                             e.scalar_tensor_tensor(
                            out=abuf[:, (i % 2) * NT:(i % 2 + 1) * NT], in0=pw1[:],
                            scalar=fb1[:, e_i * 16 + o:e_i * 16 + o + 1],
                            in1=abuf[:, (i % 2) * NT:(i % 2 + 1) * NT],
                            op0=ALU.add, op1=ALU.mult))
                        ev = p.op("vector", lambda e, i=i, o=o, tb=tb, e_i=e_i:
                                  e.tensor_single_scalar(
                            out=aexp[:, o * S + tb * NT: o * S + (tb + 1) * NT],
                            in_=abuf[:, (i % 2) * NT:(i % 2 + 1) * NT],
                            scalar=gmc[:, e_i * 8 + o:e_i * 8 + o + 1], op=ALU.mult), "EV")
                        ev_after[i] = ev
                    e1done[o] = mm
                wbfree_mm[0] = p.tally["MM"]
                wbfree_mm[1] = p.tally["MM"]
                ev_aexp_all = p.tally["EV"]
                pbfree[2] = ev_after[8 * TB - 2]
                pbfree[3] = ev_after[8 * TB - 1]
                pbfree_es[2] = es_after[8 * TB - 2]
                pbfree_es[3] = es_after[8 * TB - 1]
                p.wait("gpsimd", "EV", ev_aexp_all)
                p.dma("gpsimd", ebias, fb2d[0:1, e_i * D:(e_i + 1) * D], "DB")
                p.dwait("tensor", "DB")
                p.wait("tensor", "EV", ev_aexp_all)
                p.wait("tensor", "ES", es_after[8 * TB - 1])
                ev_prev_grp = 0
                for grp in range(2):
                    if grp:
                        p.wait("tensor", "EV", ev_prev_grp)
                    for fi in range(8):
                        for tile in range(8):
                            tt = grp * 4 + tile // 2
                            dblk = tile % 2
                            p.op("tensor", lambda e, fi=fi, tile=tile, tt=tt, dblk=dblk:
                                 e.matmul(
                                out=pb[tile][:],
                                lhsT=aexp[:, fi * S + tt * 128: fi * S + (tt + 1) * 128],
                                rhs=fslot[fi][:, dblk * NT:(dblk + 1) * NT],
                                start=(fi == 0), stop=False))
                    mm = 0
                    for tile in range(8):
                        dblk = tile % 2
                        mm = p.op("tensor", lambda e, tile=tile, dblk=dblk: e.matmul(
                            out=pb[tile][:], lhsT=ones_rb[:],
                            rhs=ebias[0:1, dblk * NT:(dblk + 1) * NT],
                            start=False, stop=True), "MM" if tile == 7 else None)
                    p.wait("vector", "MM", mm)
                    ev = 0
                    for tile in range(8):
                        tt = grp * 4 + tile // 2
                        dblk = tile % 2
                        ev = p.op("vector", lambda e, tile=tile, tt=tt, dblk=dblk, e_i=e_i:
                                  e.scalar_tensor_tensor(
                            out=yacc[:, tt * D + dblk * NT: tt * D + (dblk + 1) * NT],
                            in0=pb[tile][:],
                            scalar=comb[:, tt * NE + e_i:tt * NE + e_i + 1],
                            in1=yacc[:, tt * D + dblk * NT: tt * D + (dblk + 1) * NT],
                            op0=ALU.mult, op1=ALU.add), "EV")
                        pbfree[tile] = ev
                    ev_prev_grp = ev
                fslot_free_mm = p.tally["MM"]
                wbfree_mm[0] = p.tally["MM"]
                wbfree_mm[1] = p.tally["MM"]
            ev_yacc_done = p.tally["EV"]
            xln_free_mm = p.tally["MM"]
            psAB_free = p.tally["EV"]
            psCD_free = psAB_free

            xresf = hb_f[:, 0:JD * S]
            p.wait("sync", "MM", wbfree_mm[0])
            p.dwait("sync", "ST")
            lx_res = p.dma("sync", xresf.rearrange("p (j t) -> p j t", j=JD),
                           x[:, s * S:(s + 1) * S].rearrange("(j p) t -> p j t", p=128),
                           "LX")
            for tt in range(8):
                mm_tr = {}
                for j in range(JD):
                    i = tt * JD + j
                    pw = pb[2 + i % 2]
                    p.wait("tensor", "EV", max(pbfree[2 + i % 2], ev_yacc_done))
                    p.wait("tensor", "ES", pbfree_es[2 + i % 2])
                    pbfree_es[2 + i % 2] = 0
                    mm = p.op("tensor", lambda e, tt=tt, j=j, pw=pw: e.transpose(
                        out=pw[:, 0:128],
                        in_=yacc[:, tt * D + j * 128: tt * D + (j + 1) * 128],
                        identity=identb), "MM")
                    p.wait("vector", "MM", mm)
                    if j == 0:
                        p.dwait("vector", "ST")
                        if i == 0:
                            p.dwait("vector", "LX")
                    ev = p.op("vector", lambda e, tt=tt, j=j, pw=pw: e.scalar_tensor_tensor(
                        out=ybuf[:, j * 128:(j + 1) * 128], in0=pw[:, 0:128], scalar=0.5,
                        in1=xresf[:, j * S + tt * 128: j * S + (tt + 1) * 128],
                        op0=ALU.mult, op1=ALU.add), "EV")
                    pbfree[2 + i % 2] = ev
                p.wait("sync", "EV", pbfree[2])
                p.wait("sync", "EV", pbfree[3])
                p.dma("sync", y[:, s * S + tt * 128: s * S + (tt + 1) * 128].rearrange(
                    "(j p) t -> p j t", p=128),
                    ybuf[:, 0:JD * 128].rearrange("p (j t) -> p j t", j=JD), "ST")
            yaccfree_mm = p.tally["MM"]
        p.dwait("sync", "ST")


def build_l1ffn():
    nc = bass.Bass("TRN2", target_bir_lowering=False, detect_race_conditions=RACE_DETECT)
    x = nc.dram_tensor("x", [D, T], F32, kind="ExternalInput").ap()
    y = nc.dram_tensor("y", [D, T], F32, kind="ExternalOutput").ap()
    with contextlib.ExitStack() as es:
        p = Prog(nc)
        p.make_sems(es, ALLSEMS)
        Cx = Ctx(nc, p, es)
        emit_l1ffn(Cx, x, y, wp="", first=True)
        with nc.Block() as block:
            p.replay(block)
    return nc


def layout_l1ffn(inp):
    f = np.float32
    g = lambda k: np.asarray(inp[k], f)
    fc1 = g("L1_ffn_fc1_w")
    fc2 = g("L1_ffn_fc2_w")
    return dict(
        gw=np.ascontiguousarray(g("L1_ffn_gate_w").T.reshape(8, 128, 15)
                                .transpose(1, 0, 2).reshape(128, 120)),
        s1=np.ascontiguousarray(g("L1_ffn_s_fc1_w").T),
        s2=np.ascontiguousarray(g("L1_ffn_s_fc2_w").T),
        f1=np.ascontiguousarray(fc1.transpose(0, 2, 1)),
        f2=np.ascontiguousarray(fc2.transpose(0, 2, 1)),
        fb1=np.ascontiguousarray(g("L1_ffn_fc1_b").reshape(15, 16, 128)
                                 .transpose(2, 0, 1).reshape(128, 240)),
        fb2=np.ascontiguousarray(g("L1_ffn_fc2_b").reshape(1, -1)),
        gm=np.ascontiguousarray(g("L1_ffn_geglu_mult").reshape(15, 8, 128)
                                .transpose(2, 0, 1).reshape(128, 120)),
        sb1=_colmajor(g("L1_ffn_s_fc1_b")),
        sgm=_colmajor(g("L1_ffn_s_geglu_mult")),
        sb2=np.ascontiguousarray(g("L1_ffn_s_fc2_b").reshape(1, 1024)),
        idd=np.eye(128, dtype=f),
    )


_PROGS = {}


def _prog(name):
    if name not in _PROGS:
        _PROGS[name] = {"l0ffn": build_l0ffn, "l0attn": build_l0attn,
                        "l1ffn": build_l1ffn, "l1attn": build_l1attn,
                        "fused": build_fused}[name]()
    return _PROGS[name]


def build_fused():
    nc = bass.Bass("TRN2", target_bir_lowering=False, detect_race_conditions=RACE_DETECT)
    x = nc.dram_tensor("x", [D, T], F32, kind="ExternalInput").ap()
    y = nc.dram_tensor("y", [D, T], F32, kind="ExternalOutput").ap()
    b0 = nc.dram_tensor("b0", [D, T], F32, kind="Internal").ap()
    b1 = nc.dram_tensor("b1", [D, T], F32, kind="Internal").ap()
    with contextlib.ExitStack() as es:
        p = Prog(nc)
        p.make_sems(es, ALLSEMS)
        Cx = Ctx(nc, p, es)
        emit_l0ffn(Cx, x, b0, wp="f0", first=True)
        Cx.barrier()
        emit_l0attn(Cx, b0, b1, wp="a0", first=True)
        Cx.barrier()
        emit_l0ffn(Cx, b1, b0, wp="f0", first=False)
        Cx.barrier()
        emit_l1ffn(Cx, b0, b1, wp="f1", first=True)
        Cx.barrier()
        emit_l1attn(Cx, b1, b0, wp="a1", first=True)
        Cx.barrier()
        emit_l1ffn(Cx, b0, y, wp="f1", first=False)
        with nc.Block() as block:
            p.replay(block)
    return nc


def kernel(**inputs):
    x = np.asarray(inputs["L0_attn_x"], np.float32)
    w = {}
    for pref, lay in (("f0", layout_l0ffn), ("a0", layout_l0attn),
                      ("f1", layout_l1ffn), ("a1", layout_l1attn)):
        for k, v in lay(inputs).items():
            w[f"{pref}_{k}"] = v
    nc = _prog("fused")
    in_maps = [dict(w, x=np.ascontiguousarray(x[b].T)) for b in range(NCORE)]
    res = run_bass_kernel_spmd(nc, in_maps, core_ids=list(range(NCORE)))
    return np.stack([np.asarray(res.results[b]["y"]).T for b in range(NCORE)], axis=0)

